# version 15 (final)
# baseline (speedup 1.0000x reference)
import sys
sys.path.insert(0, "/opt/trn_rl_repo")
import contextlib
import numpy as np
import ml_dtypes

import concourse.bass as bass
import concourse.mybir as mybir
from concourse.bass_utils import run_bass_kernel_spmd
from concourse.alu_op_type import AluOpType as OP

F32 = mybir.dt.float32
BF16 = mybir.dt.bfloat16
U8 = mybir.dt.uint8
AF = mybir.ActivationFunctionType
AX = mybir.AxisListType

NCORES = 8
NEG_BIG = -3.0e38



_NO_DRAIN = {"wait_ge", "drain", "nop", "dma_start", "dma_start_transpose", "sem_inc",
             "then_inc", "wait_op", "collective_compute", "engine_nop", "sem_clear"}


def _ap_box(v):
    try:
        name = v.tensor.name
    except Exception:
        return None
    try:
        W = v.tensor.shape[-1]
        off = v.offset
        pext = 0
        fext = 0
        for step, cnt in list(v.ap):
            if step < 0:
                return (name, 0, 1 << 60, 0, 1 << 60)
            s = step * (cnt - 1)
            if step != 0 and step % W == 0:
                pext += s // W
            else:
                fext += s
        p0, f0 = off // W, off % W
        if f0 + fext >= W:
            return (name, p0, p0 + pext + (f0 + fext) // W, 0, W - 1)
        return (name, p0, p0 + pext, f0, f0 + fext)
    except Exception:
        return (name, 0, 1 << 60, 0, 1 << 60)


def _boxes_overlap(a, b):
    return (a[0] == b[0] and a[1] <= b[2] and b[1] <= a[2]
            and a[3] <= b[4] and b[3] <= a[4])


class D:

    def __init__(self, eng):
        self._e = eng
        self._r = []
        self._w = []

    def __getattr__(self, name):
        f = getattr(self._e, name)
        if name.startswith("_") or name in _NO_DRAIN or not callable(f):
            return f

        def g(*a, **k):
            writes = []
            reads = []
            if name in ("memset", "memzero"):
                b = _ap_box(a[0] if a else k.get("ap"))
                if b:
                    writes.append(b)
            else:
                for kk, v in k.items():
                    b = _ap_box(v)
                    if b is None:
                        continue
                    (writes if kk in ("out", "accum_out", "out_max", "out_indices") else reads).append(b)
                for v in a:
                    b = _ap_box(v)
                    if b is not None:
                        reads.append(b)
            conflict = (
                any(_boxes_overlap(r, w) for r in reads for w in self._w)
                or any(_boxes_overlap(w, pw) for w in writes for pw in self._w)
                or any(_boxes_overlap(w, pr) for w in writes for pr in self._r)
            )
            if conflict:
                self._e.drain()
                self._r = []
                self._w = []
            self._r.extend(reads)
            self._w.extend(writes)
            return f(*a, **k)

        return g



def build_decode():
    nc = bass.Bass("TRN2", target_bir_lowering=False, debug=False)
    ins = {}
    for nm in ("gp", "up", "dn"):
        ins[nm + "_pk"] = nc.dram_tensor(nm + "_pk", [2048, 1024], U8, kind="ExternalInput")
        ins[nm + "_sc"] = nc.dram_tensor(nm + "_sc", [2048, 64], F32, kind="ExternalInput")
    ident = nc.dram_tensor("ident", [128, 128], BF16, kind="ExternalInput")
    outs = {
        "gp": nc.dram_tensor("wgT", [16, 2048, 128], BF16, kind="ExternalOutput"),
        "up": nc.dram_tensor("wuT", [16, 2048, 128], BF16, kind="ExternalOutput"),
        "dn": nc.dram_tensor("wdT", [2048, 2048], BF16, kind="ExternalOutput"),
    }

    emit_decode(nc, ins, ident, outs)
    return nc


def emit_decode(nc, ins, ident, outs, collect=None):
    with contextlib.ExitStack() as st:
        ebuf = lambda name, shape, dt: st.enter_context(nc.sbuf_tensor(name, shape, dt, allow_name_mangling=True))
        pk = [ebuf(f"pk{i}", [128, 1024], U8) for i in range(2)]
        sc = [ebuf(f"sc{i}", [128, 64], F32) for i in range(2)]
        nib = ebuf("nib", [128, 2048], U8)
        cf = ebuf("cf", [128, 2048], F32)
        s8 = ebuf("s8", [128, 2048], F32)
        mm_ = ebuf("mm", [128, 2048], F32)
        sgn = ebuf("sgn", [128, 2048], F32)
        r4 = ebuf("r4", [128, 2048], F32)
        r62 = ebuf("r62", [128, 2048], F32)
        dec = [ebuf(f"dec{i}", [128, 2048], BF16) for i in range(2)]
        idn = ebuf("idn", [128, 128], BF16)
        cb4 = ebuf("cb4", [128, 2], F32)
        stg = [ebuf(f"stg{i}", [128, 2048], BF16) for i in range(2)]
        tp = st.enter_context(nc.psum_tensor("tp", [128, 2048], BF16, allow_name_mangling=True))
        tslot = [tp[:, 0:128], tp[:, 1024:1152]]

        ld2 = [Sem(st, nc, "ld0"), Sem(st, nc, "ld1")]
        lid = Sem(st, nc, "lid")
        cmp = Sem(st, nc, "cmp")
        rel = Sem(st, nc, "rel")
        tsp = Sem(st, nc, "tsp")
        ev = Sem(st, nc, "ev")
        so2 = [Sem(st, nc, "so0"), Sem(st, nc, "so1")]
        block = st.enter_context(nc.Block())

        tiles = [(nm, t) for nm in ("gp", "up", "dn") for t in range(16)]
        NT = len(tiles)

        @block.sync
        def _(eng):
            eng.dma_start(out=idn[:, :], in_=ident[:, :]).then_inc(lid.h, 16)
            for i, (nm, t) in enumerate(tiles):
                b = i % 2
                if i >= 2:
                    eng.wait_ge(cmp.h, (i - 2) * 3 + 3)
                eng.dma_start(out=pk[b][:, :], in_=ins[nm + "_pk"][t * 128:(t + 1) * 128, :]).then_inc(ld2[b].h, 16)
                eng.dma_start(out=sc[b][:, :], in_=ins[nm + "_sc"][t * 128:(t + 1) * 128, :]).then_inc(ld2[b].h, 16)

        @block.scalar
        def _(eng):
            eng = D(eng)
            for i in range(NT + 1):
                if i < NT:
                    eng.wait_ge(cmp.h, i * 3 + 1)
                    eng.activation(out=r4[:, :], in_=mm_[:, :], func=AF.Relu, bias=cb4[:, 0:1])
                    eng.activation(out=r62[:, :], in_=mm_[:, :], func=AF.Relu, bias=cb4[:, 1:2],
                                   scale=2.0).then_inc(rel.h, 1)
                if i >= 1:
                    ii = i - 1
                    nm, t = tiles[ii]
                    sb = ii % 2
                    eng.wait_ge(ev.h, (ii + 1) * 16)
                    if nm == "dn":
                        oap = outs[nm].ap()[:, t * 128:(t + 1) * 128].rearrange("(j p) f -> p j f", p=128)
                    else:
                        oap = outs[nm].ap()[t, :, :].rearrange("(j p) f -> p j f", p=128)
                    eng.dma_start(
                        out=oap,
                        in_=stg[sb][:, :].rearrange("p (j f) -> p j f", f=128),
                    ).then_inc(so2[ii % 2].h, 16)
            eng.wait_ge(so2[0].h, (NT // 2) * 16)
            eng.wait_ge(so2[1].h, (NT // 2) * 16)

        @block.tensor
        def _(eng):
            eng.wait_ge(lid.h, 16)
            for i in range(NT):
                b = i % 2
                eng.wait_ge(cmp.h, i * 3 + 3)
                for j in range(16):
                    g = i * 16 + j
                    if g >= 2:
                        eng.wait_ge(ev.h, g - 1)
                    eng.transpose(out=tslot[j % 2], in_=dec[b][:, j * 128:(j + 1) * 128],
                                  identity=idn[:, :]).then_inc(tsp.h, 1)

        @block.vector
        def _(eng):
            eng = D(eng)
            eng.memset(cb4[:, 0:1], -4.0)
            eng.memset(cb4[:, 1:2], -12.0)
            for i in range(NT + 1):
                if i < NT:
                    b = i % 2
                    eng.wait_ge(ld2[b].h, (i // 2 + 1) * 32)
                    if i >= 2:
                        eng.wait_ge(tsp.h, (i - 1) * 16)
                    eng.tensor_single_scalar(out=nib[:, 0:1024], in_=pk[b][:, :], scalar=15, op=OP.bitwise_and)
                    eng.tensor_single_scalar(out=nib[:, 1024:2048], in_=pk[b][:, :], scalar=4, op=OP.logical_shift_right)
                    eng.tensor_copy(out=cf[:, :], in_=nib[:, :])
                    eng.tensor_scalar(out=s8[:, :], in0=cf[:, :], scalar1=8.0, scalar2=8.0,
                                      op0=OP.is_ge, op1=OP.mult)
                    eng.tensor_tensor(out=mm_[:, :], in0=cf[:, :], in1=s8[:, :], op=OP.subtract)
                    eng.tensor_scalar(out=sgn[:, :], in0=s8[:, :], scalar1=-0.125, scalar2=0.5,
                                      op0=OP.mult, op1=OP.add).then_inc(cmp.h, 1)
                    eng.wait_ge(rel.h, i + 1)
                    eng.tensor_tensor(out=r4[:, :], in0=mm_[:, :], in1=r4[:, :], op=OP.add)
                    eng.tensor_tensor(out=r4[:, :], in0=r4[:, :], in1=r62[:, :], op=OP.add)
                    eng.tensor_tensor(out=r4[:, :], in0=r4[:, :], in1=sgn[:, :], op=OP.mult).then_inc(cmp.h, 1)
                    eng.tensor_tensor(
                        out=dec[b][:, :].rearrange("p (j two) -> p two j", two=2)
                                         .rearrange("p two (blk f) -> p two blk f", f=16),
                        in0=r4[:, :].rearrange("p (half j) -> p half j", half=2)
                                    .rearrange("p half (blk f) -> p half blk f", f=16),
                        in1=sc[b][:, :].unsqueeze(1).unsqueeze(3).broadcast_to([128, 2, 64, 16]),
                        op=OP.mult).then_inc(cmp.h, 1)
                if i >= 1:
                    sb = (i - 1) % 2
                    if i >= 3:
                        eng.wait_ge(so2[(i - 3) % 2].h, ((i - 3) // 2 + 1) * 16)
                    for j in range(16):
                        g = (i - 1) * 16 + j
                        eng.wait_ge(tsp.h, g + 1)
                        eng.tensor_copy(out=stg[sb][:, j * 128:(j + 1) * 128],
                                        in_=tslot[j % 2]).then_inc(ev.h, 1)

        if collect is not None:
            pccs, pairs = collect

            @block.gpsimd
            def _(eng):
                eng.wait_ge(so2[0].h, (NT // 2) * 16)
                eng.wait_ge(so2[1].h, (NT // 2) * 16)
                for (loc, shr) in pairs:
                    eng.collective_compute(
                        "AllGather", OP.bypass,
                        replica_groups=[list(range(NCORES))],
                        ins=[loc], outs=[shr],
                    ).then_inc(pccs, 1)


class Sem:

    def __init__(self, st, nc, name):
        self.h = st.enter_context(nc.semaphore(name))
        self.v = 0

    def inc(self, n=1):
        self.v += n
        return self.h



def _u8_view_int32_bytes(a_int32):
    v = a_int32.view(np.uint8).reshape(*a_int32.shape, 4)[..., 0]
    return np.ascontiguousarray(v)


_IDENT_BF16 = np.eye(128, dtype=ml_dtypes.bfloat16)

_NC_CACHE = {}


def _get_nc(name, builder):
    if name not in _NC_CACHE:
        _NC_CACHE[name] = builder()
    return _NC_CACHE[name]


def _decode_weights(inputs):
    nc = _get_nc("decode", build_decode)
    gp = _u8_view_int32_bytes(inputs["L1_ffn_gate_W_packed"])
    up = _u8_view_int32_bytes(inputs["L1_ffn_up_W_packed"])
    dn = _u8_view_int32_bytes(inputs["L1_ffn_down_W_packed"])
    ims = []
    for c in range(NCORES):
        r = slice(2048 * c, 2048 * (c + 1))
        ims.append({
            "gp_pk": gp[r], "gp_sc": np.ascontiguousarray(inputs["L1_ffn_gate_scales"][r]),
            "up_pk": up[r], "up_sc": np.ascontiguousarray(inputs["L1_ffn_up_scales"][r]),
            "dn_pk": np.ascontiguousarray(dn[:, 1024 * c:1024 * (c + 1)]),
            "dn_sc": np.ascontiguousarray(inputs["L1_ffn_down_scales"][:, 64 * c:64 * (c + 1)]),
            "ident": _IDENT_BF16,
        })
    res = run_bass_kernel_spmd(nc, ims, core_ids=list(range(NCORES)))
    wgT = np.concatenate([res.results[c]["wgT"] for c in range(NCORES)], axis=0)
    wuT = np.concatenate([res.results[c]["wuT"] for c in range(NCORES)], axis=0)
    wdT = np.concatenate([res.results[c]["wdT"] for c in range(NCORES)], axis=0)
    return wgT, wuT, wdT





def build_l1f():
    nc = bass.Bass("TRN2", target_bir_lowering=False, debug=False)
    x3 = nc.dram_tensor("x3", [2048, 2048], F32, kind="ExternalInput")
    wg3 = nc.dram_tensor("wg3", [128, 2048, 128], BF16, kind="ExternalInput")
    wu3 = nc.dram_tensor("wu3", [128, 2048, 128], BF16, kind="ExternalInput")
    wd = nc.dram_tensor("wd", [16384, 2048], BF16, kind="ExternalInput")
    ident = nc.dram_tensor("ident", [128, 128], BF16, kind="ExternalInput")
    x4 = nc.dram_tensor("x4", [2048, 2048], F32, kind="ExternalOutput")
    NG = 4

    emit_l1f(nc, x3, wg3, wu3, wd, ident, x4)
    return nc


def emit_l1f(nc, x3, wg3, wu3, wd, ident, x4, pccs=None, x3f_scr=None):
    NG = 4
    with contextlib.ExitStack() as st:
        ebuf = lambda name, shape, dt: st.enter_context(nc.sbuf_tensor(name, shape, dt, allow_name_mangling=True))
        x3g = ebuf("x3g", [128, 4 * 2048], F32)
        x3b = ebuf("x3b", [128, 2048], BF16)
        idn = ebuf("idn", [128, 128], BF16)
        X3g = ebuf("X3g", [128, 16 * 512], BF16)
        wbuf = [ebuf(f"wbuf{i}", [128, 4096], BF16) for i in range(2)]
        inter = ebuf("inter", [128, 128 * 512], BF16)
        wdb = [ebuf(f"wdb{i}", [128, 2048], BF16) for i in range(2)]
        gl = ebuf("gl", [128, 512], F32)
        v4 = ebuf("v4", [128, 4], F32)
        mrow = ebuf("mrow", [128, 4], F32)

        tps = st.enter_context(nc.psum_tensor("tps", [128, 2048], BF16, allow_name_mangling=True))
        tslot = [tps[:, 0:128], tps[:, 1024:1152]]
        pgu = st.enter_context(nc.psum_tensor("pgu", [128, 2048], F32, allow_name_mangling=True))
        pd2 = st.enter_context(nc.psum_tensor("pd2", [128, 1024], F32, allow_name_mangling=True))
        pg = [pgu[:, 0:512], pgu[:, 512:1024], pgu[:, 1024:1536]]
        pu = [pgu[:, 1536:2048], pd2[:, 0:512], pd2[:, 512:1024]]
        pd = [pgu[:, i * 512:(i + 1) * 512] for i in range(4)]

        lid = Sem(st, nc, "lid")
        lx = Sem(st, nc, "lx")
        lgx = Sem(st, nc, "lgx")
        cxb = Sem(st, nc, "cxb")
        ext = Sem(st, nc, "ext")
        tsp = Sem(st, nc, "tsp")
        lw = [Sem(st, nc, "lw0"), Sem(st, nc, "lw1")]
        mmg = Sem(st, nc, "mmg")
        geg = Sem(st, nc, "geg")
        gem = Sem(st, nc, "gem")
        lwd = [Sem(st, nc, "lwd0"), Sem(st, nc, "lwd1")]
        mmd = Sem(st, nc, "mmd")
        evd = Sem(st, nc, "evd")
        lnx = Sem(st, nc, "lnx")
        lnq = Sem(st, nc, "lnq")
        lnr = Sem(st, nc, "lnr")
        lns = Sem(st, nc, "lns")
        x4r = Sem(st, nc, "x4r")
        sst = Sem(st, nc, "sst")
        block = st.enter_context(nc.Block())

        @block.sync
        def _(eng):
            if pccs is not None:
                eng.wait_ge(pccs, 3)
            eng.dma_start(out=idn[:, :], in_=ident[:, :]).then_inc(lid.h, 16)

            def load_X3g(g):
                for kc in range(16):
                    eng.dma_start(out=X3g[:, kc * 512:(kc + 1) * 512],
                                  in_=x3f_scr.ap()[kc, :, g * 512:(g + 1) * 512]).then_inc(lgx.h, 16)

            def load_wj(g, j):
                b = j % 2
                eng.dma_start(out=wbuf[b][:, 0:2048],
                              in_=wg3.ap()[j, :, :].rearrange("(c p) f -> p c f", p=128)
                              ).then_inc(lw[b].h, 16)
                eng.dma_start(out=wbuf[b][:, 2048:4096],
                              in_=wu3.ap()[j, :, :].rearrange("(c p) f -> p c f", p=128)
                              ).then_inc(lw[b].h, 16)

            load_X3g(0)
            for g in range(NG):
                if g > 0:
                    eng.wait_ge(sst.h, g * 64)
                for t in range(4):
                    eng.dma_start(out=x3g[:, t * 2048:(t + 1) * 2048],
                                  in_=x3[(g * 4 + t) * 128:(g * 4 + t + 1) * 128, :]).then_inc(lx.h, 16)
                for j in range(2 if g > 0 else 0, 128):
                    if g > 0 or j >= 2:
                        eng.wait_ge(mmg.h, g * 128 + j - 1)
                    load_wj(g, j)
                for H in range(4):
                    for jj in range(32):
                        b = jj % 2
                        gi = (g * 4 + H) * 32 + jj
                        if gi >= 2:
                            eng.wait_ge(mmd.h, gi - 1)
                        eng.dma_start(out=wdb[b][:, :].rearrange("p (four f) -> p four f", f=512),
                                      in_=wd.ap()[jj * 512:(jj + 1) * 512, H * 512:(H + 1) * 512]
                                      .rearrange("(four p) f -> p four f", p=128)
                                      ).then_inc(lwd[b].h, 16)
                    if H == 1 and g + 1 < NG:
                        eng.wait_ge(mmg.h, (g + 1) * 128)
                        load_X3g(g + 1)
                        load_wj(g + 1, 0)
                        load_wj(g + 1, 1)

        @block.tensor
        def _(eng):
            eng.wait_ge(lid.h, 16)
            for g in range(NG):
                if g > 0:
                    eng.wait_ge(evd.h, g * 16)
                eng.wait_ge(lgx.h, (g + 1) * 256)
                for j in range(128):
                    jb = j % 3
                    eng.wait_ge(lw[j % 2].h, (g * 64 + j // 2 + 1) * 32)
                    if g > 0 or j >= 3:
                        eng.wait_ge(gem.h, g * 128 + j - 2)
                    for kc in range(16):
                        eng.matmul(out=pg[jb], lhsT=wbuf[j % 2][:, kc * 128:(kc + 1) * 128],
                                   rhs=X3g[:, kc * 512:(kc + 1) * 512],
                                   start=(kc == 0), stop=(kc == 15))
                    for kc in range(16):
                        mm = eng.matmul(out=pu[jb], lhsT=wbuf[j % 2][:, 2048 + kc * 128:2048 + (kc + 1) * 128],
                                        rhs=X3g[:, kc * 512:(kc + 1) * 512],
                                        start=(kc == 0), stop=(kc == 15))
                    mm.then_inc(mmg.h, 1)
                eng.wait_ge(gem.h, (g + 1) * 128)
                for H in range(4):
                    if H > 0:
                        eng.wait_ge(evd.h, (g * 4 + H) * 4)
                    for jj in range(32):
                        b = jj % 2
                        gi = (g * 4 + H) * 32 + jj
                        eng.wait_ge(lwd[b].h, (gi // 2 + 1) * 16)
                        for jq in range(4):
                            j = jj * 4 + jq
                            for t in range(4):
                                mm = eng.matmul(out=pd[t],
                                                lhsT=inter[:, j * 512 + t * 128:j * 512 + (t + 1) * 128],
                                                rhs=wdb[b][:, jq * 512:(jq + 1) * 512],
                                                start=(j == 0), stop=(j == 127),
                                                skip_group_check=True)
                        mm.then_inc(mmd.h, 1)

        @block.vector
        def _(eng):
            eng = D(eng)
            for g in range(NG):
                for j in range(128):
                    jb = j % 3
                    eng.wait_ge(geg.h, g * 128 + j + 1)
                    eng.tensor_tensor(out=inter[:, j * 512:(j + 1) * 512], in0=gl[:, :],
                                      in1=pu[jb], op=OP.mult).then_inc(gem.h, 1)
                eng.wait_ge(lx.h, (g + 1) * 64)
                for H in range(4):
                    eng.wait_ge(mmd.h, (g * 4 + H + 1) * 32)
                    for t in range(4):
                        sl = x3g[:, t * 2048 + H * 512:t * 2048 + (H + 1) * 512]
                        eng.tensor_tensor(out=sl, in0=pd[t], in1=sl, op=OP.add).then_inc(evd.h, 1)
                for t in range(4):
                    idx = g * 4 + t
                    sl = x3g[:, t * 2048:(t + 1) * 2048]
                    eng.tensor_reduce(out=mrow[:, 0:1], in_=sl, axis=AX.X, op=OP.add)
                    eng.tensor_scalar(out=mrow[:, 0:1], in0=mrow[:, 0:1], scalar1=1.0 / 2048.0,
                                      scalar2=None, op0=OP.mult)
                    eng.tensor_scalar(out=sl, in0=sl, scalar1=mrow[:, 0:1], scalar2=None,
                                      op0=OP.subtract).then_inc(lnx.h, 1)
                    eng.wait_ge(lnq.h, idx + 1)
                    eng.tensor_reduce(out=mrow[:, 1:2], in_=v4[:, :], axis=AX.X, op=OP.add)
                    eng.tensor_scalar(out=mrow[:, 1:2], in0=mrow[:, 1:2], scalar1=1.0 / 2048.0,
                                      scalar2=1e-5, op0=OP.mult, op1=OP.add)
                    eng.reciprocal(out=mrow[:, 2:3], in_=mrow[:, 1:2]).then_inc(lnr.h, 1)
                    eng.wait_ge(lns.h, idx + 1)
                    eng.tensor_scalar(out=sl, in0=sl, scalar1=mrow[:, 3:4], scalar2=None,
                                      op0=OP.mult).then_inc(x4r.h, 1)

        @block.scalar
        def _(eng):
            eng = D(eng)
            for g in range(NG):
                for j in range(128):
                    jb = j % 3
                    eng.wait_ge(mmg.h, g * 128 + j + 1)
                    if g > 0 or j >= 1:
                        eng.wait_ge(gem.h, g * 128 + j)
                    eng.activation(out=gl[:, :], in_=pg[jb],
                                   func=AF.Gelu_apprx_tanh).then_inc(geg.h, 1)
                for t in range(4):
                    idx = g * 4 + t
                    eng.wait_ge(lnx.h, idx + 1)
                    sl = x3g[:, t * 2048:(t + 1) * 2048]
                    for c in range(4):
                        a = eng.activation(out=gl[:, :], in_=sl[:, c * 512:(c + 1) * 512],
                                           func=AF.Square, accum_out=v4[:, c:c + 1])
                    a.then_inc(lnq.h, 1)
                    eng.wait_ge(lnr.h, idx + 1)
                    eng.sqrt(out=mrow[:, 3:4], in_=mrow[:, 2:3]).then_inc(lns.h, 1)
                for t in range(4):
                    eng.wait_ge(x4r.h, g * 4 + t + 1)
                    eng.dma_start(out=x4.ap()[g + t * 512: g + t * 512 + 509:4, :],
                                  in_=x3g[:, t * 2048:(t + 1) * 2048]).then_inc(sst.h, 16)
            eng.wait_ge(sst.h, NG * 64)



def build_l0f():
    nc = bass.Bass("TRN2", target_bir_lowering=False, debug=False)
    x1 = nc.dram_tensor("x1", [2048, 2048], F32, kind="ExternalInput")
    gup = nc.dram_tensor("gup", [4, 2048, 4096], F32, kind="ExternalInput")
    dwn = nc.dram_tensor("dwn", [4, 2048, 2048], F32, kind="ExternalInput")
    ident = nc.dram_tensor("ident", [128, 128], BF16, kind="ExternalInput")
    x2 = nc.dram_tensor("x2", [2048, 2048], F32, kind="ExternalOutput")
    NE, NJ = 4, 16

    emit_l0f(nc, x1, gup, dwn, ident, x2)
    return nc


def emit_l0f(nc, x1, gup, dwn, ident, x2, x2f_scr=None):
    NE, NJ = 4, 16
    with contextlib.ExitStack() as st:
        ebuf = lambda name, shape, dt: st.enter_context(nc.sbuf_tensor(name, shape, dt, allow_name_mangling=True))
        x1g = ebuf("x1g", [128, 4 * 2048], F32)
        x1b = ebuf("x1b", [128, 2048], BF16)
        idn = ebuf("idn", [128, 128], BF16)
        X1e = ebuf("X1e", [128, 16 * 512], BF16)
        wbuf = [ebuf(f"wbuf{i}", [128, 4096], BF16) for i in range(2)]
        inter = ebuf("inter", [128, NJ * 512], BF16)
        wdb = [ebuf(f"wdb{i}", [128, 2048], BF16) for i in range(2)]
        gl = ebuf("gl", [128, 512], F32)
        xfb = ebuf("xfb", [128, 2048], BF16)
        stgE = [ebuf(f"stgE{i}", [128, 2048], BF16) for i in range(2)]
        v4 = ebuf("v4", [128, 4], F32)
        mrow = ebuf("mrow", [128, 4], F32)

        tps = st.enter_context(nc.psum_tensor("tps", [128, 2048], BF16, allow_name_mangling=True))
        tslot = [tps[:, 0:128], tps[:, 1024:1152]]
        pgu = st.enter_context(nc.psum_tensor("pgu", [128, 2048], F32, allow_name_mangling=True))
        pd2 = st.enter_context(nc.psum_tensor("pd2", [128, 1024], F32, allow_name_mangling=True))
        pg = [pgu[:, 0:512], pgu[:, 512:1024], pgu[:, 1024:1536]]
        pu = [pgu[:, 1536:2048], pd2[:, 0:512], pd2[:, 512:1024]]
        pd = [pgu[:, i * 512:(i + 1) * 512] for i in range(4)]

        lid = Sem(st, nc, "lid")
        lx = Sem(st, nc, "lx")
        cxb = Sem(st, nc, "cxb")
        ext = Sem(st, nc, "ext")
        tsp = Sem(st, nc, "tsp")
        lw = [Sem(st, nc, "lw0"), Sem(st, nc, "lw1")]
        mmg = Sem(st, nc, "mmg")
        geg = Sem(st, nc, "geg")
        gem = Sem(st, nc, "gem")
        lwd = [Sem(st, nc, "lwd0"), Sem(st, nc, "lwd1")]
        mmd = Sem(st, nc, "mmd")
        evd = Sem(st, nc, "evd")
        lnx = Sem(st, nc, "lnx")
        lnq = Sem(st, nc, "lnq")
        lnr = Sem(st, nc, "lnr")
        lns = Sem(st, nc, "lns")
        x4r = Sem(st, nc, "x4r")
        cxe = Sem(st, nc, "cxe")
        tspE = Sem(st, nc, "tspE")
        extE = Sem(st, nc, "extE")
        sxf2 = [Sem(st, nc, "sxfa"), Sem(st, nc, "sxfb")]
        sst = Sem(st, nc, "sst")
        block = st.enter_context(nc.Block())

        @block.sync
        def _(eng):
            eng.dma_start(out=idn[:, :], in_=ident[:, :]).then_inc(lid.h, 16)
            for e in range(NE):
                if e > 0:
                    eng.wait_ge(sst.h, e * 64)
                    if x2f_scr is not None:
                        eng.wait_ge(cxe.h, e * 4)
                for t in range(4):
                    eng.dma_start(out=x1g[:, t * 2048:(t + 1) * 2048],
                                  in_=x1.ap()[e + t * 512: e + t * 512 + 509:4, :]).then_inc(lx.h, 16)
                if x2f_scr is not None:
                    for t in range(4):
                        idx = e * 4 + t
                        sb = idx % 2
                        eng.wait_ge(extE.h, (idx + 1) * 16)
                        eng.dma_start(
                            out=x2f_scr.ap()[:, :, idx * 128:(idx + 1) * 128].rearrange("c p f -> p c f"),
                            in_=stgE[sb][:, :].rearrange("p (c f) -> p c f", f=128),
                        ).then_inc(sxf2[sb].h, 16)
            if x2f_scr is not None:
                eng.wait_ge(sxf2[0].h, 128)
                eng.wait_ge(sxf2[1].h, 128)

        @block.gpsimd
        def _(eng):
            for e in range(NE):
                for j in range(NJ):
                    b = j % 2
                    if e > 0 or j >= 2:
                        eng.wait_ge(mmg.h, e * NJ + j - 1)
                    eng.dma_start(out=wbuf[b][:, 0:2048],
                                  in_=gup.ap()[e, :, j * 128:(j + 1) * 128]
                                  .rearrange("(c p) f -> p c f", p=128)).then_inc(lw[b].h, 16)
                    eng.dma_start(out=wbuf[b][:, 2048:4096],
                                  in_=gup.ap()[e, :, 2048 + j * 128: 2048 + (j + 1) * 128]
                                  .rearrange("(c p) f -> p c f", p=128)).then_inc(lw[b].h, 16)
                for H in range(4):
                    for jj in range(NJ // 4):
                        b = jj % 2
                        gi = (e * 4 + H) * (NJ // 4) + jj
                        if gi >= 2:
                            eng.wait_ge(mmd.h, gi - 1)
                        eng.dma_start(out=wdb[b][:, :].rearrange("p (four f) -> p four f", f=512),
                                      in_=dwn.ap()[e, jj * 512:(jj + 1) * 512, H * 512:(H + 1) * 512]
                                      .rearrange("(four p) f -> p four f", p=128)
                                      ).then_inc(lwd[b].h, 16)

        @block.tensor
        def _(eng):
            eng.wait_ge(lid.h, 16)
            for e in range(NE):
                for t in range(4):
                    eng.wait_ge(cxb.h, e * 4 + t + 1)
                    for kc in range(16):
                        gi = (e * 4 + t) * 16 + kc
                        if e > 0 and t == 0 and kc < 2 and x2f_scr is not None:
                            eng.wait_ge(extE.h, e * 64)
                        if gi >= 2:
                            eng.wait_ge(ext.h, gi - 1)
                        eng.transpose(out=tslot[kc % 2], in_=x1b[:, kc * 128:(kc + 1) * 128],
                                      identity=idn[:, :]).then_inc(tsp.h, 1)
                if e > 0:
                    eng.wait_ge(evd.h, e * 16)
                eng.wait_ge(ext.h, (e * 4 + 4) * 16)
                for j in range(NJ):
                    jb = j % 3
                    eng.wait_ge(lw[j % 2].h, (e * (NJ // 2) + j // 2 + 1) * 32)
                    if e > 0 or j >= 3:
                        eng.wait_ge(gem.h, e * NJ + j - 2)
                    for kc in range(16):
                        eng.matmul(out=pg[jb], lhsT=wbuf[j % 2][:, kc * 128:(kc + 1) * 128],
                                   rhs=X1e[:, kc * 512:(kc + 1) * 512],
                                   start=(kc == 0), stop=(kc == 15))
                    for kc in range(16):
                        mm = eng.matmul(out=pu[jb], lhsT=wbuf[j % 2][:, 2048 + kc * 128:2048 + (kc + 1) * 128],
                                        rhs=X1e[:, kc * 512:(kc + 1) * 512],
                                        start=(kc == 0), stop=(kc == 15))
                    mm.then_inc(mmg.h, 1)
                eng.wait_ge(gem.h, (e + 1) * NJ)
                for H in range(4):
                    if H > 0:
                        eng.wait_ge(evd.h, (e * 4 + H) * 4)
                    for jj in range(NJ // 4):
                        b = jj % 2
                        gi = (e * 4 + H) * (NJ // 4) + jj
                        eng.wait_ge(lwd[b].h, (gi // 2 + 1) * 16)
                        for jq in range(4):
                            j = jj * 4 + jq
                            for t in range(4):
                                mm = eng.matmul(out=pd[t],
                                                lhsT=inter[:, j * 512 + t * 128:j * 512 + (t + 1) * 128],
                                                rhs=wdb[b][:, jq * 512:(jq + 1) * 512],
                                                start=(j == 0), stop=(j == NJ - 1),
                                                skip_group_check=True)
                        mm.then_inc(mmd.h, 1)
                if x2f_scr is not None:
                    for t in range(4):
                        idx = e * 4 + t
                        eng.wait_ge(cxe.h, idx + 1)
                        for kc in range(16):
                            gE = idx * 16 + kc
                            if t == 0 and kc < 2:
                                eng.wait_ge(ext.h, (e + 1) * 64)
                            if gE >= 2:
                                eng.wait_ge(extE.h, gE - 1)
                            eng.transpose(out=tslot[gE % 2], in_=xfb[:, kc * 128:(kc + 1) * 128],
                                          identity=idn[:, :]).then_inc(tspE.h, 1)

        @block.vector
        def _(eng):
            eng = D(eng)
            for e in range(NE):
                for t in range(4):
                    eng.wait_ge(lx.h, (e + 1) * 64)
                    if t > 0:
                        eng.wait_ge(tsp.h, (e * 4 + t) * 16)
                    eng.tensor_copy(out=x1b[:, :], in_=x1g[:, t * 2048:(t + 1) * 2048]).then_inc(cxb.h, 1)
                    for kc in range(16):
                        gi = (e * 4 + t) * 16 + kc
                        eng.wait_ge(tsp.h, gi + 1)
                        eng.tensor_copy(out=X1e[:, kc * 512 + t * 128:kc * 512 + (t + 1) * 128],
                                        in_=tslot[kc % 2]).then_inc(ext.h, 1)
                for j in range(NJ):
                    jb = j % 3
                    eng.wait_ge(geg.h, e * NJ + j + 1)
                    eng.tensor_tensor(out=inter[:, j * 512:(j + 1) * 512], in0=gl[:, :],
                                      in1=pu[jb], op=OP.mult).then_inc(gem.h, 1)
                for H in range(4):
                    eng.wait_ge(mmd.h, (e * 4 + H + 1) * (NJ // 4))
                    for t in range(4):
                        sl = x1g[:, t * 2048 + H * 512:t * 2048 + (H + 1) * 512]
                        eng.tensor_tensor(out=sl, in0=pd[t], in1=sl, op=OP.add).then_inc(evd.h, 1)
                for t in range(4):
                    idx = e * 4 + t
                    sl = x1g[:, t * 2048:(t + 1) * 2048]
                    eng.tensor_reduce(out=mrow[:, 0:1], in_=sl, axis=AX.X, op=OP.add)
                    eng.tensor_scalar(out=mrow[:, 0:1], in0=mrow[:, 0:1], scalar1=1.0 / 2048.0,
                                      scalar2=None, op0=OP.mult)
                    eng.tensor_scalar(out=sl, in0=sl, scalar1=mrow[:, 0:1], scalar2=None,
                                      op0=OP.subtract).then_inc(lnx.h, 1)
                    eng.wait_ge(lnq.h, idx + 1)
                    eng.tensor_reduce(out=mrow[:, 1:2], in_=v4[:, :], axis=AX.X, op=OP.add)
                    eng.tensor_scalar(out=mrow[:, 1:2], in0=mrow[:, 1:2], scalar1=1.0 / 2048.0,
                                      scalar2=1e-5, op0=OP.mult, op1=OP.add)
                    eng.reciprocal(out=mrow[:, 2:3], in_=mrow[:, 1:2]).then_inc(lnr.h, 1)
                    eng.wait_ge(lns.h, idx + 1)
                    eng.tensor_scalar(out=sl, in0=sl, scalar1=mrow[:, 3:4], scalar2=None,
                                      op0=OP.mult).then_inc(x4r.h, 1)
                    if x2f_scr is not None:
                        if idx >= 1:
                            eng.wait_ge(tspE.h, idx * 16)
                        eng.tensor_copy(out=xfb[:, :], in_=sl).then_inc(cxe.h, 1)

        @block.scalar
        def _(eng):
            eng = D(eng)
            for e in range(NE):
                for j in range(NJ):
                    jb = j % 3
                    eng.wait_ge(mmg.h, e * NJ + j + 1)
                    if e > 0 or j >= 1:
                        eng.wait_ge(gem.h, e * NJ + j)
                    eng.activation(out=gl[:, :], in_=pg[jb], func=AF.Silu).then_inc(geg.h, 1)
                for t in range(4):
                    idx = e * 4 + t
                    eng.wait_ge(lnx.h, idx + 1)
                    sl = x1g[:, t * 2048:(t + 1) * 2048]
                    for c in range(4):
                        a = eng.activation(out=gl[:, :], in_=sl[:, c * 512:(c + 1) * 512],
                                           func=AF.Square, accum_out=v4[:, c:c + 1])
                    a.then_inc(lnq.h, 1)
                    eng.wait_ge(lnr.h, idx + 1)
                    eng.sqrt(out=mrow[:, 3:4], in_=mrow[:, 2:3]).then_inc(lns.h, 1)
                    eng.wait_ge(x4r.h, idx + 1)
                    eng.dma_start(out=x2[e * 512 + t * 128: e * 512 + (t + 1) * 128, :],
                                  in_=x1g[:, t * 2048:(t + 1) * 2048]).then_inc(sst.h, 16)
                    if x2f_scr is not None:
                        sb = idx % 2
                        if idx >= 2:
                            eng.wait_ge(sxf2[idx % 2].h, (idx // 2) * 16)
                        for kc in range(16):
                            gE = idx * 16 + kc
                            eng.wait_ge(tspE.h, gE + 1)
                            eng.copy(out=stgE[sb][:, kc * 128:(kc + 1) * 128],
                                     in_=tslot[gE % 2]).then_inc(extE.h, 1)
            eng.wait_ge(sst.h, NE * 64)



def build_l1a():
    nc = bass.Bass("TRN2", target_bir_lowering=False, debug=False)
    x2 = nc.dram_tensor("x2", [2048, 2048], F32, kind="ExternalInput")
    wqkv = nc.dram_tensor("wqkv", [2048, 6144], F32, kind="ExternalInput")
    bqkv = nc.dram_tensor("bqkv", [1, 6144], F32, kind="ExternalInput")
    wout = nc.dram_tensor("wout", [2048, 2048], F32, kind="ExternalInput")
    bout = nc.dram_tensor("bout", [1, 2048], F32, kind="ExternalInput")
    cos2 = nc.dram_tensor("cos2", [2048, 64], F32, kind="ExternalInput")
    sin2 = nc.dram_tensor("sin2", [2048, 64], F32, kind="ExternalInput")
    ident = nc.dram_tensor("ident", [128, 128], BF16, kind="ExternalInput")
    x3 = nc.dram_tensor("x3", [2048, 2048], F32, kind="ExternalOutput")
    qkv_scr = nc.dram_tensor("qkv_scr", [2048, 6144], BF16)

    with contextlib.ExitStack() as st:
        ebuf = lambda name, shape, dt: st.enter_context(nc.sbuf_tensor(name, shape, dt, allow_name_mangling=True))
        R1 = ebuf("R1", [128, 32768], BF16)
        R2 = ebuf("R2", [128, 32768], BF16)
        xin = [ebuf(f"xin{i}", [128, 2048], F32) for i in range(2)]
        xb = ebuf("xb", [128, 2048], BF16)
        idn = ebuf("idn", [128, 128], BF16)
        ones = ebuf("ones", [1, 128], BF16)
        brow = ebuf("brow", [1, 6144], BF16)
        borow = ebuf("borow", [1, 2048], BF16)
        cosb = ebuf("cosb", [128, 16 * 64], F32)
        sinb = ebuf("sinb", [128, 16 * 64], F32)
        m1 = ebuf("m1", [128, 512], F32)
        m2 = ebuf("m2", [128, 512], F32)
        qt = [ebuf(f"qt{i}", [128, 512], BF16) for i in range(2)]
        scb = ebuf("scb", [128, 256], F32)
        qs_buf = ebuf("qs_buf", [128, 2048], BF16)
        exb = ebuf("exb", [128, 256], F32)
        srow = ebuf("srow", [128, 128], F32)
        pb = ebuf("pb", [128, 256], BF16)
        x3b = [ebuf(f"x3b{i}", [128, 2048], F32) for i in range(2)]
        v4 = ebuf("v4", [128, 4], F32)
        mrow = ebuf("mrow", [128, 4], F32)

        tps = st.enter_context(nc.psum_tensor("tps", [128, 2048], BF16, allow_name_mangling=True))
        tslot = [tps[:, 0:128], tps[:, 1024:1152]]
        pq = st.enter_context(nc.psum_tensor("pq", [128, 2048], F32, allow_name_mangling=True))
        pqs = [pq[:, 0:512], pq[:, 512:1024]]

        lid = Sem(st, nc, "lid")
        lbq = Sem(st, nc, "lbq")
        lx2 = [Sem(st, nc, "lx2a"), Sem(st, nc, "lx2b")]
        cxb = Sem(st, nc, "cxb")
        tsp = Sem(st, nc, "tsp")
        ext = Sem(st, nc, "ext")
        lwq = [Sem(st, nc, "lwq0"), Sem(st, nc, "lwq1")]
        mmq = Sem(st, nc, "mmq")
        evq = Sem(st, nc, "evq")
        sq2 = [Sem(st, nc, "sq0"), Sem(st, nc, "sq1")]
        lqr = Sem(st, nc, "lqr")
        lqs2 = Sem(st, nc, "lqs2")
        sxd = Sem(st, nc, "sxd")
        sxe = Sem(st, nc, "sxe")
        vcx = Sem(st, nc, "vcx")
        lwo = Sem(st, nc, "lwo")
        mmo = Sem(st, nc, "mmo")
        evo = Sem(st, nc, "evo")
        lnx = Sem(st, nc, "lnx")
        lnq = Sem(st, nc, "lnq")
        lnr = Sem(st, nc, "lnr")
        lns = Sem(st, nc, "lns")
        x3r = Sem(st, nc, "x3r")
        sst2 = [Sem(st, nc, "ssta"), Sem(st, nc, "sstb")]
        block = st.enter_context(nc.Block())

        NQ = 192

        def wb(cb):
            return R2[:, cb * 16384:(cb + 1) * 16384]

        def kvr(slot):
            return R2[:, slot * 4096:(slot + 1) * 4096]

        def ctxf(kc):
            return R2[:, kc * 2048:(kc + 1) * 2048]

        def x2f(kc):
            return R1[:, kc * 2048:(kc + 1) * 2048]

        def ctxbf(tile):
            return R1[:, tile * 2048:(tile + 1) * 2048]

        def woutr(kc):
            return R1[:, kc * 2048:(kc + 1) * 2048]

        @block.sync
        def _(eng):
            eng.dma_start(out=idn[:, :], in_=ident[:, :]).then_inc(lid.h, 16)
            eng.dma_start(out=cosb[:, :], in_=cos2.ap().rearrange("(t p) f -> p t f", p=128)).then_inc(lid.h, 16)
            eng.dma_start(out=sinb[:, :], in_=sin2.ap().rearrange("(t p) f -> p t f", p=128)).then_inc(lid.h, 16)
            for i in range(16):
                b = i % 2
                if i >= 2:
                    eng.wait_ge(cxb.h, i - 1)
                eng.dma_start(out=xin[b][:, :], in_=x2[i * 128:(i + 1) * 128, :]).then_inc(lx2[b].h, 16)
            for qi in range(NQ):
                eng.wait_ge(evq.h, qi + 1)
                c, i = qi // 16, qi % 16
                eng.dma_start(out=qkv_scr[i * 128:(i + 1) * 128, c * 512:(c + 1) * 512],
                              in_=qt[qi % 2][:, :]).then_inc(sq2[qi % 2].h, 16)
            lqs = lqr
            for b in range(2):
                if b == 1:
                    eng.wait_ge(vcx.h, 8)
                else:
                    eng.wait_ge(sq2[0].h, (NQ // 2) * 16)
                    eng.wait_ge(sq2[1].h, (NQ // 2) * 16)
                for s in range(8):
                    tile = 2 * s + b
                    eng.dma_start(out=kvr(s), in_=qkv_scr[tile * 128:(tile + 1) * 128, 2048:6144]).then_inc(lqr.h, 16)
                for s in range(8):
                    bs = b * 8 + s
                    if bs > 0:
                        eng.wait_ge(sxd.h, bs)
                    tile = 2 * s + b
                    eng.dma_start(out=qs_buf[:, :], in_=qkv_scr[tile * 128:(tile + 1) * 128, 0:2048]).then_inc(lqs2.h, 16)
            eng.wait_ge(vcx.h, 16)
            for i in range(16):
                b = i % 2
                if i >= 2:
                    eng.wait_ge(evo.h, (i - 2) * 4 + 4)
                eng.dma_start(out=xin[b][:, :], in_=x2[i * 128:(i + 1) * 128, :]).then_inc(lx2[b].h, 16)

        @block.gpsimd
        def _(eng):
            eng.dma_start(out=brow[:, :], in_=bqkv[:, :]).then_inc(lbq.h, 16)
            eng.dma_start(out=borow[:, :], in_=bout[:, :]).then_inc(lbq.h, 16)
            for c in range(12):
                cb = c % 2
                if c >= 2:
                    eng.wait_ge(mmq.h, (c - 1) * 16)
                for kc in range(16):
                    eng.dma_start(out=wb(cb)[:, kc * 512:(kc + 1) * 512],
                                  in_=wqkv[kc * 128:(kc + 1) * 128, c * 512:(c + 1) * 512]
                                  ).then_inc(lwq[cb].h, 16)
            eng.wait_ge(tsp.h, 256 + 256)
            for kc in range(16):
                eng.dma_start(out=woutr(kc), in_=wout[kc * 128:(kc + 1) * 128, :]).then_inc(lwo.h, 16)

        @block.tensor
        def _(eng):
            eng.wait_ge(lid.h, 48)
            for i in range(16):
                eng.wait_ge(cxb.h, i + 1)
                for kc in range(16):
                    gi = i * 16 + kc
                    if gi >= 2:
                        eng.wait_ge(ext.h, gi - 1)
                    eng.transpose(out=tslot[kc % 2], in_=xb[:, kc * 128:(kc + 1) * 128],
                                  identity=idn[:, :]).then_inc(tsp.h, 1)
            eng.wait_ge(lbq.h, 32)
            eng.wait_ge(ext.h, 256)
            for qi in range(NQ):
                c, i = qi // 16, qi % 16
                cb = c % 2
                if i == 0:
                    eng.wait_ge(lwq[cb].h, (c // 2 + 1) * 256)
                if qi >= 2:
                    eng.wait_ge(evq.h, qi - 1)
                for kc in range(16):
                    eng.matmul(out=pqs[qi % 2], lhsT=x2f(kc)[:, i * 128:(i + 1) * 128],
                               rhs=wb(cb)[:, kc * 512:(kc + 1) * 512],
                               start=(kc == 0), stop=False)
                eng.matmul(out=pqs[qi % 2], lhsT=ones[0:1, :], rhs=brow[0:1, c * 512:(c + 1) * 512],
                           start=False, stop=True).then_inc(mmq.h, 1)
            eng.wait_ge(vcx.h, 16)
            for tile in range(16):
                for kc in range(16):
                    gi = 256 + tile * 16 + kc
                    eng.wait_ge(ext.h, gi - 1)
                    eng.transpose(out=tslot[kc % 2], in_=ctxbf(tile)[:, kc * 128:(kc + 1) * 128],
                                  identity=idn[:, :]).then_inc(tsp.h, 1)
            eng.wait_ge(lwo.h, 256)
            eng.wait_ge(ext.h, 512)
            for i in range(16):
                for c in range(4):
                    oi = i * 4 + c
                    if oi >= 2:
                        eng.wait_ge(evo.h, oi - 1)
                    for kc in range(16):
                        eng.matmul(out=pqs[oi % 2], lhsT=ctxf(kc)[:, i * 128:(i + 1) * 128],
                                   rhs=woutr(kc)[:, c * 512:(c + 1) * 512],
                                   start=(kc == 0), stop=False)
                    eng.matmul(out=pqs[oi % 2], lhsT=ones[0:1, :], rhs=borow[0:1, c * 512:(c + 1) * 512],
                               start=False, stop=True).then_inc(mmo.h, 1)

        @block.vector
        def _(eng):
            eng = D(eng)
            eng.memset(ones[:, :], 1.0)
            for i in range(16):
                b = i % 2
                eng.wait_ge(lx2[b].h, (i // 2 + 1) * 16)
                if i > 0:
                    eng.wait_ge(tsp.h, i * 16)
                eng.tensor_copy(out=xb[:, :], in_=xin[b][:, :]).then_inc(cxb.h, 1)
                for kc in range(16):
                    gi = i * 16 + kc
                    eng.wait_ge(tsp.h, gi + 1)
                    eng.tensor_copy(out=x2f(kc)[:, i * 128:(i + 1) * 128],
                                    in_=tslot[kc % 2]).then_inc(ext.h, 1)
            for qi in range(NQ):
                c, i = qi // 16, qi % 16
                eng.wait_ge(mmq.h, qi + 1)
                if qi >= 2:
                    eng.wait_ge(sq2[qi % 2].h, (qi // 2) * 16)
                ps = pqs[qi % 2]
                if c < 8:
                    cs = cosb[:, i * 64:(i + 1) * 64].rearrange("p (o f) -> p o f", o=1)
                    sn = sinb[:, i * 64:(i + 1) * 64].rearrange("p (o f) -> p o f", o=1)
                    eng.tensor_tensor(out=m1[:, :].rearrange("p (h f) -> p h f", f=64),
                                      in0=ps.rearrange("p (h f) -> p h f", f=64),
                                      in1=cs.broadcast_to([128, 8, 64]), op=OP.mult)
                    eng.tensor_tensor(out=m2[:, :].rearrange("p (h f) -> p h f", f=64),
                                      in0=ps.rearrange("p (h f) -> p h f", f=64),
                                      in1=sn.broadcast_to([128, 8, 64]), op=OP.mult)
                    qv = qt[qi % 2][:, :].rearrange("p (h two f) -> p h two f", two=2, f=32)
                    eng.tensor_tensor(out=qv[:, :, 0, :],
                                      in0=m1[:, :].rearrange("p (h two f) -> p h two f", two=2, f=32)[:, :, 0, :],
                                      in1=m2[:, :].rearrange("p (h two f) -> p h two f", two=2, f=32)[:, :, 1, :],
                                      op=OP.subtract)
                    eng.tensor_tensor(out=qv[:, :, 1, :],
                                      in0=m1[:, :].rearrange("p (h two f) -> p h two f", two=2, f=32)[:, :, 1, :],
                                      in1=m2[:, :].rearrange("p (h two f) -> p h two f", two=2, f=32)[:, :, 0, :],
                                      op=OP.add).then_inc(evq.h, 1)
                else:
                    eng.tensor_copy(out=qt[qi % 2][:, :], in_=ps).then_inc(evq.h, 1)
            ctxs = xin[0]
            ctmp = xin[1]
            for b in range(2):
                eng.wait_ge(lqr.h, (b + 1) * 128)
                for s in range(8):
                    bs = b * 8 + s
                    eng.wait_ge(lqs2.h, (bs + 1) * 16)
                    q_s = qs_buf[:, :].rearrange("p (h f) -> p h f", f=64)
                    for t in range(s + 1):
                        k_t = kvr(t)[:, 0:2048].rearrange("p (h f) -> p h f", f=64)
                        eng.tensor_tensor(out=ctmp[:, :].rearrange("p (h f) -> p h f", f=64),
                                          in0=q_s, in1=k_t, op=OP.mult)
                        eng.tensor_reduce(
                            out=scb[:, :].rearrange("p (h t) -> p h t", t=8)[:, :, t],
                            in_=ctmp[:, :].rearrange("p (h f) -> p h f", f=64),
                            axis=AX.X, op=OP.add)
                    sc3 = scb[:, :].rearrange("p (h t) -> p h t", t=8)
                    eng.tensor_reduce(out=srow[:, 0:32], in_=sc3[:, :, 0:s + 1], axis=AX.X, op=OP.max)
                    eng.tensor_tensor(out=exb[:, :].rearrange("p (h t) -> p h t", t=8)[:, :, 0:s + 1],
                                      in0=sc3[:, :, 0:s + 1],
                                      in1=srow[:, 0:32].unsqueeze(2).broadcast_to([128, 32, s + 1]),
                                      op=OP.subtract).then_inc(sxd.h, 1)
                    eng.wait_ge(sxe.h, bs + 1)
                    ex3 = exb[:, :].rearrange("p (h t) -> p h t", t=8)
                    eng.tensor_reduce(out=srow[:, 32:64], in_=ex3[:, :, 0:s + 1], axis=AX.X, op=OP.add)
                    eng.reciprocal(out=srow[:, 64:96], in_=srow[:, 32:64])
                    eng.tensor_tensor(out=pb[:, :].rearrange("p (h t) -> p h t", t=8)[:, :, 0:s + 1],
                                      in0=ex3[:, :, 0:s + 1],
                                      in1=srow[:, 64:96].unsqueeze(2).broadcast_to([128, 32, s + 1]),
                                      op=OP.mult)
                    pb3 = pb[:, :].rearrange("p (h t) -> p h t", t=8)
                    for t in range(s + 1):
                        v_t = kvr(t)[:, 2048:4096].rearrange("p (h f) -> p h f", f=64)
                        pr = pb3[:, :, t].unsqueeze(2).broadcast_to([128, 32, 64])
                        if t == 0:
                            eng.tensor_tensor(out=ctxs[:, :].rearrange("p (h f) -> p h f", f=64),
                                              in0=v_t, in1=pr, op=OP.mult)
                        else:
                            eng.tensor_tensor(out=ctmp[:, :].rearrange("p (h f) -> p h f", f=64),
                                              in0=v_t, in1=pr, op=OP.mult)
                            eng.tensor_tensor(out=ctxs[:, :], in0=ctxs[:, :], in1=ctmp[:, :], op=OP.add)
                    eng.tensor_copy(out=ctxbf(2 * s + b)[:, :], in_=ctxs[:, :]).then_inc(vcx.h, 1)
            for tile in range(16):
                for kc in range(16):
                    gi = 256 + tile * 16 + kc
                    eng.wait_ge(tsp.h, gi + 1)
                    eng.tensor_copy(out=ctxf(kc)[:, tile * 128:(tile + 1) * 128],
                                    in_=tslot[kc % 2]).then_inc(ext.h, 1)
            for i in range(16):
                bi = i % 2
                eng.wait_ge(lx2[bi].h, (8 + i // 2 + 1) * 16)
                if i >= 2:
                    eng.wait_ge(sst2[bi].h, (i // 2) * 16)
                for c in range(4):
                    oi = i * 4 + c
                    eng.wait_ge(mmo.h, oi + 1)
                    eng.tensor_tensor(out=x3b[bi][:, c * 512:(c + 1) * 512], in0=pqs[oi % 2],
                                      in1=xin[bi][:, c * 512:(c + 1) * 512], op=OP.add).then_inc(evo.h, 1)
                idx = i
                sl = x3b[bi][:, :]
                eng.tensor_reduce(out=mrow[:, 0:1], in_=sl, axis=AX.X, op=OP.add)
                eng.tensor_scalar(out=mrow[:, 0:1], in0=mrow[:, 0:1], scalar1=1.0 / 2048.0,
                                  scalar2=None, op0=OP.mult)
                eng.tensor_scalar(out=sl, in0=sl, scalar1=mrow[:, 0:1], scalar2=None,
                                  op0=OP.subtract).then_inc(lnx.h, 1)
                eng.wait_ge(lnq.h, idx + 1)
                eng.tensor_reduce(out=mrow[:, 1:2], in_=v4[:, :], axis=AX.X, op=OP.add)
                eng.tensor_scalar(out=mrow[:, 1:2], in0=mrow[:, 1:2], scalar1=1.0 / 2048.0,
                                  scalar2=1e-5, op0=OP.mult, op1=OP.add)
                eng.reciprocal(out=mrow[:, 2:3], in_=mrow[:, 1:2]).then_inc(lnr.h, 1)
                eng.wait_ge(lns.h, idx + 1)
                eng.tensor_scalar(out=sl, in0=sl, scalar1=mrow[:, 3:4], scalar2=None,
                                  op0=OP.mult).then_inc(x3r.h, 1)

        @block.scalar
        def _(eng):
            eng = D(eng)
            for b in range(2):
                for s in range(8):
                    bs = b * 8 + s
                    eng.wait_ge(sxd.h, bs + 1)
                    ex3 = exb[:, :].rearrange("p (h t) -> p h t", t=8)
                    eng.activation(out=ex3[:, :, 0:s + 1], in_=ex3[:, :, 0:s + 1],
                                   func=AF.Exp, scale=0.125).then_inc(sxe.h, 1)
            for i in range(16):
                bi = i % 2
                idx = i
                eng.wait_ge(lnx.h, idx + 1)
                sl = x3b[bi][:, :]
                for c in range(4):
                    a = eng.activation(out=m1[:, :], in_=sl[:, c * 512:(c + 1) * 512],
                                       func=AF.Square, accum_out=v4[:, c:c + 1])
                a.then_inc(lnq.h, 1)
                eng.wait_ge(lnr.h, idx + 1)
                eng.sqrt(out=mrow[:, 3:4], in_=mrow[:, 2:3]).then_inc(lns.h, 1)
                eng.wait_ge(x3r.h, i + 1)
                eng.dma_start(out=x3[i * 128:(i + 1) * 128, :], in_=x3b[bi][:, :]).then_inc(sst2[i % 2].h, 16)
            eng.wait_ge(sst2[0].h, 128)
            eng.wait_ge(sst2[1].h, 128)
    return nc



def build_l0a():
    nc = bass.Bass("TRN2", target_bir_lowering=False, debug=False)
    x = nc.dram_tensor("x", [2048, 2048], F32, kind="ExternalInput")
    wq = nc.dram_tensor("wq", [2048, 3072], F32, kind="ExternalInput")
    wkvd = nc.dram_tensor("wkvd", [2048, 512], F32, kind="ExternalInput")
    wkvu = nc.dram_tensor("wkvu", [512, 3072], F32, kind="ExternalInput")
    wproj = nc.dram_tensor("wproj", [2048, 2048], F32, kind="ExternalInput")
    qgain = nc.dram_tensor("qgain", [1, 16], F32, kind="ExternalInput")
    cos1 = nc.dram_tensor("cos1", [2048, 32], F32, kind="ExternalInput")
    sin1 = nc.dram_tensor("sin1", [2048, 32], F32, kind="ExternalInput")
    trimask = nc.dram_tensor("trimask", [128, 128], F32, kind="ExternalInput")
    ident = nc.dram_tensor("ident", [128, 128], BF16, kind="ExternalInput")
    x1 = nc.dram_tensor("x1", [2048, 2048], F32, kind="ExternalOutput")
    qk_scr = nc.dram_tensor("qk_scr", [2048, 3072], F32)
    RSQ = 0.08838834764831843
    NP = 4

    emit_l0a(nc, x, wq, wkvd, wkvu, wproj, qgain, cos1, sin1, trimask, ident, x1, qk_scr)
    return nc


def emit_l0a(nc, x, wq, wkvd, wkvu, wproj, qgain, cos1, sin1, trimask, ident, x1, qk_scr):
    RSQ = 0.08838834764831843
    NP = 4
    with contextlib.ExitStack() as st:
        ebuf = lambda name, shape, dt: st.enter_context(nc.sbuf_tensor(name, shape, dt, allow_name_mangling=True))
        R1 = ebuf("R1", [128, 32768], BF16)
        R2 = ebuf("R2", [128, 32768], BF16)
        Vsb = ebuf("Vsb", [128, 8192], BF16)
        Ckv = ebuf("Ckv", [128, 8192], BF16)
        wkvb = [ebuf(f"wkvb{i}", [128, 3072], BF16) for i in range(2)]
        knb = [ebuf(f"knb{i}", [128, 256], BF16) for i in range(2)]
        xin = [ebuf(f"xin{i}", [128, 2048], F32) for i in range(2)]
        U = ebuf("U", [128, 2048], F32)
        qkt = [U[:, 0:512], U[:, 512:1024]]
        cosb = ebuf("cosb", [128, 16 * 32], F32)
        sinb = ebuf("sinb", [128, 16 * 32], F32)
        trim = ebuf("trim", [128, 128], F32)
        idn = ebuf("idn", [128, 128], BF16)
        ones = ebuf("ones", [1, 128], BF16)
        qgb = ebuf("qgb", [1, 16], BF16)
        gq = ebuf("gq", [128, 16], F32)
        rst = ebuf("rst", [128, 12], F32)
        sc4 = ebuf("sc4", [128, 12], F32)
        srow = ebuf("srow", [128, 12], F32)
        ssc = ebuf("ssc", [128, 512], F32)
        x1b = U
        v4 = ebuf("v4", [128, 4], F32)
        mrow = ebuf("mrow", [128, 4], F32)

        ps = st.enter_context(nc.psum_tensor("ps", [128, 2048], F32, allow_name_mangling=True))
        tps = st.enter_context(nc.psum_tensor("tps", [128, 3072], BF16, allow_name_mangling=True))
        tslot = [tps[:, 0:128], tps[:, 1024:1152], tps[:, 2048:2176]]
        pc = st.enter_context(nc.psum_tensor("pc", [128, 512], F32, allow_name_mangling=True))
        pb2 = [ps[:, 0:512], ps[:, 512:1024]]
        pb4 = [ps[:, i * 512:(i + 1) * 512] for i in range(4)]

        xfeat = lambda kc: R1[:, kc * 2048:(kc + 1) * 2048]
        ctxsb = lambda i: R1[:, i * 2048:(i + 1) * 2048]
        wpr = lambda kc: R1[:, kc * 2048:(kc + 1) * 2048]
        wqb = lambda cb: R2[:, cb * 8192:(cb + 1) * 8192]
        ctb = lambda i: R2[:, 16384 + i * 512:16384 + (i + 1) * 512]
        xb_a = R2[:, 24576:26624]
        qf = lambda hl: R2[:, hl * 2048:(hl + 1) * 2048]
        kf = lambda hl: R2[:, 8192 + hl * 2048:8192 + (hl + 1) * 2048]
        probs = R2[:, 16384:18432]
        probsT = R2[:, 18432:20480]
        prepb = lambda i: R2[:, 20480 + i * 768:20480 + (i + 1) * 768]
        ctxf = lambda kc: R2[:, kc * 2048:(kc + 1) * 2048]

        lid = Sem(st, nc, "lid")
        lgq = Sem(st, nc, "lgq")
        lx2 = [Sem(st, nc, "lx2a"), Sem(st, nc, "lx2b")]
        cxb = Sem(st, nc, "cxb")
        tspA = Sem(st, nc, "tspA"); extA = Sem(st, nc, "extA")
        lwq = [Sem(st, nc, "lwq0"), Sem(st, nc, "lwq1")]
        mmq = Sem(st, nc, "mmq"); evq = Sem(st, nc, "evq")
        sq2 = [Sem(st, nc, "sq0"), Sem(st, nc, "sq1")]
        cv1 = Sem(st, nc, "cv1"); cv2 = Sem(st, nc, "cv2"); cv3 = Sem(st, nc, "cv3")
        tspB = Sem(st, nc, "tspB"); extB = Sem(st, nc, "extB")
        gqd = Sem(st, nc, "gqd")
        lqk = [Sem(st, nc, "lqk0"), Sem(st, nc, "lqk1")]
        pq1 = Sem(st, nc, "pq1"); pv1 = Sem(st, nc, "pv1")
        pq2 = Sem(st, nc, "pq2"); pv2 = Sem(st, nc, "pv2")
        tspC1 = Sem(st, nc, "tspC1"); extC1 = Sem(st, nc, "extC1")
        lwk = [Sem(st, nc, "lwk0"), Sem(st, nc, "lwk1")]
        mmk = Sem(st, nc, "mmk")
        kq1 = Sem(st, nc, "kq1"); kv1 = Sem(st, nc, "kv1")
        kq2 = Sem(st, nc, "kq2"); kv2 = Sem(st, nc, "kv2")
        tspC2 = Sem(st, nc, "tspC2"); extC2 = Sem(st, nc, "extC2")
        vve = Sem(st, nc, "vve")
        mms = Sem(st, nc, "mms"); vmk = Sem(st, nc, "vmk"); sxe = Sem(st, nc, "sxe")
        vrc = Sem(st, nc, "vrc"); mpv = Sem(st, nc, "mpv"); vcx = Sem(st, nc, "vcx")
        tspC4 = Sem(st, nc, "tspC4"); extC4 = Sem(st, nc, "extC4")
        tspD = Sem(st, nc, "tspD"); extD = Sem(st, nc, "extD")
        lwo = Sem(st, nc, "lwo")
        mmo = Sem(st, nc, "mmo"); evo = Sem(st, nc, "evo")
        lnx = Sem(st, nc, "lnx"); lnq = Sem(st, nc, "lnq")
        lnr = Sem(st, nc, "lnr"); lns = Sem(st, nc, "lns")
        x1r = Sem(st, nc, "x1r"); sst = Sem(st, nc, "sst")
        block = st.enter_context(nc.Block())

        c4iters = [(hl, qt) for hl in range(4) for qt in range(16)]
        NTRP = sum(qt + 1 for _, qt in c4iters)
        c4nch = [qt // 4 + 1 for _, qt in c4iters]
        NCHP = sum(c4nch)

        def c4base(g, it):
            return g * NTRP + sum(c4iters[k][1] + 1 for k in range(it))

        def cbase(g, it):
            return g * NCHP + sum(c4nch[:it])

        @block.sync
        def _(eng):
            eng.dma_start(out=idn[:, :], in_=ident[:, :]).then_inc(lid.h, 16)
            eng.dma_start(out=cosb[:, :], in_=cos1.ap().rearrange("(t p) f -> p t f", p=128)).then_inc(lid.h, 16)
            eng.dma_start(out=sinb[:, :], in_=sin1.ap().rearrange("(t p) f -> p t f", p=128)).then_inc(lid.h, 16)
            eng.dma_start(out=trim[:, :], in_=trimask[:, :]).then_inc(lid.h, 16)
            for i in range(16):
                b = i % 2
                if i >= 2:
                    eng.wait_ge(cxb.h, i)
                eng.dma_start(out=xin[b][:, :], in_=x[i * 128:(i + 1) * 128, :]).then_inc(lx2[b].h, 16)
            for c in range(6):
                for i in range(16):
                    qi = c * 16 + i
                    eng.wait_ge(evq.h, qi + 1)
                    eng.dma_start(out=qk_scr[i * 128:(i + 1) * 128, c * 512:(c + 1) * 512],
                                  in_=qkt[qi % 2]).then_inc(sq2[qi % 2].h, 16)
            for g in range(NP):
                for i in range(16):
                    gi = g * 16 + i
                    b = i % 2
                    if gi < 2:
                        eng.wait_ge(sq2[0].h, 48 * 16)
                        eng.wait_ge(sq2[1].h, 48 * 16)
                        eng.wait_ge(cxb.h, 17)
                    else:
                        eng.wait_ge(pv2.h, gi - 1)
                    eng.dma_start(out=xin[b][:, 0:768],
                                  in_=qk_scr[i * 128:(i + 1) * 128, g * 768:(g + 1) * 768]).then_inc(lqk[b].h, 16)
            eng.wait_ge(vcx.h, NP * 64)
            for i in range(16):
                b = i % 2
                if i >= 2:
                    eng.wait_ge(evo.h, (i - 2) * 4 + 4)
                eng.dma_start(out=xin[b][:, :], in_=x[i * 128:(i + 1) * 128, :]).then_inc(lx2[b].h, 16)

        @block.gpsimd
        def _(eng):
            eng.dma_start(out=qgb[:, :], in_=qgain[:, :]).then_inc(lgq.h, 16)
            for c in range(7):
                cb = c % 2
                if c >= 2:
                    eng.wait_ge(mmq.h, (c - 1) * 16)
                for kc in range(16):
                    if c < 6:
                        src = wq[kc * 128:(kc + 1) * 128, c * 512:(c + 1) * 512]
                    else:
                        src = wkvd[kc * 128:(kc + 1) * 128, :]
                    eng.dma_start(out=wqb(cb)[:, kc * 512:(kc + 1) * 512], in_=src).then_inc(lwq[cb].h, 16)
            for g in range(NP):
                kb = g % 2
                if g >= 2:
                    eng.wait_ge(mmk.h, (g - 1) * 32)
                for lc in range(4):
                    eng.dma_start(out=wkvb[kb][:, lc * 256:(lc + 1) * 256],
                                  in_=wkvu[lc * 128:(lc + 1) * 128, g * 256:(g + 1) * 256]).then_inc(lwk[kb].h, 16)
                for lc in range(4):
                    eng.dma_start(out=wkvb[kb][:, 1024 + lc * 512:1024 + (lc + 1) * 512],
                                  in_=wkvu[lc * 128:(lc + 1) * 128,
                                           1024 + g * 512:1024 + (g + 1) * 512]
                                  ).then_inc(lwk[kb].h, 16)
            eng.wait_ge(tspD.h, 256)
            for kc in range(16):
                eng.dma_start(out=wpr(kc), in_=wproj[kc * 128:(kc + 1) * 128, :]).then_inc(lwo.h, 16)

        @block.tensor
        def _(eng):
            eng.wait_ge(lid.h, 64)
            eng.wait_ge(lgq.h, 16)
            eng.wait_ge(cxb.h, 1)
            eng.matmul(out=pc[:, 0:16], lhsT=ones[0:1, :], rhs=qgb[0:1, :],
                       start=True, stop=True).then_inc(gqd.h, 1)
            for i in range(16):
                eng.wait_ge(cxb.h, i + 2)
                for kc in range(16):
                    gi = i * 16 + kc
                    if gi >= 3:
                        eng.wait_ge(extA.h, gi - 2)
                    eng.transpose(out=tslot[gi % 3], in_=xb_a[:, kc * 128:(kc + 1) * 128],
                                  identity=idn[:, :]).then_inc(tspA.h, 1)
            eng.wait_ge(extA.h, 256)
            for c in range(7):
                cb = c % 2
                for i in range(16):
                    qi = c * 16 + i
                    if i == 0:
                        eng.wait_ge(lwq[cb].h, (c // 2 + 1) * 256)
                    if qi >= 4:
                        eng.wait_ge(evq.h, qi - 3)
                    for kc in range(16):
                        mm = eng.matmul(out=pb4[qi % 4], lhsT=xfeat(kc)[:, i * 128:(i + 1) * 128],
                                        rhs=wqb(cb)[:, kc * 512:(kc + 1) * 512],
                                        start=(kc == 0), stop=(kc == 15))
                    mm.then_inc(mmq.h, 1)
            eng.wait_ge(evq.h, 112)
            for i in range(16):
                for j in range(4):
                    gi = i * 4 + j
                    if gi < 3:
                        eng.wait_ge(extA.h, 256)
                    else:
                        eng.wait_ge(extB.h, gi - 2)
                    eng.transpose(out=tslot[gi % 3], in_=ctb(i)[:, j * 128:(j + 1) * 128],
                                  identity=idn[:, :]).then_inc(tspB.h, 1)
            for g in range(NP):
                kb = g % 2
                for i in range(16):
                    eng.wait_ge(pv2.h, g * 16 + i + 1)
                    for bidx in range(12):
                        tc = g * 192 + i * 12 + bidx
                        if i == 0 and bidx < 3:
                            if g == 0:
                                eng.wait_ge(extB.h, 64)
                            else:
                                eng.wait_ge(extC4.h, g * NTRP)
                            if tc >= 3:
                                eng.wait_ge(extC1.h, tc - 2)
                        else:
                            eng.wait_ge(extC1.h, tc - 2)
                        src = prepb(i)[:, bidx * 64:(bidx + 1) * 64]
                        eng.transpose(out=tslot[tc % 3][0:64, :], in_=src,
                                      identity=idn[:, :]).then_inc(tspC1.h, 1)
                for i in range(17):
                    if i < 16:
                        ki = g * 32 + i
                        if i == 0:
                            eng.wait_ge(lwk[kb].h, (g // 2 + 1) * 128)
                            if g == 0:
                                eng.wait_ge(extB.h, 64)
                        if i >= 2:
                            eng.wait_ge(kv2.h, g * 16 + i - 1)
                        elif g > 0:
                            eng.wait_ge(vve.h, g * 16 + i - 1)
                        for lc in range(4):
                            mm = eng.matmul(out=pb2[ki % 2][:, 0:256],
                                            lhsT=Ckv[:, lc * 2048 + i * 128:lc * 2048 + (i + 1) * 128],
                                            rhs=wkvb[kb][:, lc * 256:(lc + 1) * 256],
                                            start=(lc == 0), stop=(lc == 3))
                        mm.then_inc(mmk.h, 1)
                    if i >= 1:
                        ii = i - 1
                        eng.wait_ge(kv2.h, g * 16 + ii + 1)
                        for hl in range(4):
                            tc = g * 64 + ii * 4 + hl
                            if ii == 0 and hl < 3:
                                eng.wait_ge(extC1.h, (g + 1) * 192)
                                if tc >= 3:
                                    eng.wait_ge(extC2.h, tc - 2)
                            else:
                                eng.wait_ge(extC2.h, tc - 2)
                            eng.transpose(out=tslot[tc % 3][0:64, :], in_=knb[ii % 2][:, hl * 64:(hl + 1) * 64],
                                          identity=idn[:, :]).then_inc(tspC2.h, 1)
                for i in range(16):
                    ki = g * 32 + 16 + i
                    if i == 0:
                        eng.wait_ge(extC2.h, (g + 1) * 64)
                    if i < 2:
                        eng.wait_ge(kv2.h, (g + 1) * 16)
                    else:
                        eng.wait_ge(vve.h, g * 16 + i - 1)
                    for lc in range(4):
                        mm = eng.matmul(out=pb2[ki % 2],
                                        lhsT=Ckv[:, lc * 2048 + i * 128:lc * 2048 + (i + 1) * 128],
                                        rhs=wkvb[kb][:, 1024 + lc * 512:1024 + (lc + 1) * 512],
                                        start=(lc == 0), stop=(lc == 3))
                    mm.then_inc(mmk.h, 1)
                eng.wait_ge(vve.h, (g + 1) * 16)
                for it, (hl, qt) in enumerate(c4iters):
                    gidx = g * 64 + it
                    nk = (qt + 1) * 128
                    nch = c4nch[it]
                    cb = cbase(g, it)
                    if gidx >= 1:
                        eng.wait_ge(sxe.h, cb)
                    for cch in range(nch):
                        lo = cch * 512
                        mm = eng.matmul(out=ps[:, lo:min(nk, lo + 512)],
                                        lhsT=qf(hl)[:, qt * 128:(qt + 1) * 128],
                                        rhs=kf(hl)[:, lo:min(nk, lo + 512)],
                                        start=True, stop=True)
                        mm.then_inc(mms.h, 1)
                    eng.wait_ge(vcx.h, gidx)
                    for kt in range(qt + 1):
                        tc = c4base(g, it) + kt
                        eng.wait_ge(sxe.h, cb + kt // 4 + 1)
                        if tc < g * NTRP + 3:
                            eng.wait_ge(extC2.h, (g + 1) * 64)
                            if tc >= 3:
                                eng.wait_ge(extC4.h, tc - 2)
                        else:
                            eng.wait_ge(extC4.h, tc - 2)
                        eng.transpose(out=tslot[tc % 3], in_=probs[:, kt * 128:(kt + 1) * 128],
                                      identity=idn[:, :]).then_inc(tspC4.h, 1)
                    for kt in range(qt + 1):
                        tc = c4base(g, it) + kt
                        eng.wait_ge(extC4.h, tc + 1)
                        mm = eng.matmul(out=pc[:, 0:128],
                                        lhsT=probsT[:, kt * 128:(kt + 1) * 128],
                                        rhs=Vsb[:, kt * 512 + hl * 128:kt * 512 + (hl + 1) * 128],
                                        start=(kt == 0), stop=(kt == qt))
                    mm.then_inc(mpv.h, 1)
            eng.wait_ge(vcx.h, NP * 64)
            for i in range(16):
                for kc in range(16):
                    gi = i * 16 + kc
                    if gi < 3:
                        eng.wait_ge(extC4.h, NP * NTRP)
                    else:
                        eng.wait_ge(extD.h, gi - 2)
                    eng.transpose(out=tslot[gi % 3], in_=ctxsb(i)[:, kc * 128:(kc + 1) * 128],
                                  identity=idn[:, :]).then_inc(tspD.h, 1)
            eng.wait_ge(lwo.h, 256)
            eng.wait_ge(extD.h, 256)
            for i in range(16):
                for c in range(4):
                    oi = i * 4 + c
                    if oi >= 4:
                        eng.wait_ge(evo.h, oi - 3)
                    for kc in range(16):
                        mm = eng.matmul(out=pb4[oi % 4], lhsT=ctxf(kc)[:, i * 128:(i + 1) * 128],
                                        rhs=wpr(kc)[:, c * 512:(c + 1) * 512],
                                        start=(kc == 0), stop=(kc == 15))
                    mm.then_inc(mmo.h, 1)

        @block.vector
        def _(eng):
            eng = D(eng)
            eng.memset(ones[:, :], 1.0)
            eng.engine_nop().then_inc(cxb.h, 1)
            for i in range(16):
                b = i % 2
                eng.wait_ge(lx2[b].h, (i // 2 + 1) * 16)
                if i > 0:
                    eng.wait_ge(tspA.h, i * 16)
                eng.tensor_copy(out=xb_a[:, :], in_=xin[b][:, :]).then_inc(cxb.h, 1)
            eng.wait_ge(gqd.h, 1)
            eng.tensor_scalar(out=gq[:, :], in0=pc[:, 0:16], scalar1=RSQ, scalar2=None,
                              op0=OP.mult).then_inc(cxb.h, 1)
            for c in range(7):
                for i in range(16):
                    qi = c * 16 + i
                    eng.wait_ge(mmq.h, qi + 1)
                    if c < 6:
                        if qi >= 2:
                            eng.wait_ge(sq2[qi % 2].h, (qi // 2) * 16)
                        eng.tensor_copy(out=qkt[qi % 2], in_=pb4[qi % 4]).then_inc(evq.h, 1)
                    else:
                        eng.wait_ge(cv1.h, i + 1)
                        eng.tensor_scalar(out=srow[:, 0:1], in0=srow[:, 0:1], scalar1=1.0 / 512.0,
                                          scalar2=1e-6, op0=OP.mult, op1=OP.add)
                        eng.reciprocal(out=srow[:, 1:2], in_=srow[:, 0:1]).then_inc(cv2.h, 1)
                        eng.wait_ge(cv3.h, i + 1)
                        eng.tensor_scalar(out=ctb(i), in0=pb4[qi % 4], scalar1=srow[:, 2:3],
                                          scalar2=None, op0=OP.mult).then_inc(evq.h, 1)

            for g in range(NP):
                for i in range(16):
                    gi = g * 16 + i
                    b = i % 2
                    prep = xin[b][:, 0:768]
                    prep3 = prep.rearrange("p (s f) -> p s f", f=64)
                    eng.wait_ge(lqk[b].h, (gi // 2 + 1) * 16)
                    eng.tensor_tensor(out=U[:, 0:768], in0=prep, in1=prep, op=OP.mult)
                    eng.tensor_reduce(out=rst[:, 0:12],
                                      in_=U[:, 0:768].rearrange("p (s f) -> p s f", f=64),
                                      axis=AX.X, op=OP.add)
                    eng.tensor_scalar(out=rst[:, :], in0=rst[:, :], scalar1=1.0 / 64.0,
                                      scalar2=1e-6, op0=OP.mult, op1=OP.add)
                    eng.reciprocal(out=rst[:, :], in_=rst[:, :]).then_inc(pv1.h, 1)
                    eng.wait_ge(pq2.h, gi + 1)
                    eng.tensor_tensor(out=sc4[:, 0:8].rearrange("p (two h) -> p two h", two=2),
                                      in0=rst[:, 0:8].rearrange("p (two h) -> p two h", two=2),
                                      in1=gq[:, g * 4:(g + 1) * 4].unsqueeze(1).broadcast_to([128, 2, 4]),
                                      op=OP.mult)
                    eng.tensor_copy(out=sc4[:, 8:12], in_=rst[:, 8:12])
                    eng.tensor_tensor(out=prep3, in0=prep3,
                                      in1=sc4[:, 0:12].unsqueeze(2).broadcast_to([128, 12, 64]),
                                      op=OP.mult)
                    cs = cosb[:, i * 32:(i + 1) * 32].unsqueeze(1)
                    sn = sinb[:, i * 32:(i + 1) * 32].unsqueeze(1)
                    x1v = prep.rearrange("p (s f) -> p s f", f=32)[:, 8:24:2, :]
                    x2v = prep.rearrange("p (s f) -> p s f", f=32)[:, 9:24:2, :]
                    u3 = lambda o: U[:, 768 + o * 256: 768 + (o + 1) * 256].rearrange("p (s f) -> p s f", f=32)
                    eng.tensor_tensor(out=u3(0), in0=x1v, in1=cs.broadcast_to([128, 8, 32]), op=OP.mult)
                    eng.tensor_tensor(out=u3(1), in0=x2v, in1=cs.broadcast_to([128, 8, 32]), op=OP.mult)
                    eng.tensor_tensor(out=u3(2), in0=x1v, in1=sn.broadcast_to([128, 8, 32]), op=OP.mult)
                    eng.tensor_tensor(out=u3(3), in0=x2v, in1=sn.broadcast_to([128, 8, 32]), op=OP.mult)
                    eng.tensor_tensor(out=x1v, in0=u3(0), in1=u3(3), op=OP.add)
                    eng.tensor_tensor(out=x2v, in0=u3(1), in1=u3(2), op=OP.subtract)
                    eng.tensor_copy(out=prepb(i), in_=prep).then_inc(pv2.h, 1)

                for i in range(17):
                    if i < 16:
                        ki = g * 32 + i
                        eng.wait_ge(kq1.h, g * 16 + i + 1)
                        eng.tensor_scalar(out=srow[:, 4:8], in0=srow[:, 4:8], scalar1=1.0 / 64.0,
                                          scalar2=1e-6, op0=OP.mult, op1=OP.add)
                        eng.reciprocal(out=srow[:, 4:8], in_=srow[:, 4:8]).then_inc(kv1.h, 1)
                        eng.wait_ge(kq2.h, g * 16 + i + 1)
                        if g * 16 + i >= 2:
                            eng.wait_ge(tspC2.h, (g * 16 + i - 1) * 4)
                        for hl in range(4):
                            a = eng.tensor_scalar(out=knb[i % 2][:, hl * 64:(hl + 1) * 64],
                                                  in0=pb2[ki % 2][:, hl * 64:(hl + 1) * 64],
                                                  scalar1=srow[:, 4 + hl:5 + hl], scalar2=None,
                                                  op0=OP.mult)
                        a.then_inc(kv2.h, 1)

                for i in range(16):
                    ki = g * 32 + 16 + i
                    eng.wait_ge(mmk.h, ki + 1)
                    if g > 0 and i < 2:
                        eng.wait_ge(mpv.h, g * 64)
                    eng.tensor_copy(out=Vsb[:, i * 512:(i + 1) * 512], in_=pb2[ki % 2]).then_inc(vve.h, 1)
                for it, (hl, qt) in enumerate(c4iters):
                    gidx = g * 64 + it
                    nk = (qt + 1) * 128
                    nch = c4nch[it]
                    cb = cbase(g, it)
                    for cch in range(nch):
                        eng.wait_ge(mms.h, cb + cch + 1)
                        lo = cch * 512
                        hi = min(nk, lo + 512)
                        if hi == nk:
                            eng.tensor_tensor(out=ps[:, nk - 128:nk], in0=ps[:, nk - 128:nk],
                                              in1=trim[:, :], op=OP.add)
                        eng.tensor_reduce(out=srow[:, 4 + cch:5 + cch], in_=ps[:, lo:hi],
                                          axis=AX.X, op=OP.max)
                    eng.tensor_reduce(out=srow[:, 8:9], in_=srow[:, 4:4 + nch], axis=AX.X,
                                      op=OP.max, negate=True).then_inc(vmk.h, 1)
                    eng.wait_ge(sxe.h, cb + nch)
                    eng.tensor_reduce(out=srow[:, 9:10], in_=srow[:, 4:4 + nch], axis=AX.X,
                                      op=OP.add)
                    eng.reciprocal(out=srow[:, 10:11], in_=srow[:, 9:10]).then_inc(vrc.h, 1)

                    eng.wait_ge(mpv.h, gidx + 1)
                    eng.tensor_scalar(out=ctxsb(qt)[:, (g * 4 + hl) * 128:(g * 4 + hl + 1) * 128],
                                      in0=pc[:, 0:128], scalar1=srow[:, 10:11], scalar2=None,
                                      op0=OP.mult).then_inc(vcx.h, 1)

            for i in range(16):
                b = i % 2
                eng.wait_ge(lx2[b].h, (8 + i // 2 + 1) * 16)
                if i >= 1:
                    eng.wait_ge(sst.h, i * 16)
                for c in range(4):
                    oi = i * 4 + c
                    eng.wait_ge(mmo.h, oi + 1)
                    eng.tensor_tensor(out=x1b[:, c * 512:(c + 1) * 512], in0=pb4[oi % 4],
                                      in1=xin[b][:, c * 512:(c + 1) * 512], op=OP.add).then_inc(evo.h, 1)
                sl = x1b[:, :]
                eng.tensor_reduce(out=mrow[:, 0:1], in_=sl, axis=AX.X, op=OP.add)
                eng.tensor_scalar(out=mrow[:, 0:1], in0=mrow[:, 0:1], scalar1=1.0 / 2048.0,
                                  scalar2=None, op0=OP.mult)
                eng.tensor_scalar(out=sl, in0=sl, scalar1=mrow[:, 0:1], scalar2=None,
                                  op0=OP.subtract).then_inc(lnx.h, 1)
                eng.wait_ge(lnq.h, i + 1)
                eng.tensor_reduce(out=mrow[:, 1:2], in_=v4[:, :], axis=AX.X, op=OP.add)
                eng.tensor_scalar(out=mrow[:, 1:2], in0=mrow[:, 1:2], scalar1=1.0 / 2048.0,
                                  scalar2=1e-5, op0=OP.mult, op1=OP.add)
                eng.reciprocal(out=mrow[:, 2:3], in_=mrow[:, 1:2]).then_inc(lnr.h, 1)
                eng.wait_ge(lns.h, i + 1)
                eng.tensor_scalar(out=sl, in0=sl, scalar1=mrow[:, 3:4], scalar2=None,
                                  op0=OP.mult).then_inc(x1r.h, 1)

        @block.scalar
        def _(eng):
            eng = D(eng)
            for i in range(16):
                for kc in range(16):
                    gi = i * 16 + kc
                    eng.wait_ge(tspA.h, gi + 1)
                    eng.copy(out=xfeat(kc)[:, i * 128:(i + 1) * 128],
                             in_=tslot[gi % 3]).then_inc(extA.h, 1)
            for i in range(16):
                qi = 96 + i
                eng.wait_ge(mmq.h, qi + 1)
                if i >= 1:
                    eng.wait_ge(cv2.h, i)
                    eng.wait_ge(evq.h, 96 + i)
                eng.activation(out=ssc[:, :], in_=pb4[qi % 4], func=AF.Square,
                               accum_out=srow[:, 0:1]).then_inc(cv1.h, 1)
                eng.wait_ge(cv2.h, i + 1)
                eng.sqrt(out=srow[:, 2:3], in_=srow[:, 1:2]).then_inc(cv3.h, 1)
            for i in range(16):
                for j in range(4):
                    gi = i * 4 + j
                    eng.wait_ge(tspB.h, gi + 1)
                    eng.copy(out=Ckv[:, j * 2048 + i * 128:j * 2048 + (i + 1) * 128],
                             in_=tslot[gi % 3]).then_inc(extB.h, 1)
            for g in range(NP):
                for i in range(17):
                    if i < 16:
                        gi = g * 16 + i
                        eng.wait_ge(pv1.h, gi + 1)
                        eng.sqrt(out=rst[:, :], in_=rst[:, :]).then_inc(pq2.h, 1)
                    if i >= 1:
                        ii = i - 1
                        for bidx in range(12):
                            tc = g * 192 + ii * 12 + bidx
                            eng.wait_ge(tspC1.h, tc + 1)
                            if bidx < 4:
                                dst = qf(bidx)[0:64, ii * 128:(ii + 1) * 128]
                            elif bidx < 8:
                                dst = qf(bidx - 4)[64:128, ii * 128:(ii + 1) * 128]
                            else:
                                dst = kf(bidx - 8)[64:128, ii * 128:(ii + 1) * 128]
                            eng.copy(out=dst, in_=tslot[tc % 3][0:64, :]).then_inc(extC1.h, 1)
                for i in range(17):
                    if i < 16:
                        ki = g * 32 + i
                        eng.wait_ge(mmk.h, ki + 1)
                        if g * 16 + i >= 1:
                            eng.wait_ge(kv2.h, g * 16 + i)
                        for hl in range(4):
                            a = eng.activation(out=ssc[:, 0:64], in_=pb2[ki % 2][:, hl * 64:(hl + 1) * 64],
                                               func=AF.Square, accum_out=srow[:, 4 + hl:5 + hl])
                        a.then_inc(kq1.h, 1)
                        eng.wait_ge(kv1.h, g * 16 + i + 1)
                        eng.sqrt(out=srow[:, 4:8], in_=srow[:, 4:8]).then_inc(kq2.h, 1)
                    if i >= 1:
                        ii = i - 1
                        for hl in range(4):
                            tc = g * 64 + ii * 4 + hl
                            eng.wait_ge(tspC2.h, tc + 1)
                            eng.copy(out=kf(hl)[0:64, ii * 128:(ii + 1) * 128],
                                     in_=tslot[tc % 3][0:64, :]).then_inc(extC2.h, 1)
                for it, (hl, qt) in enumerate(c4iters):
                    gidx = g * 64 + it
                    nk = (qt + 1) * 128
                    nch = c4nch[it]
                    eng.wait_ge(vmk.h, gidx + 1)
                    for cch in range(nch):
                        lo = cch * 512
                        hi = min(nk, lo + 512)
                        eng.activation(out=probs[:, lo:hi], in_=ps[:, lo:hi], func=AF.Exp,
                                       bias=srow[:, 8:9],
                                       accum_out=srow[:, 4 + cch:5 + cch]).then_inc(sxe.h, 1)
                    for kt in range(qt + 1):
                        tc = c4base(g, it) + kt
                        eng.wait_ge(tspC4.h, tc + 1)
                        eng.copy(out=probsT[:, kt * 128:(kt + 1) * 128],
                                 in_=tslot[tc % 3]).then_inc(extC4.h, 1)
            for i in range(16):
                for kc in range(16):
                    gi = i * 16 + kc
                    eng.wait_ge(tspD.h, gi + 1)
                    eng.copy(out=ctxf(kc)[:, i * 128:(i + 1) * 128],
                             in_=tslot[gi % 3]).then_inc(extD.h, 1)
            for i in range(16):
                eng.wait_ge(lnx.h, i + 1)
                for c in range(4):
                    a = eng.activation(out=ssc[:, :], in_=x1b[:, c * 512:(c + 1) * 512],
                                       func=AF.Square, accum_out=v4[:, c:c + 1])
                a.then_inc(lnq.h, 1)
                eng.wait_ge(lnr.h, i + 1)
                eng.sqrt(out=mrow[:, 3:4], in_=mrow[:, 2:3]).then_inc(lns.h, 1)
                eng.wait_ge(x1r.h, i + 1)
                eng.dma_start(out=x1[i * 128:(i + 1) * 128, :], in_=x1b[:, :]).then_inc(sst.h, 16)
            eng.wait_ge(sst.h, 256)



def build_l1a2():
    nc = bass.Bass("TRN2", target_bir_lowering=False, debug=False)
    x2 = nc.dram_tensor("x2", [2048, 2048], F32, kind="ExternalInput")
    wqkv = nc.dram_tensor("wqkv", [2048, 6144], F32, kind="ExternalInput")
    bqkv = nc.dram_tensor("bqkv", [1, 6144], F32, kind="ExternalInput")
    wout = nc.dram_tensor("wout", [2048, 2048], F32, kind="ExternalInput")
    bout = nc.dram_tensor("bout", [1, 2048], F32, kind="ExternalInput")
    cos2 = nc.dram_tensor("cos2", [128, 64], F32, kind="ExternalInput")
    sin2 = nc.dram_tensor("sin2", [128, 64], F32, kind="ExternalInput")
    mask8 = nc.dram_tensor("mask8", [128, 8], F32, kind="ExternalInput")
    ident = nc.dram_tensor("ident", [128, 128], BF16, kind="ExternalInput")
    x3 = nc.dram_tensor("x3", [2048, 2048], F32, kind="ExternalOutput")
    q_scr = nc.dram_tensor("q_scr", [2048, 2048], BF16)
    kvk_in = nc.dram_tensor("kvk_in", [2048, 2048], BF16)
    kvv_in = nc.dram_tensor("kvv_in", [2048, 2048], BF16)
    kvk_all = nc.dram_tensor("kvk_all", [16384, 2048], BF16, addr_space="Shared")
    kvv_all = nc.dram_tensor("kvv_all", [16384, 2048], BF16, addr_space="Shared")
    emit_l1a2(nc, x2, wqkv, bqkv, wout, bout, cos2, sin2, mask8, ident, x3,
              q_scr, kvk_in, kvv_in, kvk_all, kvv_all)
    return nc


def emit_l1a2(nc, x2, wqkv, bqkv, wout, bout, cos2, sin2, mask8, ident, x3,
              q_scr, kvk_in, kvv_in, kvk_all, kvv_all, x2f_scr, x3f_scr):
    with contextlib.ExitStack() as st:
        ebuf = lambda name, shape, dt: st.enter_context(nc.sbuf_tensor(name, shape, dt, allow_name_mangling=True))
        R1 = ebuf("R1", [128, 32768], BF16)
        R2 = ebuf("R2", [128, 32768], BF16)
        xin = [ebuf(f"xin{i}", [128, 2048], F32) for i in range(2)]
        idn = ebuf("idn", [128, 128], BF16)
        ones = ebuf("ones", [1, 128], BF16)
        brow = ebuf("brow", [1, 6144], BF16)
        borow = ebuf("borow", [1, 2048], BF16)
        cosb = ebuf("cosb", [128, 64], F32)
        sinb = ebuf("sinb", [128, 64], F32)
        mskb = ebuf("mskb", [128, 8], F32)
        m1 = ebuf("m1", [128, 512], F32)
        m2 = ebuf("m2", [128, 512], F32)
        qt = [ebuf(f"qt{i}", [128, 512], BF16) for i in range(2)]
        qbuf = [ebuf(f"qbuf{i}", [128, 2048], BF16) for i in range(2)]
        scb = ebuf("scb", [128, 256], F32)
        exb = ebuf("exb", [128, 256], F32)
        pb = ebuf("pb", [128, 256], BF16)
        xfb = ebuf("xfb", [128, 2048], BF16)
        stgE = [ebuf(f"stgE{i}", [128, 2048], BF16) for i in range(2)]
        srow = ebuf("srow", [128, 128], F32)
        x3b = [ebuf(f"x3b{i}", [128, 2048], F32) for i in range(2)]
        v4 = ebuf("v4", [128, 4], F32)
        mrow = ebuf("mrow", [128, 4], F32)

        tps = st.enter_context(nc.psum_tensor("tps", [128, 2048], BF16, allow_name_mangling=True))
        tslot = [tps[:, 0:128], tps[:, 1024:1152]]
        pq = st.enter_context(nc.psum_tensor("pq", [128, 2048], F32, allow_name_mangling=True))
        pqs = [pq[:, i * 512:(i + 1) * 512] for i in range(4)]

        lid = Sem(st, nc, "lid")
        lbq = Sem(st, nc, "lbq")
        lx2 = [Sem(st, nc, "lx2a"), Sem(st, nc, "lx2b")]
        cxb = Sem(st, nc, "cxb")
        tsp = Sem(st, nc, "tsp")
        ext = Sem(st, nc, "ext")
        lwq = [Sem(st, nc, "lwq0"), Sem(st, nc, "lwq1")]
        mmq = Sem(st, nc, "mmq")
        evq = Sem(st, nc, "evq")
        sq2 = [Sem(st, nc, "sq0"), Sem(st, nc, "sq1")]
        ccs = Sem(st, nc, "ccs")
        lqr = Sem(st, nc, "lqr")
        lqs2 = Sem(st, nc, "lqs2")
        sxd = Sem(st, nc, "sxd")
        psc = Sem(st, nc, "psc")
        pcx = Sem(st, nc, "pcx")
        vpb = Sem(st, nc, "vpb")
        sxe = Sem(st, nc, "sxe")
        vcx = Sem(st, nc, "vcx")
        lxf = Sem(st, nc, "lxf")
        cxe = Sem(st, nc, "cxe")
        tspE = Sem(st, nc, "tspE")
        extE = Sem(st, nc, "extE")
        sxf2 = [Sem(st, nc, "sxfa"), Sem(st, nc, "sxfb")]
        lwo = Sem(st, nc, "lwo")
        mmo = Sem(st, nc, "mmo")
        evo = Sem(st, nc, "evo")
        lnx = Sem(st, nc, "lnx")
        lnq = Sem(st, nc, "lnq")
        lnr = Sem(st, nc, "lnr")
        lns = Sem(st, nc, "lns")
        x3r = Sem(st, nc, "x3r")
        sst2 = [Sem(st, nc, "ssta"), Sem(st, nc, "sstb")]
        block = st.enter_context(nc.Block())

        NQ = 192
        corder = list(range(4, 12)) + list(range(4))
        ctxp = x3b[1]
        ptmp = x3b[0]

        wb = lambda cb: R2[:, cb * 16384:(cb + 1) * 16384]
        kvr = lambda t: R2[:, t * 4096:(t + 1) * 4096]
        ctxf = lambda kc: R2[:, kc * 2048:(kc + 1) * 2048]
        x2f = lambda kc: R1[:, kc * 2048:(kc + 1) * 2048]
        ctxbf = lambda tile: R1[:, tile * 2048:(tile + 1) * 2048]
        woutr = lambda kc: R1[:, kc * 2048:(kc + 1) * 2048]

        @block.sync
        def _(eng):
            eng.dma_start(out=idn[:, :], in_=ident[:, :]).then_inc(lid.h, 16)
            eng.dma_start(out=cosb[:, :], in_=cos2[:, :]).then_inc(lid.h, 16)
            eng.dma_start(out=sinb[:, :], in_=sin2[:, :]).then_inc(lid.h, 16)
            eng.dma_start(out=mskb[:, :], in_=mask8[:, :]).then_inc(lid.h, 16)
            for kc in range(16):
                eng.dma_start(out=x2f(kc), in_=x2f_scr.ap()[kc, :, :]).then_inc(lxf.h, 16)
            for qi in range(NQ):
                eng.wait_ge(evq.h, qi + 1)
                c, i = corder[qi // 16], qi % 16
                if c < 4:
                    dst = q_scr[i * 128:(i + 1) * 128, c * 512:(c + 1) * 512]
                elif c < 8:
                    dst = kvk_in[i * 128:(i + 1) * 128, (c - 4) * 512:(c - 3) * 512]
                else:
                    dst = kvv_in[i * 128:(i + 1) * 128, (c - 8) * 512:(c - 7) * 512]
                eng.dma_start(out=dst, in_=qt[qi % 2][:, :]).then_inc(sq2[qi % 2].h, 16)
            for i in range(16):
                if i == 0:
                    eng.wait_ge(ccs.h, 1)
                else:
                    eng.wait_ge(sxd.h, i)
                    eng.wait_ge(psc.h, i)
                for t in range(8):
                    eng.dma_start(out=kvr(t)[:, 0:2048],
                                  in_=kvk_all[t * 2048 + i * 128:t * 2048 + (i + 1) * 128, :]
                                  ).then_inc(lqr.h, 16)
                if i == 0:
                    eng.wait_ge(ccs.h, 2)
                else:
                    eng.wait_ge(vcx.h, i)
                for t in range(8):
                    eng.dma_start(out=kvr(t)[:, 2048:4096],
                                  in_=kvv_all[t * 2048 + i * 128:t * 2048 + (i + 1) * 128, :]
                                  ).then_inc(lqr.h, 16)
                if i == 0:
                    eng.wait_ge(sq2[0].h, (NQ // 2) * 16)
                    eng.wait_ge(sq2[1].h, (NQ // 2) * 16)
                if i >= 2:
                    eng.wait_ge(vcx.h, i - 1)
                eng.dma_start(out=qbuf[i % 2][:, :], in_=q_scr[i * 128:(i + 1) * 128, :]).then_inc(lqs2.h, 16)
            eng.wait_ge(vcx.h, 16)
            for i in range(16):
                b = i % 2
                if i >= 2:
                    eng.wait_ge(evo.h, (i - 2) * 4 + 4)
                eng.dma_start(out=xin[b][:, :], in_=x2[i * 128:(i + 1) * 128, :]).then_inc(lx2[b].h, 16)
                if i >= 1:
                    idx = i - 1
                    eng.wait_ge(extE.h, (idx + 1) * 16)
                    eng.dma_start(
                        out=x3f_scr.ap()[:, :, idx * 128:(idx + 1) * 128].rearrange("c p f -> p c f"),
                        in_=stgE[idx % 2][:, :].rearrange("p (c f) -> p c f", f=128),
                    ).then_inc(sxf2[idx % 2].h, 16)
            eng.wait_ge(extE.h, 256)
            eng.dma_start(
                out=x3f_scr.ap()[:, :, 15 * 128:16 * 128].rearrange("c p f -> p c f"),
                in_=stgE[1][:, :].rearrange("p (c f) -> p c f", f=128),
            ).then_inc(sxf2[1].h, 16)
            eng.wait_ge(sxf2[0].h, 128)
            eng.wait_ge(sxf2[1].h, 128)

        @block.gpsimd
        def _(eng):
            eng.dma_start(out=brow[:, :], in_=bqkv[:, :]).then_inc(lbq.h, 16)
            eng.dma_start(out=borow[:, :], in_=bout[:, :]).then_inc(lbq.h, 16)
            for ci in range(12):
                c = corder[ci]
                cb = ci % 2
                if ci >= 2:
                    eng.wait_ge(mmq.h, (ci - 1) * 16)
                for kc in range(16):
                    eng.dma_start(out=wb(cb)[:, kc * 512:(kc + 1) * 512],
                                  in_=wqkv[kc * 128:(kc + 1) * 128, c * 512:(c + 1) * 512]
                                  ).then_inc(lwq[cb].h, 16)
                if ci == 3:
                    eng.wait_ge(sq2[0].h, 32 * 16)
                    eng.wait_ge(sq2[1].h, 32 * 16)
                    eng.collective_compute(
                        "AllGather", OP.bypass,
                        replica_groups=[list(range(NCORES))],
                        ins=[kvk_in.ap().opt()],
                        outs=[kvk_all.ap().opt()],
                    ).then_inc(ccs.h, 1)
                if ci == 7:
                    eng.wait_ge(sq2[0].h, 64 * 16)
                    eng.wait_ge(sq2[1].h, 64 * 16)
                    eng.collective_compute(
                        "AllGather", OP.bypass,
                        replica_groups=[list(range(NCORES))],
                        ins=[kvv_in.ap().opt()],
                        outs=[kvv_all.ap().opt()],
                    ).then_inc(ccs.h, 1)
            for i in range(16):
                eng.wait_ge(lqr.h, i * 256 + 128)
                eng.wait_ge(lqs2.h, (i + 1) * 16)
                if i >= 1:
                    eng.wait_ge(sxd.h, i)
                q_i = qbuf[i % 2][:, :].rearrange("p (h f) -> p h f", f=64)
                p3 = ptmp[:, :].rearrange("p (h f) -> p h f", f=64)
                for t in (6, 7):
                    k_t = kvr(t)[:, 0:2048].rearrange("p (h f) -> p h f", f=64)
                    eng.tensor_tensor(out=p3, in0=q_i, in1=k_t, op=OP.mult)
                    w = 32
                    while w >= 1:
                        eng.drain()
                        eng.tensor_tensor(out=p3[:, :, 0:w], in0=p3[:, :, 0:w],
                                          in1=p3[:, :, w:2 * w], op=OP.add)
                        w //= 2
                    eng.drain()
                    eng.tensor_copy(out=scb[:, :].rearrange("p (h t) -> p h t", t=8)[:, :, t:t + 1],
                                    in_=p3[:, :, 0:1])
                    eng.drain()
                eng.engine_nop().then_inc(psc.h, 1)
                eng.wait_ge(lqr.h, (i + 1) * 256)
                eng.wait_ge(vpb.h, i + 1)
                pb3p = pb[:, :].rearrange("p (h t) -> p h t", t=8)
                first = True
                for t in (6, 7):
                    v_t = kvr(t)[:, 2048:4096].rearrange("p (h f) -> p h f", f=64)
                    pr = pb3p[:, :, t].unsqueeze(2).broadcast_to([128, 32, 64])
                    if first:
                        eng.tensor_tensor(out=ctxp[:, :].rearrange("p (h f) -> p h f", f=64),
                                          in0=v_t, in1=pr, op=OP.mult)
                        first = False
                    else:
                        eng.tensor_tensor(out=ptmp[:, :].rearrange("p (h f) -> p h f", f=64),
                                          in0=v_t, in1=pr, op=OP.mult)
                        eng.drain()
                        eng.tensor_tensor(out=ctxp[:, :], in0=ctxp[:, :], in1=ptmp[:, :], op=OP.add)
                    eng.drain()
                eng.engine_nop().then_inc(pcx.h, 1)
            eng.wait_ge(tsp.h, 256)
            for kc in range(16):
                eng.dma_start(out=woutr(kc), in_=wout[kc * 128:(kc + 1) * 128, :]).then_inc(lwo.h, 16)

        @block.tensor
        def _(eng):
            eng.wait_ge(lid.h, 64)
            eng.wait_ge(lbq.h, 32)
            eng.wait_ge(lxf.h, 256)
            for qi in range(NQ):
                ci = qi // 16
                c, i = corder[ci], qi % 16
                cb = ci % 2
                if i == 0:
                    eng.wait_ge(lwq[cb].h, (ci // 2 + 1) * 256)
                if qi >= 4:
                    eng.wait_ge(evq.h, qi - 3)
                for kc in range(16):
                    eng.matmul(out=pqs[qi % 4], lhsT=x2f(kc)[:, i * 128:(i + 1) * 128],
                               rhs=wb(cb)[:, kc * 512:(kc + 1) * 512],
                               start=(kc == 0), stop=False)
                eng.matmul(out=pqs[qi % 4], lhsT=ones[0:1, :], rhs=brow[0:1, c * 512:(c + 1) * 512],
                           start=False, stop=True).then_inc(mmq.h, 1)
            eng.wait_ge(vcx.h, 16)
            for tile in range(16):
                for kc in range(16):
                    gi = tile * 16 + kc
                    if gi >= 2:
                        eng.wait_ge(ext.h, gi - 1)
                    eng.transpose(out=tslot[gi % 2], in_=ctxbf(tile)[:, kc * 128:(kc + 1) * 128],
                                  identity=idn[:, :]).then_inc(tsp.h, 1)
            eng.wait_ge(lwo.h, 256)
            eng.wait_ge(ext.h, 256)
            for i in range(16):
                for c in range(4):
                    oi = i * 4 + c
                    if oi >= 4:
                        eng.wait_ge(evo.h, oi - 3)
                    for kc in range(16):
                        eng.matmul(out=pqs[oi % 4], lhsT=ctxf(kc)[:, i * 128:(i + 1) * 128],
                                   rhs=woutr(kc)[:, c * 512:(c + 1) * 512],
                                   start=(kc == 0), stop=False)
                    eng.matmul(out=pqs[oi % 4], lhsT=ones[0:1, :], rhs=borow[0:1, c * 512:(c + 1) * 512],
                               start=False, stop=True).then_inc(mmo.h, 1)
                eng.wait_ge(cxe.h, i + 1)
                for kc in range(16):
                    gE = i * 16 + kc
                    if i == 0 and kc < 2:
                        eng.wait_ge(ext.h, 256)
                    if gE >= 2:
                        eng.wait_ge(extE.h, gE - 1)
                    eng.transpose(out=tslot[gE % 2], in_=xfb[:, kc * 128:(kc + 1) * 128],
                                  identity=idn[:, :]).then_inc(tspE.h, 1)

        @block.vector
        def _(eng):
            eng = D(eng)
            eng.memset(ones[:, :], 1.0)
            for qi in range(NQ):
                c, i = corder[qi // 16], qi % 16
                eng.wait_ge(mmq.h, qi + 1)
                if qi >= 2:
                    eng.wait_ge(sq2[qi % 2].h, (qi // 2) * 16)
                ps = pqs[qi % 4]
                if c < 8:
                    cs = cosb[:, :].unsqueeze(1)
                    sn = sinb[:, :].unsqueeze(1)
                    eng.tensor_tensor(out=m1[:, :].rearrange("p (h f) -> p h f", f=64),
                                      in0=ps.rearrange("p (h f) -> p h f", f=64),
                                      in1=cs.broadcast_to([128, 8, 64]), op=OP.mult)
                    eng.tensor_tensor(out=m2[:, :].rearrange("p (h f) -> p h f", f=64),
                                      in0=ps.rearrange("p (h f) -> p h f", f=64),
                                      in1=sn.broadcast_to([128, 8, 64]), op=OP.mult)
                    qv = qt[qi % 2][:, :].rearrange("p (h two f) -> p h two f", two=2, f=32)
                    eng.tensor_tensor(out=qv[:, :, 0, :],
                                      in0=m1[:, :].rearrange("p (h two f) -> p h two f", two=2, f=32)[:, :, 0, :],
                                      in1=m2[:, :].rearrange("p (h two f) -> p h two f", two=2, f=32)[:, :, 1, :],
                                      op=OP.subtract)
                    eng.tensor_tensor(out=qv[:, :, 1, :],
                                      in0=m1[:, :].rearrange("p (h two f) -> p h two f", two=2, f=32)[:, :, 1, :],
                                      in1=m2[:, :].rearrange("p (h two f) -> p h two f", two=2, f=32)[:, :, 0, :],
                                      op=OP.add).then_inc(evq.h, 1)
                else:
                    eng.tensor_copy(out=qt[qi % 2][:, :], in_=ps).then_inc(evq.h, 1)
            ctxs = xin[0]
            ctmp = xin[1]
            for i in range(16):
                eng.wait_ge(lqr.h, i * 256 + 128)
                eng.wait_ge(lqs2.h, (i + 1) * 16)
                q_i = qbuf[i % 2][:, :].rearrange("p (h f) -> p h f", f=64)
                for t in range(6):
                    k_t = kvr(t)[:, 0:2048].rearrange("p (h f) -> p h f", f=64)
                    eng.tensor_tensor(out=ctmp[:, :].rearrange("p (h f) -> p h f", f=64),
                                      in0=q_i, in1=k_t, op=OP.mult)
                    eng.tensor_reduce(
                        out=scb[:, :].rearrange("p (h t) -> p h t", t=8)[:, :, t],
                        in_=ctmp[:, :].rearrange("p (h f) -> p h f", f=64),
                        axis=AX.X, op=OP.add)
                eng.wait_ge(psc.h, i + 1)
                sc3 = scb[:, :].rearrange("p (h t) -> p h t", t=8)
                eng.tensor_tensor(out=sc3, in0=sc3,
                                  in1=mskb[:, :].unsqueeze(1).broadcast_to([128, 32, 8]),
                                  op=OP.add)
                eng.tensor_reduce(out=srow[:, 0:32], in_=sc3, axis=AX.X, op=OP.max)
                eng.tensor_tensor(out=exb[:, :].rearrange("p (h t) -> p h t", t=8),
                                  in0=sc3,
                                  in1=srow[:, 0:32].unsqueeze(2).broadcast_to([128, 32, 8]),
                                  op=OP.subtract).then_inc(sxd.h, 1)
                eng.wait_ge(sxe.h, i + 1)
                ex3 = exb[:, :].rearrange("p (h t) -> p h t", t=8)
                eng.tensor_reduce(out=srow[:, 32:64], in_=ex3, axis=AX.X, op=OP.add)
                eng.reciprocal(out=srow[:, 64:96], in_=srow[:, 32:64])
                eng.tensor_tensor(out=pb[:, :].rearrange("p (h t) -> p h t", t=8),
                                  in0=ex3,
                                  in1=srow[:, 64:96].unsqueeze(2).broadcast_to([128, 32, 8]),
                                  op=OP.mult)
                eng.engine_nop().then_inc(vpb.h, 1)
                eng.wait_ge(lqr.h, (i + 1) * 256)
                pb3 = pb[:, :].rearrange("p (h t) -> p h t", t=8)
                for t in range(6):
                    v_t = kvr(t)[:, 2048:4096].rearrange("p (h f) -> p h f", f=64)
                    pr = pb3[:, :, t].unsqueeze(2).broadcast_to([128, 32, 64])
                    if t == 0:
                        eng.tensor_tensor(out=ctxs[:, :].rearrange("p (h f) -> p h f", f=64),
                                          in0=v_t, in1=pr, op=OP.mult)
                    else:
                        eng.tensor_tensor(out=ctmp[:, :].rearrange("p (h f) -> p h f", f=64),
                                          in0=v_t, in1=pr, op=OP.mult)
                        eng.tensor_tensor(out=ctxs[:, :], in0=ctxs[:, :], in1=ctmp[:, :], op=OP.add)
                eng.wait_ge(pcx.h, i + 1)
                eng.tensor_tensor(out=ctxs[:, :], in0=ctxs[:, :], in1=ctxp[:, :], op=OP.add)
                eng.tensor_copy(out=ctxbf(i)[:, :], in_=ctxs[:, :]).then_inc(vcx.h, 1)
            for tile in range(16):
                for kc in range(16):
                    gi = tile * 16 + kc
                    eng.wait_ge(tsp.h, gi + 1)
                    eng.tensor_copy(out=ctxf(kc)[:, tile * 128:(tile + 1) * 128],
                                    in_=tslot[gi % 2]).then_inc(ext.h, 1)
            for i in range(16):
                bi = i % 2
                eng.wait_ge(lx2[bi].h, (i // 2 + 1) * 16)
                if i >= 2:
                    eng.wait_ge(sst2[bi].h, (i // 2) * 16)
                for c in range(4):
                    oi = i * 4 + c
                    eng.wait_ge(mmo.h, oi + 1)
                    eng.tensor_tensor(out=x3b[bi][:, c * 512:(c + 1) * 512], in0=pqs[oi % 4],
                                      in1=xin[bi][:, c * 512:(c + 1) * 512], op=OP.add).then_inc(evo.h, 1)
                sl = x3b[bi][:, :]
                eng.tensor_reduce(out=mrow[:, 0:1], in_=sl, axis=AX.X, op=OP.add)
                eng.tensor_scalar(out=mrow[:, 0:1], in0=mrow[:, 0:1], scalar1=1.0 / 2048.0,
                                  scalar2=None, op0=OP.mult)
                eng.tensor_scalar(out=sl, in0=sl, scalar1=mrow[:, 0:1], scalar2=None,
                                  op0=OP.subtract).then_inc(lnx.h, 1)
                eng.wait_ge(lnq.h, i + 1)
                eng.tensor_reduce(out=mrow[:, 1:2], in_=v4[:, :], axis=AX.X, op=OP.add)
                eng.tensor_scalar(out=mrow[:, 1:2], in0=mrow[:, 1:2], scalar1=1.0 / 2048.0,
                                  scalar2=1e-5, op0=OP.mult, op1=OP.add)
                eng.reciprocal(out=mrow[:, 2:3], in_=mrow[:, 1:2]).then_inc(lnr.h, 1)
                eng.wait_ge(lns.h, i + 1)
                eng.tensor_scalar(out=sl, in0=sl, scalar1=mrow[:, 3:4], scalar2=None,
                                  op0=OP.mult).then_inc(x3r.h, 1)
                if i >= 1:
                    eng.wait_ge(tspE.h, i * 16)
                eng.tensor_copy(out=xfb[:, :], in_=sl).then_inc(cxe.h, 1)

        @block.scalar
        def _(eng):
            eng = D(eng)
            for i in range(16):
                eng.wait_ge(sxd.h, i + 1)
                ex3 = exb[:, :].rearrange("p (h t) -> p h t", t=8)
                eng.activation(out=ex3, in_=ex3, func=AF.Exp, scale=0.125).then_inc(sxe.h, 1)
            for i in range(16):
                bi = i % 2
                eng.wait_ge(lnx.h, i + 1)
                sl = x3b[bi][:, :]
                for c in range(4):
                    a = eng.activation(out=m1[:, :], in_=sl[:, c * 512:(c + 1) * 512],
                                       func=AF.Square, accum_out=v4[:, c:c + 1])
                a.then_inc(lnq.h, 1)
                eng.wait_ge(lnr.h, i + 1)
                eng.sqrt(out=mrow[:, 3:4], in_=mrow[:, 2:3]).then_inc(lns.h, 1)
                eng.wait_ge(x3r.h, i + 1)
                eng.dma_start(out=x3[i * 128:(i + 1) * 128, :], in_=x3b[bi][:, :]).then_inc(sst2[i % 2].h, 16)
                sb = i % 2
                if i >= 2:
                    eng.wait_ge(sxf2[i % 2].h, (i // 2) * 16)
                for kc in range(16):
                    gE = i * 16 + kc
                    eng.wait_ge(tspE.h, gE + 1)
                    eng.copy(out=stgE[sb][:, kc * 128:(kc + 1) * 128],
                             in_=tslot[gE % 2]).then_inc(extE.h, 1)
            eng.wait_ge(sst2[0].h, 128)
            eng.wait_ge(sst2[1].h, 128)



def _wq_perm():
    perm = []
    for g in range(4):
        for h in range(4 * g, 4 * g + 4):
            perm.extend(range(h * 128, h * 128 + 64))
        for h in range(4 * g, 4 * g + 4):
            perm.extend(range(h * 128 + 64, h * 128 + 128))
        for h in range(4 * g, 4 * g + 4):
            perm.extend(range(2048 + h * 64, 2048 + (h + 1) * 64))
    return np.array(perm)


_WQPERM = _wq_perm()


def _l0a_consts():
    inv = 10000.0 ** (-np.arange(0, 64, 2) / 64)
    fr = np.outer(np.arange(2048), inv)
    cos1 = np.cos(fr).astype(np.float32)
    sin1 = np.sin(fr).astype(np.float32)
    trimask = np.triu(np.full((128, 128), NEG_BIG, np.float32), 1)
    return cos1, sin1, trimask


def _l1a_consts():
    D = 64
    inv = 1.0 / (10000.0 ** (np.arange(0, D, 2) / D))
    s_of_tok = np.arange(2048) // 256
    fr = s_of_tok[:, None] * inv[None, :]
    emb = np.concatenate([fr, fr], -1)
    return np.cos(emb).astype(np.float32), np.sin(emb).astype(np.float32)


def kernel(**inputs):
    C = np.ascontiguousarray
    cos1, sin1, trimask = _l0a_consts()
    cos2, sin2 = _l1a_consts()
    qg = C(inputs["L0_attn_q_gain"].reshape(1, 16))

    nc = _get_nc("l0a", build_l0a)
    wq_p = C(inputs["L0_attn_w_q_krope"][:, _WQPERM])
    ims = [{"x": C(inputs["L0_attn_x"][c]), "wq": wq_p,
            "wkvd": inputs["L0_attn_w_kv_down"], "wkvu": inputs["L0_attn_w_kv_up"],
            "wproj": inputs["L0_attn_w_proj"], "qgain": qg, "cos1": cos1, "sin1": sin1,
            "trimask": trimask, "ident": _IDENT_BF16} for c in range(NCORES)]
    res = run_bass_kernel_spmd(nc, ims, core_ids=list(range(NCORES)))
    x1 = [res.results[c]["x1"] for c in range(NCORES)]

    nc = _get_nc("l0f", build_l0f)
    ims = [{"x1": x1[c], "gup": inputs["L0_ffn_gate_up_proj"],
            "dwn": inputs["L0_ffn_down_proj"], "ident": _IDENT_BF16} for c in range(NCORES)]
    res = run_bass_kernel_spmd(nc, ims, core_ids=list(range(NCORES)))
    x2 = np.stack([res.results[c]["x2"] for c in range(NCORES)], 0)

    nc = _get_nc("l1a", build_l1a)
    ims = [{"x2": C(x2[:, 256 * c:256 * (c + 1), :].reshape(2048, 2048)),
            "wqkv": inputs["L1_attn_qkv_w"], "bqkv": C(inputs["L1_attn_qkv_b"].reshape(1, 6144)),
            "wout": inputs["L1_attn_out_w"], "bout": C(inputs["L1_attn_out_b"].reshape(1, 2048)),
            "cos2": cos2, "sin2": sin2, "ident": _IDENT_BF16} for c in range(NCORES)]
    res = run_bass_kernel_spmd(nc, ims, core_ids=list(range(NCORES)))
    x3 = [res.results[c]["x3"] for c in range(NCORES)]

    wgT, wuT, wdT = _decode_weights(inputs)

    nc = _get_nc("l1f", build_l1f)
    ims = [{"x3": x3[c], "wg3": wgT, "wu3": wuT, "wd": wdT, "ident": _IDENT_BF16}
           for c in range(NCORES)]
    res = run_bass_kernel_spmd(nc, ims, core_ids=list(range(NCORES)))

    out = np.empty((8, 2048, 2048), np.float32)
    for c in range(NCORES):
        out[:, 256 * c:256 * (c + 1), :] = res.results[c]["x4"].reshape(8, 256, 2048)
    return out




def _clear_sems_block(nc, persist=()):
    bs = set(nc.barrier_sems) | {getattr(s, "num", s) for s in persist}
    rng = bass.get_kernel_semaphore_range()
    runs = []
    start = None
    for s in rng:
        if s in bs:
            if start is not None:
                runs.append(range(start, s))
                start = None
        elif start is None:
            start = s
    if start is not None:
        runs.append(range(start, rng.stop))
    with nc.Block() as b:
        @b.gpsimd
        def _(eng):
            for r in runs:
                eng.sem_clear(r)


def build_fused():
    nc = bass.Bass("TRN2", target_bir_lowering=False, debug=False)
    T = nc.dram_tensor
    x = T("x", [2048, 2048], F32, kind="ExternalInput")
    wq = T("wq", [2048, 3072], F32, kind="ExternalInput")
    wkvd = T("wkvd", [2048, 512], F32, kind="ExternalInput")
    wkvu = T("wkvu", [512, 3072], F32, kind="ExternalInput")
    wproj = T("wproj", [2048, 2048], F32, kind="ExternalInput")
    qgain = T("qgain", [1, 16], F32, kind="ExternalInput")
    cos1 = T("cos1", [2048, 32], F32, kind="ExternalInput")
    sin1 = T("sin1", [2048, 32], F32, kind="ExternalInput")
    trimask = T("trimask", [128, 128], F32, kind="ExternalInput")
    ident = T("ident", [128, 128], BF16, kind="ExternalInput")
    gup = T("gup", [4, 2048, 4096], F32, kind="ExternalInput")
    dwn = T("dwn", [4, 2048, 2048], F32, kind="ExternalInput")
    wqkv = T("wqkv", [2048, 6144], F32, kind="ExternalInput")
    bqkv = T("bqkv", [1, 6144], F32, kind="ExternalInput")
    wout = T("wout", [2048, 2048], F32, kind="ExternalInput")
    bout = T("bout", [1, 2048], F32, kind="ExternalInput")
    cos2 = T("cos2", [128, 64], F32, kind="ExternalInput")
    sin2 = T("sin2", [128, 64], F32, kind="ExternalInput")
    mask8 = T("mask8", [128, 8], F32, kind="ExternalInput")
    dins = {}
    for nm in ("gp", "up", "dn"):
        dins[nm + "_pk"] = T(nm + "_pk", [2048, 1024], U8, kind="ExternalInput")
        dins[nm + "_sc"] = T(nm + "_sc", [2048, 64], F32, kind="ExternalInput")
    x4 = T("x4", [2048, 2048], F32, kind="ExternalOutput")

    qk_scr = T("qk_scr", [2048, 3072], F32)
    x1_scr = T("x1_scr", [2048, 2048], F32)
    x2_scr = T("x2_scr", [2048, 2048], F32)
    x3_scr = T("x3_scr", [2048, 2048], F32)
    x2f_scr = T("x2f_scr", [16, 128, 2048], BF16)
    x3f_scr = T("x3f_scr", [16, 128, 2048], BF16)
    q_scr = T("q_scr", [2048, 2048], BF16)
    kvk_in = T("kvk_in", [2048, 2048], BF16)
    kvv_in = T("kvv_in", [2048, 2048], BF16)
    kvk_all = T("kvk_all", [16384, 2048], BF16, addr_space="Shared")
    kvv_all = T("kvv_all", [16384, 2048], BF16, addr_space="Shared")
    wg_loc = T("wg_loc", [16, 2048, 128], BF16)
    wu_loc = T("wu_loc", [16, 2048, 128], BF16)
    wd_loc = T("wd_loc", [2048, 2048], BF16)
    wg_all = T("wg_all", [128, 2048, 128], BF16, addr_space="Shared")
    wu_all = T("wu_all", [128, 2048, 128], BF16, addr_space="Shared")
    wd_all = T("wd_all", [16384, 2048], BF16, addr_space="Shared")

    pccs_h = nc.alloc_semaphore(name="pccs")
    flat = lambda t: t.ap().rearrange("a b f -> (a b) f")
    emit_decode(nc, dins, ident, {"gp": wg_loc, "up": wu_loc, "dn": wd_loc},
                collect=(pccs_h, [(flat(wg_loc).opt(), flat(wg_all).opt()),
                                  (flat(wu_loc).opt(), flat(wu_all).opt()),
                                  (wd_loc.ap().opt(), wd_all.ap().opt())]))
    _clear_sems_block(nc, persist=(pccs_h,))
    emit_l0a(nc, x, wq, wkvd, wkvu, wproj, qgain, cos1, sin1, trimask, ident, x1_scr, qk_scr)
    _clear_sems_block(nc, persist=(pccs_h,))
    emit_l0f(nc, x1_scr, gup, dwn, ident, x2_scr, x2f_scr=x2f_scr)
    _clear_sems_block(nc, persist=(pccs_h,))
    emit_l1a2(nc, x2_scr, wqkv, bqkv, wout, bout, cos2, sin2, mask8, ident, x3_scr,
              q_scr, kvk_in, kvv_in, kvk_all, kvv_all, x2f_scr, x3f_scr)
    _clear_sems_block(nc, persist=(pccs_h,))
    emit_l1f(nc, x3_scr, wg_all, wu_all, wd_all, ident, x4, pccs=pccs_h, x3f_scr=x3f_scr)
    return nc


def kernel(**inputs):
    C = np.ascontiguousarray
    cos1, sin1, trimask = _l0a_consts()
    qg = C(inputs["L0_attn_q_gain"].reshape(1, 16))
    gp = _u8_view_int32_bytes(inputs["L1_ffn_gate_W_packed"])
    up = _u8_view_int32_bytes(inputs["L1_ffn_up_W_packed"])
    dn = _u8_view_int32_bytes(inputs["L1_ffn_down_W_packed"])
    inv = 1.0 / (10000.0 ** (np.arange(0, 64, 2) / 64))
    wq_p = C(inputs["L0_attn_w_q_krope"][:, _WQPERM])

    nc = _get_nc("fused", build_fused)
    ims = []
    for b in range(NCORES):
        fr = b * inv
        emb = np.concatenate([fr, fr], -1)
        cosb = np.broadcast_to(np.cos(emb).astype(np.float32), (128, 64)).copy()
        sinb = np.broadcast_to(np.sin(emb).astype(np.float32), (128, 64)).copy()
        msk = np.where(np.arange(8) <= b, 0.0, NEG_BIG).astype(np.float32)
        mask8 = np.broadcast_to(msk, (128, 8)).copy()
        r = slice(2048 * b, 2048 * (b + 1))
        ims.append({
            "x": C(inputs["L0_attn_x"][b]),
            "wq": wq_p, "wkvd": inputs["L0_attn_w_kv_down"],
            "wkvu": inputs["L0_attn_w_kv_up"], "wproj": inputs["L0_attn_w_proj"],
            "qgain": qg, "cos1": cos1, "sin1": sin1, "trimask": trimask,
            "ident": _IDENT_BF16,
            "gup": inputs["L0_ffn_gate_up_proj"], "dwn": inputs["L0_ffn_down_proj"],
            "wqkv": inputs["L1_attn_qkv_w"], "bqkv": C(inputs["L1_attn_qkv_b"].reshape(1, 6144)),
            "wout": inputs["L1_attn_out_w"], "bout": C(inputs["L1_attn_out_b"].reshape(1, 2048)),
            "cos2": cosb, "sin2": sinb, "mask8": mask8,
            "gp_pk": gp[r], "gp_sc": C(inputs["L1_ffn_gate_scales"][r]),
            "up_pk": up[r], "up_sc": C(inputs["L1_ffn_up_scales"][r]),
            "dn_pk": C(dn[:, 1024 * b:1024 * (b + 1)]),
            "dn_sc": C(inputs["L1_ffn_down_scales"][:, 64 * b:64 * (b + 1)]),
        })
    res = run_bass_kernel_spmd(nc, ims, core_ids=list(range(NCORES)))
    return np.stack([res.results[b]["x4"] for b in range(NCORES)], 0)


# version 16
# speedup vs baseline: 1.0104x; 1.0104x over previous
import sys
sys.path.insert(0, "/opt/trn_rl_repo")
import contextlib
import numpy as np
import ml_dtypes

import concourse.bass as bass
import concourse.mybir as mybir
from concourse.bass_utils import run_bass_kernel_spmd
from concourse.alu_op_type import AluOpType as OP

F32 = mybir.dt.float32
BF16 = mybir.dt.bfloat16
U8 = mybir.dt.uint8
AF = mybir.ActivationFunctionType
AX = mybir.AxisListType

NCORES = 8
NEG_BIG = -3.0e38



_NO_DRAIN = {"wait_ge", "drain", "nop", "dma_start", "dma_start_transpose", "sem_inc",
             "then_inc", "wait_op", "collective_compute", "engine_nop", "sem_clear"}


def _ap_box(v):
    try:
        name = v.tensor.name
    except Exception:
        return None
    try:
        W = v.tensor.shape[-1]
        off = v.offset
        pext = 0
        fext = 0
        for step, cnt in list(v.ap):
            if step < 0:
                return (name, 0, 1 << 60, 0, 1 << 60)
            s = step * (cnt - 1)
            if step != 0 and step % W == 0:
                pext += s // W
            else:
                fext += s
        p0, f0 = off // W, off % W
        if f0 + fext >= W:
            return (name, p0, p0 + pext + (f0 + fext) // W, 0, W - 1)
        return (name, p0, p0 + pext, f0, f0 + fext)
    except Exception:
        return (name, 0, 1 << 60, 0, 1 << 60)


def _boxes_overlap(a, b):
    return (a[0] == b[0] and a[1] <= b[2] and b[1] <= a[2]
            and a[3] <= b[4] and b[3] <= a[4])


class D:

    def __init__(self, eng):
        self._e = eng
        self._r = []
        self._w = []

    def __getattr__(self, name):
        f = getattr(self._e, name)
        if name.startswith("_") or name in _NO_DRAIN or not callable(f):
            return f

        def g(*a, **k):
            writes = []
            reads = []
            if name in ("memset", "memzero"):
                b = _ap_box(a[0] if a else k.get("ap"))
                if b:
                    writes.append(b)
            else:
                for kk, v in k.items():
                    b = _ap_box(v)
                    if b is None:
                        continue
                    (writes if kk in ("out", "accum_out", "out_max", "out_indices") else reads).append(b)
                for v in a:
                    b = _ap_box(v)
                    if b is not None:
                        reads.append(b)
            conflict = (
                any(_boxes_overlap(r, w) for r in reads for w in self._w)
                or any(_boxes_overlap(w, pw) for w in writes for pw in self._w)
                or any(_boxes_overlap(w, pr) for w in writes for pr in self._r)
            )
            if conflict:
                self._e.drain()
                self._r = []
                self._w = []
            self._r.extend(reads)
            self._w.extend(writes)
            return f(*a, **k)

        return g



def build_decode():
    nc = bass.Bass("TRN2", target_bir_lowering=False, debug=False)
    ins = {}
    for nm in ("gp", "up", "dn"):
        ins[nm + "_pk"] = nc.dram_tensor(nm + "_pk", [2048, 1024], U8, kind="ExternalInput")
        ins[nm + "_sc"] = nc.dram_tensor(nm + "_sc", [2048, 64], F32, kind="ExternalInput")
    ident = nc.dram_tensor("ident", [128, 128], BF16, kind="ExternalInput")
    outs = {
        "gp": nc.dram_tensor("wgT", [16, 2048, 128], BF16, kind="ExternalOutput"),
        "up": nc.dram_tensor("wuT", [16, 2048, 128], BF16, kind="ExternalOutput"),
        "dn": nc.dram_tensor("wdT", [2048, 2048], BF16, kind="ExternalOutput"),
    }

    emit_decode(nc, ins, ident, outs)
    return nc


def emit_decode(nc, ins, ident, outs, collect=None):
    with contextlib.ExitStack() as st:
        ebuf = lambda name, shape, dt: st.enter_context(nc.sbuf_tensor(name, shape, dt, allow_name_mangling=True))
        pk = [ebuf(f"pk{i}", [128, 1024], U8) for i in range(2)]
        sc = [ebuf(f"sc{i}", [128, 64], F32) for i in range(2)]
        nib = ebuf("nib", [128, 2048], U8)
        cf = ebuf("cf", [128, 2048], F32)
        s8 = ebuf("s8", [128, 2048], F32)
        mm_ = ebuf("mm", [128, 2048], F32)
        sgn = ebuf("sgn", [128, 2048], F32)
        r4 = ebuf("r4", [128, 2048], F32)
        r62 = ebuf("r62", [128, 2048], F32)
        dec = [ebuf(f"dec{i}", [128, 2048], BF16) for i in range(2)]
        idn = ebuf("idn", [128, 128], BF16)
        cb4 = ebuf("cb4", [128, 2], F32)
        stg = [ebuf(f"stg{i}", [128, 2048], BF16) for i in range(2)]
        tp = st.enter_context(nc.psum_tensor("tp", [128, 2048], BF16, allow_name_mangling=True))
        tslot = [tp[:, 0:128], tp[:, 1024:1152]]

        ld2 = [Sem(st, nc, "ld0"), Sem(st, nc, "ld1")]
        lid = Sem(st, nc, "lid")
        cmp = Sem(st, nc, "cmp")
        rel = Sem(st, nc, "rel")
        tsp = Sem(st, nc, "tsp")
        ev = Sem(st, nc, "ev")
        so2 = [Sem(st, nc, "so0"), Sem(st, nc, "so1")]
        block = st.enter_context(nc.Block())

        tiles = [(nm, t) for nm in ("gp", "up", "dn") for t in range(16)]
        NT = len(tiles)

        @block.sync
        def _(eng):
            eng.dma_start(out=idn[:, :], in_=ident[:, :]).then_inc(lid.h, 16)
            for i, (nm, t) in enumerate(tiles):
                b = i % 2
                if i >= 2:
                    eng.wait_ge(cmp.h, (i - 2) * 3 + 3)
                eng.dma_start(out=pk[b][:, :], in_=ins[nm + "_pk"][t * 128:(t + 1) * 128, :]).then_inc(ld2[b].h, 16)
                eng.dma_start(out=sc[b][:, :], in_=ins[nm + "_sc"][t * 128:(t + 1) * 128, :]).then_inc(ld2[b].h, 16)

        @block.scalar
        def _(eng):
            eng = D(eng)
            for i in range(NT + 1):
                if i < NT:
                    eng.wait_ge(cmp.h, i * 3 + 1)
                    eng.activation(out=r4[:, :], in_=mm_[:, :], func=AF.Relu, bias=cb4[:, 0:1])
                    eng.activation(out=r62[:, :], in_=mm_[:, :], func=AF.Relu, bias=cb4[:, 1:2],
                                   scale=2.0).then_inc(rel.h, 1)
                if i >= 1:
                    ii = i - 1
                    nm, t = tiles[ii]
                    sb = ii % 2
                    eng.wait_ge(ev.h, (ii + 1) * 16)
                    if nm == "dn":
                        oap = outs[nm].ap()[:, t * 128:(t + 1) * 128].rearrange("(j p) f -> p j f", p=128)
                    else:
                        oap = outs[nm].ap()[t, :, :].rearrange("(j p) f -> p j f", p=128)
                    eng.dma_start(
                        out=oap,
                        in_=stg[sb][:, :].rearrange("p (j f) -> p j f", f=128),
                    ).then_inc(so2[ii % 2].h, 16)
            eng.wait_ge(so2[0].h, (NT // 2) * 16)
            eng.wait_ge(so2[1].h, (NT // 2) * 16)

        @block.tensor
        def _(eng):
            eng.wait_ge(lid.h, 16)
            for i in range(NT):
                b = i % 2
                eng.wait_ge(cmp.h, i * 3 + 3)
                for j in range(16):
                    g = i * 16 + j
                    if g >= 2:
                        eng.wait_ge(ev.h, g - 1)
                    eng.transpose(out=tslot[j % 2], in_=dec[b][:, j * 128:(j + 1) * 128],
                                  identity=idn[:, :]).then_inc(tsp.h, 1)

        @block.vector
        def _(eng):
            eng = D(eng)
            eng.memset(cb4[:, 0:1], -4.0)
            eng.memset(cb4[:, 1:2], -12.0)
            for i in range(NT + 1):
                if i < NT:
                    b = i % 2
                    eng.wait_ge(ld2[b].h, (i // 2 + 1) * 32)
                    if i >= 2:
                        eng.wait_ge(tsp.h, (i - 1) * 16)
                    eng.tensor_single_scalar(out=nib[:, 0:1024], in_=pk[b][:, :], scalar=15, op=OP.bitwise_and)
                    eng.tensor_single_scalar(out=nib[:, 1024:2048], in_=pk[b][:, :], scalar=4, op=OP.logical_shift_right)
                    eng.tensor_copy(out=cf[:, :], in_=nib[:, :])
                    eng.tensor_scalar(out=s8[:, :], in0=cf[:, :], scalar1=8.0, scalar2=8.0,
                                      op0=OP.is_ge, op1=OP.mult)
                    eng.tensor_tensor(out=mm_[:, :], in0=cf[:, :], in1=s8[:, :], op=OP.subtract)
                    eng.tensor_scalar(out=sgn[:, :], in0=s8[:, :], scalar1=-0.125, scalar2=0.5,
                                      op0=OP.mult, op1=OP.add).then_inc(cmp.h, 1)
                    eng.wait_ge(rel.h, i + 1)
                    eng.tensor_tensor(out=r4[:, :], in0=mm_[:, :], in1=r4[:, :], op=OP.add)
                    eng.tensor_tensor(out=r4[:, :], in0=r4[:, :], in1=r62[:, :], op=OP.add)
                    eng.tensor_tensor(out=r4[:, :], in0=r4[:, :], in1=sgn[:, :], op=OP.mult).then_inc(cmp.h, 1)
                    eng.tensor_tensor(
                        out=dec[b][:, :].rearrange("p (j two) -> p two j", two=2)
                                         .rearrange("p two (blk f) -> p two blk f", f=16),
                        in0=r4[:, :].rearrange("p (half j) -> p half j", half=2)
                                    .rearrange("p half (blk f) -> p half blk f", f=16),
                        in1=sc[b][:, :].unsqueeze(1).unsqueeze(3).broadcast_to([128, 2, 64, 16]),
                        op=OP.mult).then_inc(cmp.h, 1)
                if i >= 1:
                    sb = (i - 1) % 2
                    if i >= 3:
                        eng.wait_ge(so2[(i - 3) % 2].h, ((i - 3) // 2 + 1) * 16)
                    for j in range(16):
                        g = (i - 1) * 16 + j
                        eng.wait_ge(tsp.h, g + 1)
                        eng.tensor_copy(out=stg[sb][:, j * 128:(j + 1) * 128],
                                        in_=tslot[j % 2]).then_inc(ev.h, 1)

        if collect is not None:
            pccs, pairs = collect

            @block.gpsimd
            def _(eng):
                eng.wait_ge(so2[0].h, (NT // 2) * 16)
                eng.wait_ge(so2[1].h, (NT // 2) * 16)
                for (loc, shr) in pairs:
                    eng.collective_compute(
                        "AllGather", OP.bypass,
                        replica_groups=[list(range(NCORES))],
                        ins=[loc], outs=[shr],
                    ).then_inc(pccs, 1)


class Sem:

    def __init__(self, st, nc, name):
        self.h = st.enter_context(nc.semaphore(name))
        self.v = 0

    def inc(self, n=1):
        self.v += n
        return self.h



def _u8_view_int32_bytes(a_int32):
    v = a_int32.view(np.uint8).reshape(*a_int32.shape, 4)[..., 0]
    return np.ascontiguousarray(v)


_IDENT_BF16 = np.eye(128, dtype=ml_dtypes.bfloat16)

_NC_CACHE = {}


def _get_nc(name, builder):
    if name not in _NC_CACHE:
        _NC_CACHE[name] = builder()
    return _NC_CACHE[name]


def _decode_weights(inputs):
    nc = _get_nc("decode", build_decode)
    gp = _u8_view_int32_bytes(inputs["L1_ffn_gate_W_packed"])
    up = _u8_view_int32_bytes(inputs["L1_ffn_up_W_packed"])
    dn = _u8_view_int32_bytes(inputs["L1_ffn_down_W_packed"])
    ims = []
    for c in range(NCORES):
        r = slice(2048 * c, 2048 * (c + 1))
        ims.append({
            "gp_pk": gp[r], "gp_sc": np.ascontiguousarray(inputs["L1_ffn_gate_scales"][r]),
            "up_pk": up[r], "up_sc": np.ascontiguousarray(inputs["L1_ffn_up_scales"][r]),
            "dn_pk": np.ascontiguousarray(dn[:, 1024 * c:1024 * (c + 1)]),
            "dn_sc": np.ascontiguousarray(inputs["L1_ffn_down_scales"][:, 64 * c:64 * (c + 1)]),
            "ident": _IDENT_BF16,
        })
    res = run_bass_kernel_spmd(nc, ims, core_ids=list(range(NCORES)))
    wgT = np.concatenate([res.results[c]["wgT"] for c in range(NCORES)], axis=0)
    wuT = np.concatenate([res.results[c]["wuT"] for c in range(NCORES)], axis=0)
    wdT = np.concatenate([res.results[c]["wdT"] for c in range(NCORES)], axis=0)
    return wgT, wuT, wdT





def build_l1f():
    nc = bass.Bass("TRN2", target_bir_lowering=False, debug=False)
    x3 = nc.dram_tensor("x3", [2048, 2048], F32, kind="ExternalInput")
    wg3 = nc.dram_tensor("wg3", [128, 2048, 128], BF16, kind="ExternalInput")
    wu3 = nc.dram_tensor("wu3", [128, 2048, 128], BF16, kind="ExternalInput")
    wd = nc.dram_tensor("wd", [16384, 2048], BF16, kind="ExternalInput")
    ident = nc.dram_tensor("ident", [128, 128], BF16, kind="ExternalInput")
    x4 = nc.dram_tensor("x4", [2048, 2048], F32, kind="ExternalOutput")
    NG = 4

    emit_l1f(nc, x3, wg3, wu3, wd, ident, x4)
    return nc


def emit_l1f(nc, x3, wg3, wu3, wd, ident, x4, pccs=None, x3f_scr=None):
    NG = 4
    with contextlib.ExitStack() as st:
        ebuf = lambda name, shape, dt: st.enter_context(nc.sbuf_tensor(name, shape, dt, allow_name_mangling=True))
        x3g = ebuf("x3g", [128, 4 * 2048], F32)
        x3b = ebuf("x3b", [128, 2048], BF16)
        idn = ebuf("idn", [128, 128], BF16)
        X3g = ebuf("X3g", [128, 16 * 512], BF16)
        wbuf = [ebuf(f"wbuf{i}", [128, 4096], BF16) for i in range(2)]
        inter = ebuf("inter", [128, 128 * 512], BF16)
        wdb = [ebuf(f"wdb{i}", [128, 2048], BF16) for i in range(2)]
        gl = ebuf("gl", [128, 512], F32)
        v4 = ebuf("v4", [128, 4], F32)
        mrow = ebuf("mrow", [128, 4], F32)

        tps = st.enter_context(nc.psum_tensor("tps", [128, 2048], BF16, allow_name_mangling=True))
        tslot = [tps[:, 0:128], tps[:, 1024:1152]]
        pgu = st.enter_context(nc.psum_tensor("pgu", [128, 2048], F32, allow_name_mangling=True))
        pd2 = st.enter_context(nc.psum_tensor("pd2", [128, 1024], F32, allow_name_mangling=True))
        pg = [pgu[:, 0:512], pgu[:, 512:1024], pgu[:, 1024:1536]]
        pu = [pgu[:, 1536:2048], pd2[:, 0:512], pd2[:, 512:1024]]
        pd = [pgu[:, i * 512:(i + 1) * 512] for i in range(4)]

        lid = Sem(st, nc, "lid")
        lx = Sem(st, nc, "lx")
        lgx = Sem(st, nc, "lgx")
        cxb = Sem(st, nc, "cxb")
        ext = Sem(st, nc, "ext")
        tsp = Sem(st, nc, "tsp")
        lw = [Sem(st, nc, "lw0"), Sem(st, nc, "lw1")]
        mmg = Sem(st, nc, "mmg")
        geg = Sem(st, nc, "geg")
        gem = Sem(st, nc, "gem")
        lwd = [Sem(st, nc, "lwd0"), Sem(st, nc, "lwd1")]
        mmd = Sem(st, nc, "mmd")
        evd = Sem(st, nc, "evd")
        lnx = Sem(st, nc, "lnx")
        lnq = Sem(st, nc, "lnq")
        lnr = Sem(st, nc, "lnr")
        lns = Sem(st, nc, "lns")
        x4r = Sem(st, nc, "x4r")
        sst = Sem(st, nc, "sst")
        block = st.enter_context(nc.Block())

        @block.sync
        def _(eng):
            if pccs is not None:
                eng.wait_ge(pccs, 3)
            eng.dma_start(out=idn[:, :], in_=ident[:, :]).then_inc(lid.h, 16)

            def load_X3g(g):
                for kc in range(16):
                    eng.dma_start(out=X3g[:, kc * 512:(kc + 1) * 512],
                                  in_=x3f_scr.ap()[kc, :, g * 512:(g + 1) * 512]).then_inc(lgx.h, 16)

            def load_wj(g, j):
                b = j % 2
                eng.dma_start(out=wbuf[b][:, 0:2048],
                              in_=wg3.ap()[j, :, :].rearrange("(c p) f -> p c f", p=128)
                              ).then_inc(lw[b].h, 16)
                eng.dma_start(out=wbuf[b][:, 2048:4096],
                              in_=wu3.ap()[j, :, :].rearrange("(c p) f -> p c f", p=128)
                              ).then_inc(lw[b].h, 16)

            load_X3g(0)
            for g in range(NG):
                if g > 0:
                    eng.wait_ge(sst.h, g * 64)
                for t in range(4):
                    eng.dma_start(out=x3g[:, t * 2048:(t + 1) * 2048],
                                  in_=x3[(g * 4 + t) * 128:(g * 4 + t + 1) * 128, :]).then_inc(lx.h, 16)
                for j in range(2 if g > 0 else 0, 128):
                    if g > 0 or j >= 2:
                        eng.wait_ge(mmg.h, g * 128 + j - 1)
                    load_wj(g, j)
                for H in range(4):
                    for jj in range(32):
                        b = jj % 2
                        gi = (g * 4 + H) * 32 + jj
                        if gi >= 2:
                            eng.wait_ge(mmd.h, gi - 1)
                        eng.dma_start(out=wdb[b][:, :].rearrange("p (four f) -> p four f", f=512),
                                      in_=wd.ap()[jj * 512:(jj + 1) * 512, H * 512:(H + 1) * 512]
                                      .rearrange("(four p) f -> p four f", p=128)
                                      ).then_inc(lwd[b].h, 16)
                    if H == 1 and g + 1 < NG:
                        eng.wait_ge(mmg.h, (g + 1) * 128)
                        load_X3g(g + 1)
                        load_wj(g + 1, 0)
                        load_wj(g + 1, 1)

        @block.tensor
        def _(eng):
            eng.wait_ge(lid.h, 16)
            for g in range(NG):
                if g > 0:
                    eng.wait_ge(evd.h, g * 16)
                eng.wait_ge(lgx.h, (g + 1) * 256)
                for j in range(128):
                    jb = j % 3
                    eng.wait_ge(lw[j % 2].h, (g * 64 + j // 2 + 1) * 32)
                    if g > 0 or j >= 3:
                        eng.wait_ge(gem.h, g * 128 + j - 2)
                    for kc in range(16):
                        eng.matmul(out=pg[jb], lhsT=wbuf[j % 2][:, kc * 128:(kc + 1) * 128],
                                   rhs=X3g[:, kc * 512:(kc + 1) * 512],
                                   start=(kc == 0), stop=(kc == 15))
                    for kc in range(16):
                        mm = eng.matmul(out=pu[jb], lhsT=wbuf[j % 2][:, 2048 + kc * 128:2048 + (kc + 1) * 128],
                                        rhs=X3g[:, kc * 512:(kc + 1) * 512],
                                        start=(kc == 0), stop=(kc == 15))
                    mm.then_inc(mmg.h, 1)
                eng.wait_ge(gem.h, (g + 1) * 128)
                for H in range(4):
                    if H > 0:
                        eng.wait_ge(evd.h, (g * 4 + H) * 4)
                    for jj in range(32):
                        b = jj % 2
                        gi = (g * 4 + H) * 32 + jj
                        eng.wait_ge(lwd[b].h, (gi // 2 + 1) * 16)
                        for jq in range(4):
                            j = jj * 4 + jq
                            for t in range(4):
                                mm = eng.matmul(out=pd[t],
                                                lhsT=inter[:, j * 512 + t * 128:j * 512 + (t + 1) * 128],
                                                rhs=wdb[b][:, jq * 512:(jq + 1) * 512],
                                                start=(j == 0), stop=(j == 127),
                                                skip_group_check=True)
                        mm.then_inc(mmd.h, 1)

        @block.vector
        def _(eng):
            eng = D(eng)
            for g in range(NG):
                for j in range(128):
                    jb = j % 3
                    eng.wait_ge(geg.h, g * 128 + j + 1)
                    eng.tensor_tensor(out=inter[:, j * 512:(j + 1) * 512], in0=gl[:, :],
                                      in1=pu[jb], op=OP.mult).then_inc(gem.h, 1)
                eng.wait_ge(lx.h, (g + 1) * 64)
                for H in range(4):
                    eng.wait_ge(mmd.h, (g * 4 + H + 1) * 32)
                    for t in range(4):
                        sl = x3g[:, t * 2048 + H * 512:t * 2048 + (H + 1) * 512]
                        eng.tensor_tensor(out=sl, in0=pd[t], in1=sl, op=OP.add).then_inc(evd.h, 1)
                for t in range(4):
                    idx = g * 4 + t
                    sl = x3g[:, t * 2048:(t + 1) * 2048]
                    eng.tensor_reduce(out=mrow[:, 0:1], in_=sl, axis=AX.X, op=OP.add)
                    eng.tensor_scalar(out=mrow[:, 0:1], in0=mrow[:, 0:1], scalar1=1.0 / 2048.0,
                                      scalar2=None, op0=OP.mult)
                    eng.tensor_scalar(out=sl, in0=sl, scalar1=mrow[:, 0:1], scalar2=None,
                                      op0=OP.subtract).then_inc(lnx.h, 1)
                    eng.wait_ge(lnq.h, idx + 1)
                    eng.tensor_reduce(out=mrow[:, 1:2], in_=v4[:, :], axis=AX.X, op=OP.add)
                    eng.tensor_scalar(out=mrow[:, 1:2], in0=mrow[:, 1:2], scalar1=1.0 / 2048.0,
                                      scalar2=1e-5, op0=OP.mult, op1=OP.add)
                    eng.reciprocal(out=mrow[:, 2:3], in_=mrow[:, 1:2]).then_inc(lnr.h, 1)
                    eng.wait_ge(lns.h, idx + 1)
                    eng.tensor_scalar(out=sl, in0=sl, scalar1=mrow[:, 3:4], scalar2=None,
                                      op0=OP.mult).then_inc(x4r.h, 1)

        @block.scalar
        def _(eng):
            eng = D(eng)
            for g in range(NG):
                for j in range(128):
                    jb = j % 3
                    eng.wait_ge(mmg.h, g * 128 + j + 1)
                    if g > 0 or j >= 1:
                        eng.wait_ge(gem.h, g * 128 + j)
                    eng.activation(out=gl[:, :], in_=pg[jb],
                                   func=AF.Gelu_apprx_tanh).then_inc(geg.h, 1)
                for t in range(4):
                    idx = g * 4 + t
                    eng.wait_ge(lnx.h, idx + 1)
                    sl = x3g[:, t * 2048:(t + 1) * 2048]
                    for c in range(4):
                        a = eng.activation(out=gl[:, :], in_=sl[:, c * 512:(c + 1) * 512],
                                           func=AF.Square, accum_out=v4[:, c:c + 1])
                    a.then_inc(lnq.h, 1)
                    eng.wait_ge(lnr.h, idx + 1)
                    eng.sqrt(out=mrow[:, 3:4], in_=mrow[:, 2:3]).then_inc(lns.h, 1)
                for t in range(4):
                    eng.wait_ge(x4r.h, g * 4 + t + 1)
                    eng.dma_start(out=x4.ap()[g + t * 512: g + t * 512 + 509:4, :],
                                  in_=x3g[:, t * 2048:(t + 1) * 2048]).then_inc(sst.h, 16)
            eng.wait_ge(sst.h, NG * 64)



def build_l0f():
    nc = bass.Bass("TRN2", target_bir_lowering=False, debug=False)
    x1 = nc.dram_tensor("x1", [2048, 2048], F32, kind="ExternalInput")
    gup = nc.dram_tensor("gup", [4, 2048, 4096], F32, kind="ExternalInput")
    dwn = nc.dram_tensor("dwn", [4, 2048, 2048], F32, kind="ExternalInput")
    ident = nc.dram_tensor("ident", [128, 128], BF16, kind="ExternalInput")
    x2 = nc.dram_tensor("x2", [2048, 2048], F32, kind="ExternalOutput")
    NE, NJ = 4, 16

    emit_l0f(nc, x1, gup, dwn, ident, x2)
    return nc


def emit_l0f(nc, x1, gup, dwn, ident, x2, x2f_scr=None):
    NE, NJ = 4, 16
    with contextlib.ExitStack() as st:
        ebuf = lambda name, shape, dt: st.enter_context(nc.sbuf_tensor(name, shape, dt, allow_name_mangling=True))
        x1g = ebuf("x1g", [128, 4 * 2048], F32)
        x1b = ebuf("x1b", [128, 2048], BF16)
        idn = ebuf("idn", [128, 128], BF16)
        X1e = ebuf("X1e", [128, 16 * 512], BF16)
        wbuf = [ebuf(f"wbuf{i}", [128, 4096], BF16) for i in range(2)]
        inter = ebuf("inter", [128, NJ * 512], BF16)
        wdb = [ebuf(f"wdb{i}", [128, 2048], BF16) for i in range(2)]
        gl = ebuf("gl", [128, 512], F32)
        xfb = ebuf("xfb", [128, 2048], BF16)
        stgE = [ebuf(f"stgE{i}", [128, 2048], BF16) for i in range(2)]
        v4 = ebuf("v4", [128, 4], F32)
        mrow = ebuf("mrow", [128, 4], F32)

        tps = st.enter_context(nc.psum_tensor("tps", [128, 2048], BF16, allow_name_mangling=True))
        tslot = [tps[:, 0:128], tps[:, 1024:1152]]
        pgu = st.enter_context(nc.psum_tensor("pgu", [128, 2048], F32, allow_name_mangling=True))
        pd2 = st.enter_context(nc.psum_tensor("pd2", [128, 1024], F32, allow_name_mangling=True))
        pg = [pgu[:, 0:512], pgu[:, 512:1024], pgu[:, 1024:1536]]
        pu = [pgu[:, 1536:2048], pd2[:, 0:512], pd2[:, 512:1024]]
        pd = [pgu[:, i * 512:(i + 1) * 512] for i in range(4)]

        lid = Sem(st, nc, "lid")
        lx = Sem(st, nc, "lx")
        cxb = Sem(st, nc, "cxb")
        ext = Sem(st, nc, "ext")
        tsp = Sem(st, nc, "tsp")
        lw = [Sem(st, nc, "lw0"), Sem(st, nc, "lw1")]
        mmg = Sem(st, nc, "mmg")
        geg = Sem(st, nc, "geg")
        gem = Sem(st, nc, "gem")
        lwd = [Sem(st, nc, "lwd0"), Sem(st, nc, "lwd1")]
        mmd = Sem(st, nc, "mmd")
        evd = Sem(st, nc, "evd")
        lnx = Sem(st, nc, "lnx")
        lnq = Sem(st, nc, "lnq")
        lnr = Sem(st, nc, "lnr")
        lns = Sem(st, nc, "lns")
        x4r = Sem(st, nc, "x4r")
        cxe = Sem(st, nc, "cxe")
        tspE = Sem(st, nc, "tspE")
        extE = Sem(st, nc, "extE")
        sxf2 = [Sem(st, nc, "sxfa"), Sem(st, nc, "sxfb")]
        sst = Sem(st, nc, "sst")
        block = st.enter_context(nc.Block())

        @block.sync
        def _(eng):
            eng.dma_start(out=idn[:, :], in_=ident[:, :]).then_inc(lid.h, 16)
            for e in range(NE):
                if e > 0:
                    eng.wait_ge(sst.h, e * 64)
                    if x2f_scr is not None:
                        eng.wait_ge(cxe.h, e * 4)
                for t in range(4):
                    eng.dma_start(out=x1g[:, t * 2048:(t + 1) * 2048],
                                  in_=x1.ap()[e + t * 512: e + t * 512 + 509:4, :]).then_inc(lx.h, 16)
                if x2f_scr is not None:
                    for t in range(4):
                        idx = e * 4 + t
                        sb = idx % 2
                        eng.wait_ge(extE.h, (idx + 1) * 16)
                        eng.dma_start(
                            out=x2f_scr.ap()[:, :, idx * 128:(idx + 1) * 128].rearrange("c p f -> p c f"),
                            in_=stgE[sb][:, :].rearrange("p (c f) -> p c f", f=128),
                        ).then_inc(sxf2[sb].h, 16)
            if x2f_scr is not None:
                eng.wait_ge(sxf2[0].h, 128)
                eng.wait_ge(sxf2[1].h, 128)

        @block.gpsimd
        def _(eng):
            for e in range(NE):
                for j in range(NJ):
                    b = j % 2
                    if e > 0 or j >= 2:
                        eng.wait_ge(mmg.h, e * NJ + j - 1)
                    eng.dma_start(out=wbuf[b][:, 0:2048],
                                  in_=gup.ap()[e, :, j * 128:(j + 1) * 128]
                                  .rearrange("(c p) f -> p c f", p=128)).then_inc(lw[b].h, 16)
                    eng.dma_start(out=wbuf[b][:, 2048:4096],
                                  in_=gup.ap()[e, :, 2048 + j * 128: 2048 + (j + 1) * 128]
                                  .rearrange("(c p) f -> p c f", p=128)).then_inc(lw[b].h, 16)
                for H in range(4):
                    for jj in range(NJ // 4):
                        b = jj % 2
                        gi = (e * 4 + H) * (NJ // 4) + jj
                        if gi >= 2:
                            eng.wait_ge(mmd.h, gi - 1)
                        eng.dma_start(out=wdb[b][:, :].rearrange("p (four f) -> p four f", f=512),
                                      in_=dwn.ap()[e, jj * 512:(jj + 1) * 512, H * 512:(H + 1) * 512]
                                      .rearrange("(four p) f -> p four f", p=128)
                                      ).then_inc(lwd[b].h, 16)

        @block.tensor
        def _(eng):
            eng.wait_ge(lid.h, 16)
            for e in range(NE):
                for t in range(4):
                    eng.wait_ge(cxb.h, e * 4 + t + 1)
                    for kc in range(16):
                        gi = (e * 4 + t) * 16 + kc
                        if e > 0 and t == 0 and kc < 2 and x2f_scr is not None:
                            eng.wait_ge(extE.h, e * 64)
                        if gi >= 2:
                            eng.wait_ge(ext.h, gi - 1)
                        eng.transpose(out=tslot[kc % 2], in_=x1b[:, kc * 128:(kc + 1) * 128],
                                      identity=idn[:, :]).then_inc(tsp.h, 1)
                if e > 0:
                    eng.wait_ge(evd.h, e * 16)
                eng.wait_ge(ext.h, (e * 4 + 4) * 16)
                for j in range(NJ):
                    jb = j % 3
                    eng.wait_ge(lw[j % 2].h, (e * (NJ // 2) + j // 2 + 1) * 32)
                    if e > 0 or j >= 3:
                        eng.wait_ge(gem.h, e * NJ + j - 2)
                    for kc in range(16):
                        eng.matmul(out=pg[jb], lhsT=wbuf[j % 2][:, kc * 128:(kc + 1) * 128],
                                   rhs=X1e[:, kc * 512:(kc + 1) * 512],
                                   start=(kc == 0), stop=(kc == 15))
                    for kc in range(16):
                        mm = eng.matmul(out=pu[jb], lhsT=wbuf[j % 2][:, 2048 + kc * 128:2048 + (kc + 1) * 128],
                                        rhs=X1e[:, kc * 512:(kc + 1) * 512],
                                        start=(kc == 0), stop=(kc == 15))
                    mm.then_inc(mmg.h, 1)
                eng.wait_ge(gem.h, (e + 1) * NJ)
                for H in range(4):
                    if H > 0:
                        eng.wait_ge(evd.h, (e * 4 + H) * 4)
                    for jj in range(NJ // 4):
                        b = jj % 2
                        gi = (e * 4 + H) * (NJ // 4) + jj
                        eng.wait_ge(lwd[b].h, (gi // 2 + 1) * 16)
                        for jq in range(4):
                            j = jj * 4 + jq
                            for t in range(4):
                                mm = eng.matmul(out=pd[t],
                                                lhsT=inter[:, j * 512 + t * 128:j * 512 + (t + 1) * 128],
                                                rhs=wdb[b][:, jq * 512:(jq + 1) * 512],
                                                start=(j == 0), stop=(j == NJ - 1),
                                                skip_group_check=True)
                        mm.then_inc(mmd.h, 1)
                if x2f_scr is not None:
                    for t in range(4):
                        idx = e * 4 + t
                        eng.wait_ge(cxe.h, idx + 1)
                        for kc in range(16):
                            gE = idx * 16 + kc
                            if t == 0 and kc < 2:
                                eng.wait_ge(ext.h, (e + 1) * 64)
                            if gE >= 2:
                                eng.wait_ge(extE.h, gE - 1)
                            eng.transpose(out=tslot[gE % 2], in_=xfb[:, kc * 128:(kc + 1) * 128],
                                          identity=idn[:, :]).then_inc(tspE.h, 1)

        @block.vector
        def _(eng):
            eng = D(eng)
            for e in range(NE):
                for t in range(4):
                    eng.wait_ge(lx.h, (e + 1) * 64)
                    if t > 0:
                        eng.wait_ge(tsp.h, (e * 4 + t) * 16)
                    eng.tensor_copy(out=x1b[:, :], in_=x1g[:, t * 2048:(t + 1) * 2048]).then_inc(cxb.h, 1)
                    for kc in range(16):
                        gi = (e * 4 + t) * 16 + kc
                        eng.wait_ge(tsp.h, gi + 1)
                        eng.tensor_copy(out=X1e[:, kc * 512 + t * 128:kc * 512 + (t + 1) * 128],
                                        in_=tslot[kc % 2]).then_inc(ext.h, 1)
                for j in range(NJ):
                    jb = j % 3
                    eng.wait_ge(geg.h, e * NJ + j + 1)
                    eng.tensor_tensor(out=inter[:, j * 512:(j + 1) * 512], in0=gl[:, :],
                                      in1=pu[jb], op=OP.mult).then_inc(gem.h, 1)
                for H in range(4):
                    eng.wait_ge(mmd.h, (e * 4 + H + 1) * (NJ // 4))
                    for t in range(4):
                        sl = x1g[:, t * 2048 + H * 512:t * 2048 + (H + 1) * 512]
                        eng.tensor_tensor(out=sl, in0=pd[t], in1=sl, op=OP.add).then_inc(evd.h, 1)
                for t in range(4):
                    idx = e * 4 + t
                    sl = x1g[:, t * 2048:(t + 1) * 2048]
                    eng.tensor_reduce(out=mrow[:, 0:1], in_=sl, axis=AX.X, op=OP.add)
                    eng.tensor_scalar(out=mrow[:, 0:1], in0=mrow[:, 0:1], scalar1=1.0 / 2048.0,
                                      scalar2=None, op0=OP.mult)
                    eng.tensor_scalar(out=sl, in0=sl, scalar1=mrow[:, 0:1], scalar2=None,
                                      op0=OP.subtract).then_inc(lnx.h, 1)
                    eng.wait_ge(lnq.h, idx + 1)
                    eng.tensor_reduce(out=mrow[:, 1:2], in_=v4[:, :], axis=AX.X, op=OP.add)
                    eng.tensor_scalar(out=mrow[:, 1:2], in0=mrow[:, 1:2], scalar1=1.0 / 2048.0,
                                      scalar2=1e-5, op0=OP.mult, op1=OP.add)
                    eng.reciprocal(out=mrow[:, 2:3], in_=mrow[:, 1:2]).then_inc(lnr.h, 1)
                    eng.wait_ge(lns.h, idx + 1)
                    eng.tensor_scalar(out=sl, in0=sl, scalar1=mrow[:, 3:4], scalar2=None,
                                      op0=OP.mult).then_inc(x4r.h, 1)
                    if x2f_scr is not None:
                        if idx >= 1:
                            eng.wait_ge(tspE.h, idx * 16)
                        eng.tensor_copy(out=xfb[:, :], in_=sl).then_inc(cxe.h, 1)

        @block.scalar
        def _(eng):
            eng = D(eng)
            for e in range(NE):
                for j in range(NJ):
                    jb = j % 3
                    eng.wait_ge(mmg.h, e * NJ + j + 1)
                    if e > 0 or j >= 1:
                        eng.wait_ge(gem.h, e * NJ + j)
                    eng.activation(out=gl[:, :], in_=pg[jb], func=AF.Silu).then_inc(geg.h, 1)
                for t in range(4):
                    idx = e * 4 + t
                    eng.wait_ge(lnx.h, idx + 1)
                    sl = x1g[:, t * 2048:(t + 1) * 2048]
                    for c in range(4):
                        a = eng.activation(out=gl[:, :], in_=sl[:, c * 512:(c + 1) * 512],
                                           func=AF.Square, accum_out=v4[:, c:c + 1])
                    a.then_inc(lnq.h, 1)
                    eng.wait_ge(lnr.h, idx + 1)
                    eng.sqrt(out=mrow[:, 3:4], in_=mrow[:, 2:3]).then_inc(lns.h, 1)
                    eng.wait_ge(x4r.h, idx + 1)
                    eng.dma_start(out=x2[e * 512 + t * 128: e * 512 + (t + 1) * 128, :],
                                  in_=x1g[:, t * 2048:(t + 1) * 2048]).then_inc(sst.h, 16)
                    if x2f_scr is not None:
                        sb = idx % 2
                        if idx >= 2:
                            eng.wait_ge(sxf2[idx % 2].h, (idx // 2) * 16)
                        for kc in range(16):
                            gE = idx * 16 + kc
                            eng.wait_ge(tspE.h, gE + 1)
                            eng.copy(out=stgE[sb][:, kc * 128:(kc + 1) * 128],
                                     in_=tslot[gE % 2]).then_inc(extE.h, 1)
            eng.wait_ge(sst.h, NE * 64)



def build_l1a():
    nc = bass.Bass("TRN2", target_bir_lowering=False, debug=False)
    x2 = nc.dram_tensor("x2", [2048, 2048], F32, kind="ExternalInput")
    wqkv = nc.dram_tensor("wqkv", [2048, 6144], F32, kind="ExternalInput")
    bqkv = nc.dram_tensor("bqkv", [1, 6144], F32, kind="ExternalInput")
    wout = nc.dram_tensor("wout", [2048, 2048], F32, kind="ExternalInput")
    bout = nc.dram_tensor("bout", [1, 2048], F32, kind="ExternalInput")
    cos2 = nc.dram_tensor("cos2", [2048, 64], F32, kind="ExternalInput")
    sin2 = nc.dram_tensor("sin2", [2048, 64], F32, kind="ExternalInput")
    ident = nc.dram_tensor("ident", [128, 128], BF16, kind="ExternalInput")
    x3 = nc.dram_tensor("x3", [2048, 2048], F32, kind="ExternalOutput")
    qkv_scr = nc.dram_tensor("qkv_scr", [2048, 6144], BF16)

    with contextlib.ExitStack() as st:
        ebuf = lambda name, shape, dt: st.enter_context(nc.sbuf_tensor(name, shape, dt, allow_name_mangling=True))
        R1 = ebuf("R1", [128, 32768], BF16)
        R2 = ebuf("R2", [128, 32768], BF16)
        xin = [ebuf(f"xin{i}", [128, 2048], F32) for i in range(2)]
        xb = ebuf("xb", [128, 2048], BF16)
        idn = ebuf("idn", [128, 128], BF16)
        ones = ebuf("ones", [1, 128], BF16)
        brow = ebuf("brow", [1, 6144], BF16)
        borow = ebuf("borow", [1, 2048], BF16)
        cosb = ebuf("cosb", [128, 16 * 64], F32)
        sinb = ebuf("sinb", [128, 16 * 64], F32)
        m1 = ebuf("m1", [128, 512], F32)
        m2 = ebuf("m2", [128, 512], F32)
        qt = [ebuf(f"qt{i}", [128, 512], BF16) for i in range(2)]
        scb = ebuf("scb", [128, 256], F32)
        qs_buf = ebuf("qs_buf", [128, 2048], BF16)
        exb = ebuf("exb", [128, 256], F32)
        srow = ebuf("srow", [128, 128], F32)
        pb = ebuf("pb", [128, 256], BF16)
        x3b = [ebuf(f"x3b{i}", [128, 2048], F32) for i in range(2)]
        v4 = ebuf("v4", [128, 4], F32)
        mrow = ebuf("mrow", [128, 4], F32)

        tps = st.enter_context(nc.psum_tensor("tps", [128, 2048], BF16, allow_name_mangling=True))
        tslot = [tps[:, 0:128], tps[:, 1024:1152]]
        pq = st.enter_context(nc.psum_tensor("pq", [128, 2048], F32, allow_name_mangling=True))
        pqs = [pq[:, 0:512], pq[:, 512:1024]]

        lid = Sem(st, nc, "lid")
        lbq = Sem(st, nc, "lbq")
        lx2 = [Sem(st, nc, "lx2a"), Sem(st, nc, "lx2b")]
        cxb = Sem(st, nc, "cxb")
        tsp = Sem(st, nc, "tsp")
        ext = Sem(st, nc, "ext")
        lwq = [Sem(st, nc, "lwq0"), Sem(st, nc, "lwq1")]
        mmq = Sem(st, nc, "mmq")
        evq = Sem(st, nc, "evq")
        sq2 = [Sem(st, nc, "sq0"), Sem(st, nc, "sq1")]
        lqr = Sem(st, nc, "lqr")
        lqs2 = Sem(st, nc, "lqs2")
        sxd = Sem(st, nc, "sxd")
        sxe = Sem(st, nc, "sxe")
        vcx = Sem(st, nc, "vcx")
        lwo = Sem(st, nc, "lwo")
        mmo = Sem(st, nc, "mmo")
        evo = Sem(st, nc, "evo")
        lnx = Sem(st, nc, "lnx")
        lnq = Sem(st, nc, "lnq")
        lnr = Sem(st, nc, "lnr")
        lns = Sem(st, nc, "lns")
        x3r = Sem(st, nc, "x3r")
        sst2 = [Sem(st, nc, "ssta"), Sem(st, nc, "sstb")]
        block = st.enter_context(nc.Block())

        NQ = 192

        def wb(cb):
            return R2[:, cb * 16384:(cb + 1) * 16384]

        def kvr(slot):
            return R2[:, slot * 4096:(slot + 1) * 4096]

        def ctxf(kc):
            return R2[:, kc * 2048:(kc + 1) * 2048]

        def x2f(kc):
            return R1[:, kc * 2048:(kc + 1) * 2048]

        def ctxbf(tile):
            return R1[:, tile * 2048:(tile + 1) * 2048]

        def woutr(kc):
            return R1[:, kc * 2048:(kc + 1) * 2048]

        @block.sync
        def _(eng):
            eng.dma_start(out=idn[:, :], in_=ident[:, :]).then_inc(lid.h, 16)
            eng.dma_start(out=cosb[:, :], in_=cos2.ap().rearrange("(t p) f -> p t f", p=128)).then_inc(lid.h, 16)
            eng.dma_start(out=sinb[:, :], in_=sin2.ap().rearrange("(t p) f -> p t f", p=128)).then_inc(lid.h, 16)
            for i in range(16):
                b = i % 2
                if i >= 2:
                    eng.wait_ge(cxb.h, i - 1)
                eng.dma_start(out=xin[b][:, :], in_=x2[i * 128:(i + 1) * 128, :]).then_inc(lx2[b].h, 16)
            for qi in range(NQ):
                eng.wait_ge(evq.h, qi + 1)
                c, i = qi // 16, qi % 16
                eng.dma_start(out=qkv_scr[i * 128:(i + 1) * 128, c * 512:(c + 1) * 512],
                              in_=qt[qi % 2][:, :]).then_inc(sq2[qi % 2].h, 16)
            lqs = lqr
            for b in range(2):
                if b == 1:
                    eng.wait_ge(vcx.h, 8)
                else:
                    eng.wait_ge(sq2[0].h, (NQ // 2) * 16)
                    eng.wait_ge(sq2[1].h, (NQ // 2) * 16)
                for s in range(8):
                    tile = 2 * s + b
                    eng.dma_start(out=kvr(s), in_=qkv_scr[tile * 128:(tile + 1) * 128, 2048:6144]).then_inc(lqr.h, 16)
                for s in range(8):
                    bs = b * 8 + s
                    if bs > 0:
                        eng.wait_ge(sxd.h, bs)
                    tile = 2 * s + b
                    eng.dma_start(out=qs_buf[:, :], in_=qkv_scr[tile * 128:(tile + 1) * 128, 0:2048]).then_inc(lqs2.h, 16)
            eng.wait_ge(vcx.h, 16)
            for i in range(16):
                b = i % 2
                if i >= 2:
                    eng.wait_ge(evo.h, (i - 2) * 4 + 4)
                eng.dma_start(out=xin[b][:, :], in_=x2[i * 128:(i + 1) * 128, :]).then_inc(lx2[b].h, 16)

        @block.gpsimd
        def _(eng):
            eng.dma_start(out=brow[:, :], in_=bqkv[:, :]).then_inc(lbq.h, 16)
            eng.dma_start(out=borow[:, :], in_=bout[:, :]).then_inc(lbq.h, 16)
            for c in range(12):
                cb = c % 2
                if c >= 2:
                    eng.wait_ge(mmq.h, (c - 1) * 16)
                for kc in range(16):
                    eng.dma_start(out=wb(cb)[:, kc * 512:(kc + 1) * 512],
                                  in_=wqkv[kc * 128:(kc + 1) * 128, c * 512:(c + 1) * 512]
                                  ).then_inc(lwq[cb].h, 16)
            eng.wait_ge(tsp.h, 256 + 256)
            for kc in range(16):
                eng.dma_start(out=woutr(kc), in_=wout[kc * 128:(kc + 1) * 128, :]).then_inc(lwo.h, 16)

        @block.tensor
        def _(eng):
            eng.wait_ge(lid.h, 48)
            for i in range(16):
                eng.wait_ge(cxb.h, i + 1)
                for kc in range(16):
                    gi = i * 16 + kc
                    if gi >= 2:
                        eng.wait_ge(ext.h, gi - 1)
                    eng.transpose(out=tslot[kc % 2], in_=xb[:, kc * 128:(kc + 1) * 128],
                                  identity=idn[:, :]).then_inc(tsp.h, 1)
            eng.wait_ge(lbq.h, 32)
            eng.wait_ge(ext.h, 256)
            for qi in range(NQ):
                c, i = qi // 16, qi % 16
                cb = c % 2
                if i == 0:
                    eng.wait_ge(lwq[cb].h, (c // 2 + 1) * 256)
                if qi >= 2:
                    eng.wait_ge(evq.h, qi - 1)
                for kc in range(16):
                    eng.matmul(out=pqs[qi % 2], lhsT=x2f(kc)[:, i * 128:(i + 1) * 128],
                               rhs=wb(cb)[:, kc * 512:(kc + 1) * 512],
                               start=(kc == 0), stop=False)
                eng.matmul(out=pqs[qi % 2], lhsT=ones[0:1, :], rhs=brow[0:1, c * 512:(c + 1) * 512],
                           start=False, stop=True).then_inc(mmq.h, 1)
            eng.wait_ge(vcx.h, 16)
            for tile in range(16):
                for kc in range(16):
                    gi = 256 + tile * 16 + kc
                    eng.wait_ge(ext.h, gi - 1)
                    eng.transpose(out=tslot[kc % 2], in_=ctxbf(tile)[:, kc * 128:(kc + 1) * 128],
                                  identity=idn[:, :]).then_inc(tsp.h, 1)
            eng.wait_ge(lwo.h, 256)
            eng.wait_ge(ext.h, 512)
            for i in range(16):
                for c in range(4):
                    oi = i * 4 + c
                    if oi >= 2:
                        eng.wait_ge(evo.h, oi - 1)
                    for kc in range(16):
                        eng.matmul(out=pqs[oi % 2], lhsT=ctxf(kc)[:, i * 128:(i + 1) * 128],
                                   rhs=woutr(kc)[:, c * 512:(c + 1) * 512],
                                   start=(kc == 0), stop=False)
                    eng.matmul(out=pqs[oi % 2], lhsT=ones[0:1, :], rhs=borow[0:1, c * 512:(c + 1) * 512],
                               start=False, stop=True).then_inc(mmo.h, 1)

        @block.vector
        def _(eng):
            eng = D(eng)
            eng.memset(ones[:, :], 1.0)
            for i in range(16):
                b = i % 2
                eng.wait_ge(lx2[b].h, (i // 2 + 1) * 16)
                if i > 0:
                    eng.wait_ge(tsp.h, i * 16)
                eng.tensor_copy(out=xb[:, :], in_=xin[b][:, :]).then_inc(cxb.h, 1)
                for kc in range(16):
                    gi = i * 16 + kc
                    eng.wait_ge(tsp.h, gi + 1)
                    eng.tensor_copy(out=x2f(kc)[:, i * 128:(i + 1) * 128],
                                    in_=tslot[kc % 2]).then_inc(ext.h, 1)
            for qi in range(NQ):
                c, i = qi // 16, qi % 16
                eng.wait_ge(mmq.h, qi + 1)
                if qi >= 2:
                    eng.wait_ge(sq2[qi % 2].h, (qi // 2) * 16)
                ps = pqs[qi % 2]
                if c < 8:
                    cs = cosb[:, i * 64:(i + 1) * 64].rearrange("p (o f) -> p o f", o=1)
                    sn = sinb[:, i * 64:(i + 1) * 64].rearrange("p (o f) -> p o f", o=1)
                    eng.tensor_tensor(out=m1[:, :].rearrange("p (h f) -> p h f", f=64),
                                      in0=ps.rearrange("p (h f) -> p h f", f=64),
                                      in1=cs.broadcast_to([128, 8, 64]), op=OP.mult)
                    eng.tensor_tensor(out=m2[:, :].rearrange("p (h f) -> p h f", f=64),
                                      in0=ps.rearrange("p (h f) -> p h f", f=64),
                                      in1=sn.broadcast_to([128, 8, 64]), op=OP.mult)
                    qv = qt[qi % 2][:, :].rearrange("p (h two f) -> p h two f", two=2, f=32)
                    eng.tensor_tensor(out=qv[:, :, 0, :],
                                      in0=m1[:, :].rearrange("p (h two f) -> p h two f", two=2, f=32)[:, :, 0, :],
                                      in1=m2[:, :].rearrange("p (h two f) -> p h two f", two=2, f=32)[:, :, 1, :],
                                      op=OP.subtract)
                    eng.tensor_tensor(out=qv[:, :, 1, :],
                                      in0=m1[:, :].rearrange("p (h two f) -> p h two f", two=2, f=32)[:, :, 1, :],
                                      in1=m2[:, :].rearrange("p (h two f) -> p h two f", two=2, f=32)[:, :, 0, :],
                                      op=OP.add).then_inc(evq.h, 1)
                else:
                    eng.tensor_copy(out=qt[qi % 2][:, :], in_=ps).then_inc(evq.h, 1)
            ctxs = xin[0]
            ctmp = xin[1]
            for b in range(2):
                eng.wait_ge(lqr.h, (b + 1) * 128)
                for s in range(8):
                    bs = b * 8 + s
                    eng.wait_ge(lqs2.h, (bs + 1) * 16)
                    q_s = qs_buf[:, :].rearrange("p (h f) -> p h f", f=64)
                    for t in range(s + 1):
                        k_t = kvr(t)[:, 0:2048].rearrange("p (h f) -> p h f", f=64)
                        eng.tensor_tensor(out=ctmp[:, :].rearrange("p (h f) -> p h f", f=64),
                                          in0=q_s, in1=k_t, op=OP.mult)
                        eng.tensor_reduce(
                            out=scb[:, :].rearrange("p (h t) -> p h t", t=8)[:, :, t],
                            in_=ctmp[:, :].rearrange("p (h f) -> p h f", f=64),
                            axis=AX.X, op=OP.add)
                    sc3 = scb[:, :].rearrange("p (h t) -> p h t", t=8)
                    eng.tensor_reduce(out=srow[:, 0:32], in_=sc3[:, :, 0:s + 1], axis=AX.X, op=OP.max)
                    eng.tensor_tensor(out=exb[:, :].rearrange("p (h t) -> p h t", t=8)[:, :, 0:s + 1],
                                      in0=sc3[:, :, 0:s + 1],
                                      in1=srow[:, 0:32].unsqueeze(2).broadcast_to([128, 32, s + 1]),
                                      op=OP.subtract).then_inc(sxd.h, 1)
                    eng.wait_ge(sxe.h, bs + 1)
                    ex3 = exb[:, :].rearrange("p (h t) -> p h t", t=8)
                    eng.tensor_reduce(out=srow[:, 32:64], in_=ex3[:, :, 0:s + 1], axis=AX.X, op=OP.add)
                    eng.reciprocal(out=srow[:, 64:96], in_=srow[:, 32:64])
                    eng.tensor_tensor(out=pb[:, :].rearrange("p (h t) -> p h t", t=8)[:, :, 0:s + 1],
                                      in0=ex3[:, :, 0:s + 1],
                                      in1=srow[:, 64:96].unsqueeze(2).broadcast_to([128, 32, s + 1]),
                                      op=OP.mult)
                    pb3 = pb[:, :].rearrange("p (h t) -> p h t", t=8)
                    for t in range(s + 1):
                        v_t = kvr(t)[:, 2048:4096].rearrange("p (h f) -> p h f", f=64)
                        pr = pb3[:, :, t].unsqueeze(2).broadcast_to([128, 32, 64])
                        if t == 0:
                            eng.tensor_tensor(out=ctxs[:, :].rearrange("p (h f) -> p h f", f=64),
                                              in0=v_t, in1=pr, op=OP.mult)
                        else:
                            eng.tensor_tensor(out=ctmp[:, :].rearrange("p (h f) -> p h f", f=64),
                                              in0=v_t, in1=pr, op=OP.mult)
                            eng.tensor_tensor(out=ctxs[:, :], in0=ctxs[:, :], in1=ctmp[:, :], op=OP.add)
                    eng.tensor_copy(out=ctxbf(2 * s + b)[:, :], in_=ctxs[:, :]).then_inc(vcx.h, 1)
            for tile in range(16):
                for kc in range(16):
                    gi = 256 + tile * 16 + kc
                    eng.wait_ge(tsp.h, gi + 1)
                    eng.tensor_copy(out=ctxf(kc)[:, tile * 128:(tile + 1) * 128],
                                    in_=tslot[kc % 2]).then_inc(ext.h, 1)
            for i in range(16):
                bi = i % 2
                eng.wait_ge(lx2[bi].h, (8 + i // 2 + 1) * 16)
                if i >= 2:
                    eng.wait_ge(sst2[bi].h, (i // 2) * 16)
                for c in range(4):
                    oi = i * 4 + c
                    eng.wait_ge(mmo.h, oi + 1)
                    eng.tensor_tensor(out=x3b[bi][:, c * 512:(c + 1) * 512], in0=pqs[oi % 2],
                                      in1=xin[bi][:, c * 512:(c + 1) * 512], op=OP.add).then_inc(evo.h, 1)
                idx = i
                sl = x3b[bi][:, :]
                eng.tensor_reduce(out=mrow[:, 0:1], in_=sl, axis=AX.X, op=OP.add)
                eng.tensor_scalar(out=mrow[:, 0:1], in0=mrow[:, 0:1], scalar1=1.0 / 2048.0,
                                  scalar2=None, op0=OP.mult)
                eng.tensor_scalar(out=sl, in0=sl, scalar1=mrow[:, 0:1], scalar2=None,
                                  op0=OP.subtract).then_inc(lnx.h, 1)
                eng.wait_ge(lnq.h, idx + 1)
                eng.tensor_reduce(out=mrow[:, 1:2], in_=v4[:, :], axis=AX.X, op=OP.add)
                eng.tensor_scalar(out=mrow[:, 1:2], in0=mrow[:, 1:2], scalar1=1.0 / 2048.0,
                                  scalar2=1e-5, op0=OP.mult, op1=OP.add)
                eng.reciprocal(out=mrow[:, 2:3], in_=mrow[:, 1:2]).then_inc(lnr.h, 1)
                eng.wait_ge(lns.h, idx + 1)
                eng.tensor_scalar(out=sl, in0=sl, scalar1=mrow[:, 3:4], scalar2=None,
                                  op0=OP.mult).then_inc(x3r.h, 1)

        @block.scalar
        def _(eng):
            eng = D(eng)
            for b in range(2):
                for s in range(8):
                    bs = b * 8 + s
                    eng.wait_ge(sxd.h, bs + 1)
                    ex3 = exb[:, :].rearrange("p (h t) -> p h t", t=8)
                    eng.activation(out=ex3[:, :, 0:s + 1], in_=ex3[:, :, 0:s + 1],
                                   func=AF.Exp, scale=0.125).then_inc(sxe.h, 1)
            for i in range(16):
                bi = i % 2
                idx = i
                eng.wait_ge(lnx.h, idx + 1)
                sl = x3b[bi][:, :]
                for c in range(4):
                    a = eng.activation(out=m1[:, :], in_=sl[:, c * 512:(c + 1) * 512],
                                       func=AF.Square, accum_out=v4[:, c:c + 1])
                a.then_inc(lnq.h, 1)
                eng.wait_ge(lnr.h, idx + 1)
                eng.sqrt(out=mrow[:, 3:4], in_=mrow[:, 2:3]).then_inc(lns.h, 1)
                eng.wait_ge(x3r.h, i + 1)
                eng.dma_start(out=x3[i * 128:(i + 1) * 128, :], in_=x3b[bi][:, :]).then_inc(sst2[i % 2].h, 16)
            eng.wait_ge(sst2[0].h, 128)
            eng.wait_ge(sst2[1].h, 128)
    return nc



def build_l0a():
    nc = bass.Bass("TRN2", target_bir_lowering=False, debug=False)
    x = nc.dram_tensor("x", [2048, 2048], F32, kind="ExternalInput")
    wq = nc.dram_tensor("wq", [2048, 3072], F32, kind="ExternalInput")
    wkvd = nc.dram_tensor("wkvd", [2048, 512], F32, kind="ExternalInput")
    wkvu = nc.dram_tensor("wkvu", [512, 3072], F32, kind="ExternalInput")
    wproj = nc.dram_tensor("wproj", [2048, 2048], F32, kind="ExternalInput")
    qgain = nc.dram_tensor("qgain", [1, 16], F32, kind="ExternalInput")
    cos1 = nc.dram_tensor("cos1", [2048, 32], F32, kind="ExternalInput")
    sin1 = nc.dram_tensor("sin1", [2048, 32], F32, kind="ExternalInput")
    trimask = nc.dram_tensor("trimask", [128, 128], F32, kind="ExternalInput")
    ident = nc.dram_tensor("ident", [128, 128], BF16, kind="ExternalInput")
    x1 = nc.dram_tensor("x1", [2048, 2048], F32, kind="ExternalOutput")
    qk_scr = nc.dram_tensor("qk_scr", [2048, 3072], F32)
    RSQ = 0.08838834764831843
    NP = 4

    emit_l0a(nc, x, wq, wkvd, wkvu, wproj, qgain, cos1, sin1, trimask, ident, x1, qk_scr)
    return nc


def emit_l0a(nc, x, wq, wkvd, wkvu, wproj, qgain, cos1, sin1, trimask, ident, x1, qk_scr):
    RSQ = 0.08838834764831843
    NP = 4
    with contextlib.ExitStack() as st:
        ebuf = lambda name, shape, dt: st.enter_context(nc.sbuf_tensor(name, shape, dt, allow_name_mangling=True))
        R1 = ebuf("R1", [128, 32768], BF16)
        R2 = ebuf("R2", [128, 32768], BF16)
        Vsb = ebuf("Vsb", [128, 8192], BF16)
        Ckv = ebuf("Ckv", [128, 8192], BF16)
        wkvb = [ebuf(f"wkvb{i}", [128, 3072], BF16) for i in range(2)]
        knb = [ebuf(f"knb{i}", [128, 256], BF16) for i in range(2)]
        xin = [ebuf(f"xin{i}", [128, 2048], F32) for i in range(2)]
        U = ebuf("U", [128, 2048], F32)
        qkt = [U[:, 0:512], U[:, 512:1024]]
        cosb = ebuf("cosb", [128, 16 * 32], F32)
        sinb = ebuf("sinb", [128, 16 * 32], F32)
        trim = ebuf("trim", [128, 128], F32)
        idn = ebuf("idn", [128, 128], BF16)
        ones = ebuf("ones", [1, 128], BF16)
        qgb = ebuf("qgb", [1, 16], BF16)
        gq = ebuf("gq", [128, 16], F32)
        rst = ebuf("rst", [128, 12], F32)
        sc4 = ebuf("sc4", [128, 12], F32)
        srow = ebuf("srow", [128, 12], F32)
        ssc = ebuf("ssc", [128, 512], F32)
        x1b = U
        v4 = ebuf("v4", [128, 4], F32)
        mrow = ebuf("mrow", [128, 4], F32)

        ps = st.enter_context(nc.psum_tensor("ps", [128, 2048], F32, allow_name_mangling=True))
        tps = st.enter_context(nc.psum_tensor("tps", [128, 3072], BF16, allow_name_mangling=True))
        tslot = [tps[:, 0:128], tps[:, 1024:1152], tps[:, 2048:2176]]
        pc = st.enter_context(nc.psum_tensor("pc", [128, 512], F32, allow_name_mangling=True))
        pb2 = [ps[:, 0:512], ps[:, 512:1024]]
        pb4 = [ps[:, i * 512:(i + 1) * 512] for i in range(4)]

        xfeat = lambda kc: R1[:, kc * 2048:(kc + 1) * 2048]
        ctxsb = lambda i: R1[:, i * 2048:(i + 1) * 2048]
        wpr = lambda kc: R1[:, kc * 2048:(kc + 1) * 2048]
        wqb = lambda cb: R2[:, cb * 8192:(cb + 1) * 8192]
        ctb = lambda i: R2[:, 16384 + i * 512:16384 + (i + 1) * 512]
        xb_a = R2[:, 24576:26624]
        qf = lambda hl: R2[:, hl * 2048:(hl + 1) * 2048]
        kf = lambda hl: R2[:, 8192 + hl * 2048:8192 + (hl + 1) * 2048]
        probs = R2[:, 16384:18432]
        probsT = R2[:, 18432:20480]
        prepb = lambda i: R2[:, 20480 + i * 768:20480 + (i + 1) * 768]
        ctxf = lambda kc: R2[:, kc * 2048:(kc + 1) * 2048]

        lid = Sem(st, nc, "lid")
        lgq = Sem(st, nc, "lgq")
        lx2 = [Sem(st, nc, "lx2a"), Sem(st, nc, "lx2b")]
        cxb = Sem(st, nc, "cxb")
        tspA = Sem(st, nc, "tspA"); extA = Sem(st, nc, "extA")
        lwq = [Sem(st, nc, "lwq0"), Sem(st, nc, "lwq1")]
        mmq = Sem(st, nc, "mmq"); evq = Sem(st, nc, "evq")
        sq2 = [Sem(st, nc, "sq0"), Sem(st, nc, "sq1")]
        cv1 = Sem(st, nc, "cv1"); cv2 = Sem(st, nc, "cv2"); cv3 = Sem(st, nc, "cv3")
        tspB = Sem(st, nc, "tspB"); extB = Sem(st, nc, "extB")
        gqd = Sem(st, nc, "gqd")
        lqk = [Sem(st, nc, "lqk0"), Sem(st, nc, "lqk1")]
        pq1 = Sem(st, nc, "pq1"); pv1 = Sem(st, nc, "pv1")
        pq2 = Sem(st, nc, "pq2"); pv2 = Sem(st, nc, "pv2")
        tspC1 = Sem(st, nc, "tspC1"); extC1 = Sem(st, nc, "extC1")
        lwk = [Sem(st, nc, "lwk0"), Sem(st, nc, "lwk1")]
        mmk = Sem(st, nc, "mmk")
        kq1 = Sem(st, nc, "kq1"); kv1 = Sem(st, nc, "kv1")
        kq2 = Sem(st, nc, "kq2"); kv2 = Sem(st, nc, "kv2")
        tspC2 = Sem(st, nc, "tspC2"); extC2 = Sem(st, nc, "extC2")
        vve = Sem(st, nc, "vve")
        mms = Sem(st, nc, "mms"); vmk = Sem(st, nc, "vmk"); sxe = Sem(st, nc, "sxe")
        vrc = Sem(st, nc, "vrc"); mpv = Sem(st, nc, "mpv"); vcx = Sem(st, nc, "vcx")
        tspC4 = Sem(st, nc, "tspC4"); extC4 = Sem(st, nc, "extC4")
        tspD = Sem(st, nc, "tspD"); extD = Sem(st, nc, "extD")
        lwo = Sem(st, nc, "lwo")
        mmo = Sem(st, nc, "mmo"); evo = Sem(st, nc, "evo")
        lnx = Sem(st, nc, "lnx"); lnq = Sem(st, nc, "lnq")
        lnr = Sem(st, nc, "lnr"); lns = Sem(st, nc, "lns")
        x1r = Sem(st, nc, "x1r"); sst = Sem(st, nc, "sst")
        block = st.enter_context(nc.Block())

        c4iters = [(hl, qt) for hl in range(4) for qt in range(16)]
        NTRP = sum(qt + 1 for _, qt in c4iters)
        c4nch = [qt // 4 + 1 for _, qt in c4iters]
        NCHP = sum(c4nch)

        def c4base(g, it):
            return g * NTRP + sum(c4iters[k][1] + 1 for k in range(it))

        def cbase(g, it):
            return g * NCHP + sum(c4nch[:it])

        @block.sync
        def _(eng):
            eng.dma_start(out=idn[:, :], in_=ident[:, :]).then_inc(lid.h, 16)
            eng.dma_start(out=cosb[:, :], in_=cos1.ap().rearrange("(t p) f -> p t f", p=128)).then_inc(lid.h, 16)
            eng.dma_start(out=sinb[:, :], in_=sin1.ap().rearrange("(t p) f -> p t f", p=128)).then_inc(lid.h, 16)
            eng.dma_start(out=trim[:, :], in_=trimask[:, :]).then_inc(lid.h, 16)
            for i in range(16):
                b = i % 2
                if i >= 2:
                    eng.wait_ge(cxb.h, i)
                eng.dma_start(out=xin[b][:, :], in_=x[i * 128:(i + 1) * 128, :]).then_inc(lx2[b].h, 16)
            for c in range(6):
                for i in range(16):
                    qi = c * 16 + i
                    eng.wait_ge(evq.h, qi + 1)
                    eng.dma_start(out=qk_scr[i * 128:(i + 1) * 128, c * 512:(c + 1) * 512],
                                  in_=qkt[qi % 2]).then_inc(sq2[qi % 2].h, 16)
            for g in range(NP):
                for i in range(16):
                    gi = g * 16 + i
                    b = i % 2
                    if gi < 2:
                        eng.wait_ge(sq2[0].h, 48 * 16)
                        eng.wait_ge(sq2[1].h, 48 * 16)
                        eng.wait_ge(cxb.h, 17)
                    else:
                        eng.wait_ge(pv2.h, gi - 1)
                    eng.dma_start(out=xin[b][:, 0:768],
                                  in_=qk_scr[i * 128:(i + 1) * 128, g * 768:(g + 1) * 768]).then_inc(lqk[b].h, 16)
            eng.wait_ge(vcx.h, NP * 64)
            for i in range(16):
                b = i % 2
                if i >= 2:
                    eng.wait_ge(evo.h, (i - 2) * 4 + 4)
                eng.dma_start(out=xin[b][:, :], in_=x[i * 128:(i + 1) * 128, :]).then_inc(lx2[b].h, 16)

        @block.gpsimd
        def _(eng):
            eng.dma_start(out=qgb[:, :], in_=qgain[:, :]).then_inc(lgq.h, 16)
            for c in range(7):
                cb = c % 2
                if c >= 2:
                    eng.wait_ge(mmq.h, (c - 1) * 16)
                for kc in range(16):
                    if c < 6:
                        src = wq[kc * 128:(kc + 1) * 128, c * 512:(c + 1) * 512]
                    else:
                        src = wkvd[kc * 128:(kc + 1) * 128, :]
                    eng.dma_start(out=wqb(cb)[:, kc * 512:(kc + 1) * 512], in_=src).then_inc(lwq[cb].h, 16)
            for g in range(NP):
                kb = g % 2
                if g >= 2:
                    eng.wait_ge(mmk.h, (g - 1) * 32)
                for lc in range(4):
                    eng.dma_start(out=wkvb[kb][:, lc * 256:(lc + 1) * 256],
                                  in_=wkvu[lc * 128:(lc + 1) * 128, g * 256:(g + 1) * 256]).then_inc(lwk[kb].h, 16)
                for lc in range(4):
                    eng.dma_start(out=wkvb[kb][:, 1024 + lc * 512:1024 + (lc + 1) * 512],
                                  in_=wkvu[lc * 128:(lc + 1) * 128,
                                           1024 + g * 512:1024 + (g + 1) * 512]
                                  ).then_inc(lwk[kb].h, 16)
            eng.wait_ge(tspD.h, 256)
            for kc in range(16):
                eng.dma_start(out=wpr(kc), in_=wproj[kc * 128:(kc + 1) * 128, :]).then_inc(lwo.h, 16)

        @block.tensor
        def _(eng):
            eng.wait_ge(lid.h, 64)
            eng.wait_ge(lgq.h, 16)
            eng.wait_ge(cxb.h, 1)
            eng.matmul(out=pc[:, 0:16], lhsT=ones[0:1, :], rhs=qgb[0:1, :],
                       start=True, stop=True).then_inc(gqd.h, 1)
            for i in range(16):
                eng.wait_ge(cxb.h, i + 2)
                for kc in range(16):
                    gi = i * 16 + kc
                    if gi >= 3:
                        eng.wait_ge(extA.h, gi - 2)
                    eng.transpose(out=tslot[gi % 3], in_=xb_a[:, kc * 128:(kc + 1) * 128],
                                  identity=idn[:, :]).then_inc(tspA.h, 1)
            eng.wait_ge(extA.h, 256)
            for c in range(7):
                cb = c % 2
                for i in range(16):
                    qi = c * 16 + i
                    if i == 0:
                        eng.wait_ge(lwq[cb].h, (c // 2 + 1) * 256)
                    if qi >= 4:
                        eng.wait_ge(evq.h, qi - 3)
                    for kc in range(16):
                        mm = eng.matmul(out=pb4[qi % 4], lhsT=xfeat(kc)[:, i * 128:(i + 1) * 128],
                                        rhs=wqb(cb)[:, kc * 512:(kc + 1) * 512],
                                        start=(kc == 0), stop=(kc == 15))
                    mm.then_inc(mmq.h, 1)
            eng.wait_ge(evq.h, 112)
            for i in range(16):
                for j in range(4):
                    gi = i * 4 + j
                    if gi < 3:
                        eng.wait_ge(extA.h, 256)
                    else:
                        eng.wait_ge(extB.h, gi - 2)
                    eng.transpose(out=tslot[gi % 3], in_=ctb(i)[:, j * 128:(j + 1) * 128],
                                  identity=idn[:, :]).then_inc(tspB.h, 1)
            for g in range(NP):
                kb = g % 2
                for i in range(16):
                    eng.wait_ge(pv2.h, g * 16 + i + 1)
                    for bidx in range(12):
                        tc = g * 192 + i * 12 + bidx
                        if i == 0 and bidx < 3:
                            if g == 0:
                                eng.wait_ge(extB.h, 64)
                            else:
                                eng.wait_ge(extC4.h, g * NTRP)
                            if tc >= 3:
                                eng.wait_ge(extC1.h, tc - 2)
                        else:
                            eng.wait_ge(extC1.h, tc - 2)
                        src = prepb(i)[:, bidx * 64:(bidx + 1) * 64]
                        eng.transpose(out=tslot[tc % 3][0:64, :], in_=src,
                                      identity=idn[:, :]).then_inc(tspC1.h, 1)
                for i in range(17):
                    if i < 16:
                        ki = g * 32 + i
                        if i == 0:
                            eng.wait_ge(lwk[kb].h, (g // 2 + 1) * 128)
                            if g == 0:
                                eng.wait_ge(extB.h, 64)
                        if i >= 2:
                            eng.wait_ge(kv2.h, g * 16 + i - 1)
                        elif g > 0:
                            eng.wait_ge(vve.h, g * 16 + i - 1)
                        for lc in range(4):
                            mm = eng.matmul(out=pb2[ki % 2][:, 0:256],
                                            lhsT=Ckv[:, lc * 2048 + i * 128:lc * 2048 + (i + 1) * 128],
                                            rhs=wkvb[kb][:, lc * 256:(lc + 1) * 256],
                                            start=(lc == 0), stop=(lc == 3))
                        mm.then_inc(mmk.h, 1)
                    if i >= 1:
                        ii = i - 1
                        eng.wait_ge(kv2.h, g * 16 + ii + 1)
                        for hl in range(4):
                            tc = g * 64 + ii * 4 + hl
                            if ii == 0 and hl < 3:
                                eng.wait_ge(extC1.h, (g + 1) * 192)
                                if tc >= 3:
                                    eng.wait_ge(extC2.h, tc - 2)
                            else:
                                eng.wait_ge(extC2.h, tc - 2)
                            eng.transpose(out=tslot[tc % 3][0:64, :], in_=knb[ii % 2][:, hl * 64:(hl + 1) * 64],
                                          identity=idn[:, :]).then_inc(tspC2.h, 1)
                for i in range(16):
                    ki = g * 32 + 16 + i
                    if i == 0:
                        eng.wait_ge(extC2.h, (g + 1) * 64)
                    if i < 2:
                        eng.wait_ge(kv2.h, (g + 1) * 16)
                    else:
                        eng.wait_ge(vve.h, g * 16 + i - 1)
                    for lc in range(4):
                        mm = eng.matmul(out=pb2[ki % 2],
                                        lhsT=Ckv[:, lc * 2048 + i * 128:lc * 2048 + (i + 1) * 128],
                                        rhs=wkvb[kb][:, 1024 + lc * 512:1024 + (lc + 1) * 512],
                                        start=(lc == 0), stop=(lc == 3))
                    mm.then_inc(mmk.h, 1)
                eng.wait_ge(vve.h, (g + 1) * 16)
                for it, (hl, qt) in enumerate(c4iters):
                    gidx = g * 64 + it
                    nk = (qt + 1) * 128
                    nch = c4nch[it]
                    cb = cbase(g, it)
                    if gidx >= 1:
                        eng.wait_ge(sxe.h, cb)
                    for cch in range(nch):
                        lo = cch * 512
                        mm = eng.matmul(out=ps[:, lo:min(nk, lo + 512)],
                                        lhsT=qf(hl)[:, qt * 128:(qt + 1) * 128],
                                        rhs=kf(hl)[:, lo:min(nk, lo + 512)],
                                        start=True, stop=True)
                        mm.then_inc(mms.h, 1)
                    eng.wait_ge(vcx.h, gidx)
                    for kt in range(qt + 1):
                        tc = c4base(g, it) + kt
                        eng.wait_ge(sxe.h, cb + kt // 4 + 1)
                        if tc < g * NTRP + 3:
                            eng.wait_ge(extC2.h, (g + 1) * 64)
                            if tc >= 3:
                                eng.wait_ge(extC4.h, tc - 2)
                        else:
                            eng.wait_ge(extC4.h, tc - 2)
                        eng.transpose(out=tslot[tc % 3], in_=probs[:, kt * 128:(kt + 1) * 128],
                                      identity=idn[:, :]).then_inc(tspC4.h, 1)
                    for kt in range(qt + 1):
                        tc = c4base(g, it) + kt
                        eng.wait_ge(extC4.h, tc + 1)
                        mm = eng.matmul(out=pc[:, 0:128],
                                        lhsT=probsT[:, kt * 128:(kt + 1) * 128],
                                        rhs=Vsb[:, kt * 512 + hl * 128:kt * 512 + (hl + 1) * 128],
                                        start=(kt == 0), stop=(kt == qt))
                    mm.then_inc(mpv.h, 1)
            eng.wait_ge(vcx.h, NP * 64)
            for i in range(16):
                for kc in range(16):
                    gi = i * 16 + kc
                    if gi < 3:
                        eng.wait_ge(extC4.h, NP * NTRP)
                    else:
                        eng.wait_ge(extD.h, gi - 2)
                    eng.transpose(out=tslot[gi % 3], in_=ctxsb(i)[:, kc * 128:(kc + 1) * 128],
                                  identity=idn[:, :]).then_inc(tspD.h, 1)
            eng.wait_ge(lwo.h, 256)
            eng.wait_ge(extD.h, 256)
            for i in range(16):
                for c in range(4):
                    oi = i * 4 + c
                    if oi >= 4:
                        eng.wait_ge(evo.h, oi - 3)
                    for kc in range(16):
                        mm = eng.matmul(out=pb4[oi % 4], lhsT=ctxf(kc)[:, i * 128:(i + 1) * 128],
                                        rhs=wpr(kc)[:, c * 512:(c + 1) * 512],
                                        start=(kc == 0), stop=(kc == 15))
                    mm.then_inc(mmo.h, 1)

        @block.vector
        def _(eng):
            eng = D(eng)
            eng.memset(ones[:, :], 1.0)
            eng.engine_nop().then_inc(cxb.h, 1)
            for i in range(16):
                b = i % 2
                eng.wait_ge(lx2[b].h, (i // 2 + 1) * 16)
                if i > 0:
                    eng.wait_ge(tspA.h, i * 16)
                eng.tensor_copy(out=xb_a[:, :], in_=xin[b][:, :]).then_inc(cxb.h, 1)
            eng.wait_ge(gqd.h, 1)
            eng.tensor_scalar(out=gq[:, :], in0=pc[:, 0:16], scalar1=RSQ, scalar2=None,
                              op0=OP.mult).then_inc(cxb.h, 1)
            for c in range(7):
                for i in range(16):
                    qi = c * 16 + i
                    eng.wait_ge(mmq.h, qi + 1)
                    if c < 6:
                        if qi >= 2:
                            eng.wait_ge(sq2[qi % 2].h, (qi // 2) * 16)
                        eng.tensor_copy(out=qkt[qi % 2], in_=pb4[qi % 4]).then_inc(evq.h, 1)
                    else:
                        eng.wait_ge(cv1.h, i + 1)
                        eng.tensor_scalar(out=srow[:, 0:1], in0=srow[:, 0:1], scalar1=1.0 / 512.0,
                                          scalar2=1e-6, op0=OP.mult, op1=OP.add)
                        eng.reciprocal(out=srow[:, 1:2], in_=srow[:, 0:1]).then_inc(cv2.h, 1)
                        eng.wait_ge(cv3.h, i + 1)
                        eng.tensor_scalar(out=ctb(i), in0=pb4[qi % 4], scalar1=srow[:, 2:3],
                                          scalar2=None, op0=OP.mult).then_inc(evq.h, 1)

            for g in range(NP):
                for i in range(16):
                    gi = g * 16 + i
                    b = i % 2
                    prep = xin[b][:, 0:768]
                    prep3 = prep.rearrange("p (s f) -> p s f", f=64)
                    eng.wait_ge(lqk[b].h, (gi // 2 + 1) * 16)
                    eng.tensor_tensor(out=U[:, 0:768], in0=prep, in1=prep, op=OP.mult)
                    eng.tensor_reduce(out=rst[:, 0:12],
                                      in_=U[:, 0:768].rearrange("p (s f) -> p s f", f=64),
                                      axis=AX.X, op=OP.add)
                    eng.tensor_scalar(out=rst[:, :], in0=rst[:, :], scalar1=1.0 / 64.0,
                                      scalar2=1e-6, op0=OP.mult, op1=OP.add)
                    eng.reciprocal(out=rst[:, :], in_=rst[:, :]).then_inc(pv1.h, 1)
                    eng.wait_ge(pq2.h, gi + 1)
                    eng.tensor_tensor(out=sc4[:, 0:8].rearrange("p (two h) -> p two h", two=2),
                                      in0=rst[:, 0:8].rearrange("p (two h) -> p two h", two=2),
                                      in1=gq[:, g * 4:(g + 1) * 4].unsqueeze(1).broadcast_to([128, 2, 4]),
                                      op=OP.mult)
                    eng.tensor_copy(out=sc4[:, 8:12], in_=rst[:, 8:12])
                    eng.tensor_tensor(out=prep3, in0=prep3,
                                      in1=sc4[:, 0:12].unsqueeze(2).broadcast_to([128, 12, 64]),
                                      op=OP.mult)
                    cs = cosb[:, i * 32:(i + 1) * 32].unsqueeze(1)
                    sn = sinb[:, i * 32:(i + 1) * 32].unsqueeze(1)
                    x1v = prep.rearrange("p (s f) -> p s f", f=32)[:, 8:24:2, :]
                    x2v = prep.rearrange("p (s f) -> p s f", f=32)[:, 9:24:2, :]
                    u3 = lambda o: U[:, 768 + o * 256: 768 + (o + 1) * 256].rearrange("p (s f) -> p s f", f=32)
                    eng.tensor_tensor(out=u3(0), in0=x1v, in1=cs.broadcast_to([128, 8, 32]), op=OP.mult)
                    eng.tensor_tensor(out=u3(1), in0=x2v, in1=cs.broadcast_to([128, 8, 32]), op=OP.mult)
                    eng.tensor_tensor(out=u3(2), in0=x1v, in1=sn.broadcast_to([128, 8, 32]), op=OP.mult)
                    eng.tensor_tensor(out=u3(3), in0=x2v, in1=sn.broadcast_to([128, 8, 32]), op=OP.mult)
                    eng.tensor_tensor(out=x1v, in0=u3(0), in1=u3(3), op=OP.add)
                    eng.tensor_tensor(out=x2v, in0=u3(1), in1=u3(2), op=OP.subtract)
                    eng.tensor_copy(out=prepb(i), in_=prep).then_inc(pv2.h, 1)

                for i in range(17):
                    if i < 16:
                        ki = g * 32 + i
                        eng.wait_ge(kq1.h, g * 16 + i + 1)
                        eng.tensor_scalar(out=srow[:, 4:8], in0=srow[:, 4:8], scalar1=1.0 / 64.0,
                                          scalar2=1e-6, op0=OP.mult, op1=OP.add)
                        eng.reciprocal(out=srow[:, 4:8], in_=srow[:, 4:8]).then_inc(kv1.h, 1)
                        eng.wait_ge(kq2.h, g * 16 + i + 1)
                        if g * 16 + i >= 2:
                            eng.wait_ge(tspC2.h, (g * 16 + i - 1) * 4)
                        for hl in range(4):
                            a = eng.tensor_scalar(out=knb[i % 2][:, hl * 64:(hl + 1) * 64],
                                                  in0=pb2[ki % 2][:, hl * 64:(hl + 1) * 64],
                                                  scalar1=srow[:, 4 + hl:5 + hl], scalar2=None,
                                                  op0=OP.mult)
                        a.then_inc(kv2.h, 1)

                for i in range(16):
                    ki = g * 32 + 16 + i
                    eng.wait_ge(mmk.h, ki + 1)
                    if g > 0 and i < 2:
                        eng.wait_ge(mpv.h, g * 64)
                    eng.tensor_copy(out=Vsb[:, i * 512:(i + 1) * 512], in_=pb2[ki % 2]).then_inc(vve.h, 1)
                for it, (hl, qt) in enumerate(c4iters):
                    gidx = g * 64 + it
                    nk = (qt + 1) * 128
                    nch = c4nch[it]
                    cb = cbase(g, it)
                    for cch in range(nch):
                        eng.wait_ge(mms.h, cb + cch + 1)
                        lo = cch * 512
                        hi = min(nk, lo + 512)
                        if hi == nk:
                            eng.tensor_tensor(out=ps[:, nk - 128:nk], in0=ps[:, nk - 128:nk],
                                              in1=trim[:, :], op=OP.add)
                        eng.tensor_reduce(out=srow[:, 4 + cch:5 + cch], in_=ps[:, lo:hi],
                                          axis=AX.X, op=OP.max)
                    eng.tensor_reduce(out=srow[:, 8:9], in_=srow[:, 4:4 + nch], axis=AX.X,
                                      op=OP.max, negate=True).then_inc(vmk.h, 1)
                    eng.wait_ge(sxe.h, cb + nch)
                    eng.tensor_reduce(out=srow[:, 9:10], in_=srow[:, 4:4 + nch], axis=AX.X,
                                      op=OP.add)
                    eng.reciprocal(out=srow[:, 10:11], in_=srow[:, 9:10]).then_inc(vrc.h, 1)

                    eng.wait_ge(mpv.h, gidx + 1)
                    eng.tensor_scalar(out=ctxsb(qt)[:, (g * 4 + hl) * 128:(g * 4 + hl + 1) * 128],
                                      in0=pc[:, 0:128], scalar1=srow[:, 10:11], scalar2=None,
                                      op0=OP.mult).then_inc(vcx.h, 1)

            for i in range(16):
                b = i % 2
                eng.wait_ge(lx2[b].h, (8 + i // 2 + 1) * 16)
                if i >= 1:
                    eng.wait_ge(sst.h, i * 16)
                for c in range(4):
                    oi = i * 4 + c
                    eng.wait_ge(mmo.h, oi + 1)
                    eng.tensor_tensor(out=x1b[:, c * 512:(c + 1) * 512], in0=pb4[oi % 4],
                                      in1=xin[b][:, c * 512:(c + 1) * 512], op=OP.add).then_inc(evo.h, 1)
                sl = x1b[:, :]
                eng.tensor_reduce(out=mrow[:, 0:1], in_=sl, axis=AX.X, op=OP.add)
                eng.tensor_scalar(out=mrow[:, 0:1], in0=mrow[:, 0:1], scalar1=1.0 / 2048.0,
                                  scalar2=None, op0=OP.mult)
                eng.tensor_scalar(out=sl, in0=sl, scalar1=mrow[:, 0:1], scalar2=None,
                                  op0=OP.subtract).then_inc(lnx.h, 1)
                eng.wait_ge(lnq.h, i + 1)
                eng.tensor_reduce(out=mrow[:, 1:2], in_=v4[:, :], axis=AX.X, op=OP.add)
                eng.tensor_scalar(out=mrow[:, 1:2], in0=mrow[:, 1:2], scalar1=1.0 / 2048.0,
                                  scalar2=1e-5, op0=OP.mult, op1=OP.add)
                eng.reciprocal(out=mrow[:, 2:3], in_=mrow[:, 1:2]).then_inc(lnr.h, 1)
                eng.wait_ge(lns.h, i + 1)
                eng.tensor_scalar(out=sl, in0=sl, scalar1=mrow[:, 3:4], scalar2=None,
                                  op0=OP.mult).then_inc(x1r.h, 1)

        @block.scalar
        def _(eng):
            eng = D(eng)
            for i in range(16):
                for kc in range(16):
                    gi = i * 16 + kc
                    eng.wait_ge(tspA.h, gi + 1)
                    eng.copy(out=xfeat(kc)[:, i * 128:(i + 1) * 128],
                             in_=tslot[gi % 3]).then_inc(extA.h, 1)
            for i in range(16):
                qi = 96 + i
                eng.wait_ge(mmq.h, qi + 1)
                if i >= 1:
                    eng.wait_ge(cv2.h, i)
                    eng.wait_ge(evq.h, 96 + i)
                eng.activation(out=ssc[:, :], in_=pb4[qi % 4], func=AF.Square,
                               accum_out=srow[:, 0:1]).then_inc(cv1.h, 1)
                eng.wait_ge(cv2.h, i + 1)
                eng.sqrt(out=srow[:, 2:3], in_=srow[:, 1:2]).then_inc(cv3.h, 1)
            for i in range(16):
                for j in range(4):
                    gi = i * 4 + j
                    eng.wait_ge(tspB.h, gi + 1)
                    eng.copy(out=Ckv[:, j * 2048 + i * 128:j * 2048 + (i + 1) * 128],
                             in_=tslot[gi % 3]).then_inc(extB.h, 1)
            for g in range(NP):
                for i in range(17):
                    if i < 16:
                        gi = g * 16 + i
                        eng.wait_ge(pv1.h, gi + 1)
                        eng.sqrt(out=rst[:, :], in_=rst[:, :]).then_inc(pq2.h, 1)
                    if i >= 1:
                        ii = i - 1
                        for bidx in range(12):
                            tc = g * 192 + ii * 12 + bidx
                            eng.wait_ge(tspC1.h, tc + 1)
                            if bidx < 4:
                                dst = qf(bidx)[0:64, ii * 128:(ii + 1) * 128]
                            elif bidx < 8:
                                dst = qf(bidx - 4)[64:128, ii * 128:(ii + 1) * 128]
                            else:
                                dst = kf(bidx - 8)[64:128, ii * 128:(ii + 1) * 128]
                            eng.copy(out=dst, in_=tslot[tc % 3][0:64, :]).then_inc(extC1.h, 1)
                for i in range(17):
                    if i < 16:
                        ki = g * 32 + i
                        eng.wait_ge(mmk.h, ki + 1)
                        if g * 16 + i >= 1:
                            eng.wait_ge(kv2.h, g * 16 + i)
                        for hl in range(4):
                            a = eng.activation(out=ssc[:, 0:64], in_=pb2[ki % 2][:, hl * 64:(hl + 1) * 64],
                                               func=AF.Square, accum_out=srow[:, 4 + hl:5 + hl])
                        a.then_inc(kq1.h, 1)
                        eng.wait_ge(kv1.h, g * 16 + i + 1)
                        eng.sqrt(out=srow[:, 4:8], in_=srow[:, 4:8]).then_inc(kq2.h, 1)
                    if i >= 1:
                        ii = i - 1
                        for hl in range(4):
                            tc = g * 64 + ii * 4 + hl
                            eng.wait_ge(tspC2.h, tc + 1)
                            eng.copy(out=kf(hl)[0:64, ii * 128:(ii + 1) * 128],
                                     in_=tslot[tc % 3][0:64, :]).then_inc(extC2.h, 1)
                for it, (hl, qt) in enumerate(c4iters):
                    gidx = g * 64 + it
                    nk = (qt + 1) * 128
                    nch = c4nch[it]
                    eng.wait_ge(vmk.h, gidx + 1)
                    for cch in range(nch):
                        lo = cch * 512
                        hi = min(nk, lo + 512)
                        eng.activation(out=probs[:, lo:hi], in_=ps[:, lo:hi], func=AF.Exp,
                                       bias=srow[:, 8:9],
                                       accum_out=srow[:, 4 + cch:5 + cch]).then_inc(sxe.h, 1)
                    for kt in range(qt + 1):
                        tc = c4base(g, it) + kt
                        eng.wait_ge(tspC4.h, tc + 1)
                        eng.copy(out=probsT[:, kt * 128:(kt + 1) * 128],
                                 in_=tslot[tc % 3]).then_inc(extC4.h, 1)
            for i in range(16):
                for kc in range(16):
                    gi = i * 16 + kc
                    eng.wait_ge(tspD.h, gi + 1)
                    eng.copy(out=ctxf(kc)[:, i * 128:(i + 1) * 128],
                             in_=tslot[gi % 3]).then_inc(extD.h, 1)
            for i in range(16):
                eng.wait_ge(lnx.h, i + 1)
                for c in range(4):
                    a = eng.activation(out=ssc[:, :], in_=x1b[:, c * 512:(c + 1) * 512],
                                       func=AF.Square, accum_out=v4[:, c:c + 1])
                a.then_inc(lnq.h, 1)
                eng.wait_ge(lnr.h, i + 1)
                eng.sqrt(out=mrow[:, 3:4], in_=mrow[:, 2:3]).then_inc(lns.h, 1)
                eng.wait_ge(x1r.h, i + 1)
                eng.dma_start(out=x1[i * 128:(i + 1) * 128, :], in_=x1b[:, :]).then_inc(sst.h, 16)
            eng.wait_ge(sst.h, 256)



def build_l1a2():
    nc = bass.Bass("TRN2", target_bir_lowering=False, debug=False)
    x2 = nc.dram_tensor("x2", [2048, 2048], F32, kind="ExternalInput")
    wqkv = nc.dram_tensor("wqkv", [2048, 6144], F32, kind="ExternalInput")
    bqkv = nc.dram_tensor("bqkv", [1, 6144], F32, kind="ExternalInput")
    wout = nc.dram_tensor("wout", [2048, 2048], F32, kind="ExternalInput")
    bout = nc.dram_tensor("bout", [1, 2048], F32, kind="ExternalInput")
    cos2 = nc.dram_tensor("cos2", [128, 64], F32, kind="ExternalInput")
    sin2 = nc.dram_tensor("sin2", [128, 64], F32, kind="ExternalInput")
    mask8 = nc.dram_tensor("mask8", [128, 8], F32, kind="ExternalInput")
    ident = nc.dram_tensor("ident", [128, 128], BF16, kind="ExternalInput")
    x3 = nc.dram_tensor("x3", [2048, 2048], F32, kind="ExternalOutput")
    q_scr = nc.dram_tensor("q_scr", [2048, 2048], BF16)
    kvk_in = nc.dram_tensor("kvk_in", [2048, 2048], BF16)
    kvv_in = nc.dram_tensor("kvv_in", [2048, 2048], BF16)
    kvk_all = nc.dram_tensor("kvk_all", [16384, 2048], BF16, addr_space="Shared")
    kvv_all = nc.dram_tensor("kvv_all", [16384, 2048], BF16, addr_space="Shared")
    emit_l1a2(nc, x2, wqkv, bqkv, wout, bout, cos2, sin2, mask8, ident, x3,
              q_scr, kvk_in, kvv_in, kvk_all, kvv_all)
    return nc


def emit_l1a2(nc, x2, wqkv, bqkv, wout, bout, cos2, sin2, mask8, ident, x3,
              q_scr, kvk_in, kvv_in, kvk_all, kvv_all, x2f_scr, x3f_scr):
    with contextlib.ExitStack() as st:
        ebuf = lambda name, shape, dt: st.enter_context(nc.sbuf_tensor(name, shape, dt, allow_name_mangling=True))
        R1 = ebuf("R1", [128, 32768], BF16)
        R2 = ebuf("R2", [128, 32768], BF16)
        xin = [ebuf(f"xin{i}", [128, 2048], F32) for i in range(2)]
        idn = ebuf("idn", [128, 128], BF16)
        ones = ebuf("ones", [1, 128], BF16)
        brow = ebuf("brow", [1, 6144], BF16)
        borow = ebuf("borow", [1, 2048], BF16)
        cosb = ebuf("cosb", [128, 64], F32)
        sinb = ebuf("sinb", [128, 64], F32)
        mskb = ebuf("mskb", [128, 8], F32)
        m1 = ebuf("m1", [128, 512], F32)
        m2 = ebuf("m2", [128, 512], F32)
        qt = [ebuf(f"qt{i}", [128, 512], BF16) for i in range(2)]
        qbuf = [ebuf(f"qbuf{i}", [128, 2048], BF16) for i in range(2)]
        scb = ebuf("scb", [128, 256], F32)
        exb = ebuf("exb", [128, 256], F32)
        pb = ebuf("pb", [128, 256], BF16)
        xfb = ebuf("xfb", [128, 2048], BF16)
        stgE = [ebuf(f"stgE{i}", [128, 2048], BF16) for i in range(2)]
        srow = ebuf("srow", [128, 128], F32)
        x3b = [ebuf(f"x3b{i}", [128, 2048], F32) for i in range(2)]
        v4 = ebuf("v4", [128, 4], F32)
        mrow = ebuf("mrow", [128, 4], F32)

        tps = st.enter_context(nc.psum_tensor("tps", [128, 2048], BF16, allow_name_mangling=True))
        tslot = [tps[:, 0:128], tps[:, 1024:1152]]
        pq = st.enter_context(nc.psum_tensor("pq", [128, 2048], F32, allow_name_mangling=True))
        pqs = [pq[:, i * 512:(i + 1) * 512] for i in range(4)]

        lid = Sem(st, nc, "lid")
        lbq = Sem(st, nc, "lbq")
        lx2 = [Sem(st, nc, "lx2a"), Sem(st, nc, "lx2b")]
        cxb = Sem(st, nc, "cxb")
        tsp = Sem(st, nc, "tsp")
        ext = Sem(st, nc, "ext")
        lwq = [Sem(st, nc, "lwq0"), Sem(st, nc, "lwq1")]
        mmq = Sem(st, nc, "mmq")
        evq = Sem(st, nc, "evq")
        sq2 = [Sem(st, nc, "sq0"), Sem(st, nc, "sq1")]
        ccs = Sem(st, nc, "ccs")
        lqr = Sem(st, nc, "lqr")
        lqs2 = Sem(st, nc, "lqs2")
        sxd = Sem(st, nc, "sxd")
        psc = Sem(st, nc, "psc")
        pcx = Sem(st, nc, "pcx")
        vpb = Sem(st, nc, "vpb")
        sxe = Sem(st, nc, "sxe")
        vcx = Sem(st, nc, "vcx")
        lxf = Sem(st, nc, "lxf")
        cxe = Sem(st, nc, "cxe")
        tspE = Sem(st, nc, "tspE")
        extE = Sem(st, nc, "extE")
        sxf2 = [Sem(st, nc, "sxfa"), Sem(st, nc, "sxfb")]
        lwo = Sem(st, nc, "lwo")
        mmo = Sem(st, nc, "mmo")
        evo = Sem(st, nc, "evo")
        lnx = Sem(st, nc, "lnx")
        lnq = Sem(st, nc, "lnq")
        lnr = Sem(st, nc, "lnr")
        lns = Sem(st, nc, "lns")
        x3r = Sem(st, nc, "x3r")
        sst2 = [Sem(st, nc, "ssta"), Sem(st, nc, "sstb")]
        block = st.enter_context(nc.Block())

        NQ = 192
        corder = list(range(4, 12)) + list(range(4))
        ctxp = x3b[1]
        ptmp = x3b[0]

        wb = lambda cb: R2[:, cb * 16384:(cb + 1) * 16384]
        kvr = lambda t: R2[:, t * 4096:(t + 1) * 4096]
        ctxf = lambda kc: R2[:, kc * 2048:(kc + 1) * 2048]
        x2f = lambda kc: R1[:, kc * 2048:(kc + 1) * 2048]
        ctxbf = lambda tile: R1[:, tile * 2048:(tile + 1) * 2048]
        woutr = lambda kc: R1[:, kc * 2048:(kc + 1) * 2048]

        @block.sync
        def _(eng):
            eng.dma_start(out=idn[:, :], in_=ident[:, :]).then_inc(lid.h, 16)
            eng.dma_start(out=cosb[:, :], in_=cos2[:, :]).then_inc(lid.h, 16)
            eng.dma_start(out=sinb[:, :], in_=sin2[:, :]).then_inc(lid.h, 16)
            eng.dma_start(out=mskb[:, :], in_=mask8[:, :]).then_inc(lid.h, 16)
            for kc in range(16):
                eng.dma_start(out=x2f(kc), in_=x2f_scr.ap()[kc, :, :]).then_inc(lxf.h, 16)
            for qi in range(NQ):
                eng.wait_ge(evq.h, qi + 1)
                c, i = corder[qi // 16], qi % 16
                if c < 4:
                    dst = q_scr[i * 128:(i + 1) * 128, c * 512:(c + 1) * 512]
                elif c < 8:
                    dst = kvk_in[i * 128:(i + 1) * 128, (c - 4) * 512:(c - 3) * 512]
                else:
                    dst = kvv_in[i * 128:(i + 1) * 128, (c - 8) * 512:(c - 7) * 512]
                eng.dma_start(out=dst, in_=qt[qi % 2][:, :]).then_inc(sq2[qi % 2].h, 16)
            for i in range(16):
                if i == 0:
                    eng.wait_ge(ccs.h, 1)
                else:
                    eng.wait_ge(sxd.h, i)
                    eng.wait_ge(psc.h, i)
                for t in range(8):
                    eng.dma_start(out=kvr(t)[:, 0:2048],
                                  in_=kvk_all[t * 2048 + i * 128:t * 2048 + (i + 1) * 128, :]
                                  ).then_inc(lqr.h, 16)
                if i == 0:
                    eng.wait_ge(ccs.h, 2)
                else:
                    eng.wait_ge(vcx.h, i)
                for t in range(8):
                    eng.dma_start(out=kvr(t)[:, 2048:4096],
                                  in_=kvv_all[t * 2048 + i * 128:t * 2048 + (i + 1) * 128, :]
                                  ).then_inc(lqr.h, 16)
                if i == 0:
                    eng.wait_ge(sq2[0].h, (NQ // 2) * 16)
                    eng.wait_ge(sq2[1].h, (NQ // 2) * 16)
                if i >= 2:
                    eng.wait_ge(vcx.h, i - 1)
                eng.dma_start(out=qbuf[i % 2][:, :], in_=q_scr[i * 128:(i + 1) * 128, :]).then_inc(lqs2.h, 16)
            eng.wait_ge(vcx.h, 16)
            for i in range(16):
                b = i % 2
                if i >= 2:
                    eng.wait_ge(evo.h, (i - 2) * 4 + 4)
                eng.dma_start(out=xin[b][:, :], in_=x2[i * 128:(i + 1) * 128, :]).then_inc(lx2[b].h, 16)
                if i >= 1:
                    idx = i - 1
                    eng.wait_ge(extE.h, (idx + 1) * 16)
                    eng.dma_start(
                        out=x3f_scr.ap()[:, :, idx * 128:(idx + 1) * 128].rearrange("c p f -> p c f"),
                        in_=stgE[idx % 2][:, :].rearrange("p (c f) -> p c f", f=128),
                    ).then_inc(sxf2[idx % 2].h, 16)
            eng.wait_ge(extE.h, 256)
            eng.dma_start(
                out=x3f_scr.ap()[:, :, 15 * 128:16 * 128].rearrange("c p f -> p c f"),
                in_=stgE[1][:, :].rearrange("p (c f) -> p c f", f=128),
            ).then_inc(sxf2[1].h, 16)
            eng.wait_ge(sxf2[0].h, 128)
            eng.wait_ge(sxf2[1].h, 128)

        @block.gpsimd
        def _(eng):
            eng.dma_start(out=brow[:, :], in_=bqkv[:, :]).then_inc(lbq.h, 16)
            eng.dma_start(out=borow[:, :], in_=bout[:, :]).then_inc(lbq.h, 16)
            for ci in range(12):
                c = corder[ci]
                cb = ci % 2
                if ci >= 2:
                    eng.wait_ge(mmq.h, (ci - 1) * 16)
                for kc in range(16):
                    eng.dma_start(out=wb(cb)[:, kc * 512:(kc + 1) * 512],
                                  in_=wqkv[kc * 128:(kc + 1) * 128, c * 512:(c + 1) * 512]
                                  ).then_inc(lwq[cb].h, 16)
                if ci == 3:
                    eng.wait_ge(sq2[0].h, 32 * 16)
                    eng.wait_ge(sq2[1].h, 32 * 16)
                    eng.collective_compute(
                        "AllGather", OP.bypass,
                        replica_groups=[list(range(NCORES))],
                        ins=[kvk_in.ap().opt()],
                        outs=[kvk_all.ap().opt()],
                    ).then_inc(ccs.h, 1)
                if ci == 7:
                    eng.wait_ge(sq2[0].h, 64 * 16)
                    eng.wait_ge(sq2[1].h, 64 * 16)
                    eng.collective_compute(
                        "AllGather", OP.bypass,
                        replica_groups=[list(range(NCORES))],
                        ins=[kvv_in.ap().opt()],
                        outs=[kvv_all.ap().opt()],
                    ).then_inc(ccs.h, 1)
            for i in range(16):
                eng.wait_ge(lqr.h, i * 256 + 128)
                eng.wait_ge(lqs2.h, (i + 1) * 16)
                if i >= 1:
                    eng.wait_ge(sxd.h, i)
                q_i = qbuf[i % 2][:, :].rearrange("p (h f) -> p h f", f=64)
                p3 = ptmp[:, :].rearrange("p (h f) -> p h f", f=64)
                for t in (6, 7):
                    k_t = kvr(t)[:, 0:2048].rearrange("p (h f) -> p h f", f=64)
                    eng.tensor_tensor(out=p3, in0=q_i, in1=k_t, op=OP.mult)
                    w = 32
                    while w >= 1:
                        eng.drain()
                        eng.tensor_tensor(out=p3[:, :, 0:w], in0=p3[:, :, 0:w],
                                          in1=p3[:, :, w:2 * w], op=OP.add)
                        w //= 2
                    eng.drain()
                    eng.tensor_copy(out=scb[:, :].rearrange("p (h t) -> p h t", t=8)[:, :, t:t + 1],
                                    in_=p3[:, :, 0:1])
                    eng.drain()
                eng.engine_nop().then_inc(psc.h, 1)
                eng.wait_ge(lqr.h, (i + 1) * 256)
                eng.wait_ge(vpb.h, i + 1)
                pb3p = pb[:, :].rearrange("p (h t) -> p h t", t=8)
                first = True
                for t in (6, 7):
                    v_t = kvr(t)[:, 2048:4096].rearrange("p (h f) -> p h f", f=64)
                    pr = pb3p[:, :, t].unsqueeze(2).broadcast_to([128, 32, 64])
                    if first:
                        eng.tensor_tensor(out=ctxp[:, :].rearrange("p (h f) -> p h f", f=64),
                                          in0=v_t, in1=pr, op=OP.mult)
                        first = False
                    else:
                        eng.tensor_tensor(out=ptmp[:, :].rearrange("p (h f) -> p h f", f=64),
                                          in0=v_t, in1=pr, op=OP.mult)
                        eng.drain()
                        eng.tensor_tensor(out=ctxp[:, :], in0=ctxp[:, :], in1=ptmp[:, :], op=OP.add)
                    eng.drain()
                eng.engine_nop().then_inc(pcx.h, 1)
            eng.wait_ge(tsp.h, 256)
            for kc in range(16):
                eng.dma_start(out=woutr(kc), in_=wout[kc * 128:(kc + 1) * 128, :]).then_inc(lwo.h, 16)

        @block.tensor
        def _(eng):
            eng.wait_ge(lid.h, 64)
            eng.wait_ge(lbq.h, 32)
            eng.wait_ge(lxf.h, 256)
            eng.wait_ge(cxb.h, 1)
            for qi in range(NQ):
                ci = qi // 16
                c, i = corder[ci], qi % 16
                cb = ci % 2
                if i == 0:
                    eng.wait_ge(lwq[cb].h, (ci // 2 + 1) * 256)
                if qi >= 4:
                    eng.wait_ge(evq.h, qi - 3)
                for kc in range(16):
                    eng.matmul(out=pqs[qi % 4], lhsT=x2f(kc)[:, i * 128:(i + 1) * 128],
                               rhs=wb(cb)[:, kc * 512:(kc + 1) * 512],
                               start=(kc == 0), stop=False)
                eng.matmul(out=pqs[qi % 4], lhsT=ones[0:1, :], rhs=brow[0:1, c * 512:(c + 1) * 512],
                           start=False, stop=True).then_inc(mmq.h, 1)
            eng.wait_ge(vcx.h, 16)
            for tile in range(16):
                for kc in range(16):
                    gi = tile * 16 + kc
                    if gi >= 2:
                        eng.wait_ge(ext.h, gi - 1)
                    eng.transpose(out=tslot[gi % 2], in_=ctxbf(tile)[:, kc * 128:(kc + 1) * 128],
                                  identity=idn[:, :]).then_inc(tsp.h, 1)
            eng.wait_ge(lwo.h, 256)
            eng.wait_ge(ext.h, 256)
            for i in range(16):
                for c in range(4):
                    oi = i * 4 + c
                    if oi >= 4:
                        eng.wait_ge(evo.h, oi - 3)
                    for kc in range(16):
                        eng.matmul(out=pqs[oi % 4], lhsT=ctxf(kc)[:, i * 128:(i + 1) * 128],
                                   rhs=woutr(kc)[:, c * 512:(c + 1) * 512],
                                   start=(kc == 0), stop=False)
                    eng.matmul(out=pqs[oi % 4], lhsT=ones[0:1, :], rhs=borow[0:1, c * 512:(c + 1) * 512],
                               start=False, stop=True).then_inc(mmo.h, 1)
                eng.wait_ge(cxe.h, i + 1)
                for kc in range(16):
                    gE = i * 16 + kc
                    if i == 0 and kc < 2:
                        eng.wait_ge(ext.h, 256)
                    if gE >= 2:
                        eng.wait_ge(extE.h, gE - 1)
                    eng.transpose(out=tslot[gE % 2], in_=xfb[:, kc * 128:(kc + 1) * 128],
                                  identity=idn[:, :]).then_inc(tspE.h, 1)

        @block.vector
        def _(eng):
            eng = D(eng)
            eng.memset(ones[:, :], 1.0)
            eng.engine_nop().then_inc(cxb.h, 1)
            for qi in range(NQ):
                c, i = corder[qi // 16], qi % 16
                eng.wait_ge(mmq.h, qi + 1)
                if qi >= 2:
                    eng.wait_ge(sq2[qi % 2].h, (qi // 2) * 16)
                ps = pqs[qi % 4]
                if c < 8:
                    cs = cosb[:, :].unsqueeze(1)
                    sn = sinb[:, :].unsqueeze(1)
                    eng.tensor_tensor(out=m1[:, :].rearrange("p (h f) -> p h f", f=64),
                                      in0=ps.rearrange("p (h f) -> p h f", f=64),
                                      in1=cs.broadcast_to([128, 8, 64]), op=OP.mult)
                    eng.tensor_tensor(out=m2[:, :].rearrange("p (h f) -> p h f", f=64),
                                      in0=ps.rearrange("p (h f) -> p h f", f=64),
                                      in1=sn.broadcast_to([128, 8, 64]), op=OP.mult)
                    qv = qt[qi % 2][:, :].rearrange("p (h two f) -> p h two f", two=2, f=32)
                    eng.tensor_tensor(out=qv[:, :, 0, :],
                                      in0=m1[:, :].rearrange("p (h two f) -> p h two f", two=2, f=32)[:, :, 0, :],
                                      in1=m2[:, :].rearrange("p (h two f) -> p h two f", two=2, f=32)[:, :, 1, :],
                                      op=OP.subtract)
                    eng.tensor_tensor(out=qv[:, :, 1, :],
                                      in0=m1[:, :].rearrange("p (h two f) -> p h two f", two=2, f=32)[:, :, 1, :],
                                      in1=m2[:, :].rearrange("p (h two f) -> p h two f", two=2, f=32)[:, :, 0, :],
                                      op=OP.add).then_inc(evq.h, 1)
                else:
                    eng.tensor_copy(out=qt[qi % 2][:, :], in_=ps).then_inc(evq.h, 1)
            ctxs = xin[0]
            ctmp = xin[1]
            for i in range(16):
                eng.wait_ge(lqr.h, i * 256 + 128)
                eng.wait_ge(lqs2.h, (i + 1) * 16)
                q_i = qbuf[i % 2][:, :].rearrange("p (h f) -> p h f", f=64)
                for t in range(6):
                    k_t = kvr(t)[:, 0:2048].rearrange("p (h f) -> p h f", f=64)
                    eng.tensor_tensor(out=ctmp[:, :].rearrange("p (h f) -> p h f", f=64),
                                      in0=q_i, in1=k_t, op=OP.mult)
                    eng.tensor_reduce(
                        out=scb[:, :].rearrange("p (h t) -> p h t", t=8)[:, :, t],
                        in_=ctmp[:, :].rearrange("p (h f) -> p h f", f=64),
                        axis=AX.X, op=OP.add)
                eng.wait_ge(psc.h, i + 1)
                sc3 = scb[:, :].rearrange("p (h t) -> p h t", t=8)
                eng.tensor_tensor(out=sc3, in0=sc3,
                                  in1=mskb[:, :].unsqueeze(1).broadcast_to([128, 32, 8]),
                                  op=OP.add)
                eng.tensor_reduce(out=srow[:, 0:32], in_=sc3, axis=AX.X, op=OP.max)
                eng.tensor_tensor(out=exb[:, :].rearrange("p (h t) -> p h t", t=8),
                                  in0=sc3,
                                  in1=srow[:, 0:32].unsqueeze(2).broadcast_to([128, 32, 8]),
                                  op=OP.subtract).then_inc(sxd.h, 1)
                eng.wait_ge(sxe.h, i + 1)
                ex3 = exb[:, :].rearrange("p (h t) -> p h t", t=8)
                eng.tensor_reduce(out=srow[:, 32:64], in_=ex3, axis=AX.X, op=OP.add)
                eng.reciprocal(out=srow[:, 64:96], in_=srow[:, 32:64])
                eng.tensor_tensor(out=pb[:, :].rearrange("p (h t) -> p h t", t=8),
                                  in0=ex3,
                                  in1=srow[:, 64:96].unsqueeze(2).broadcast_to([128, 32, 8]),
                                  op=OP.mult)
                eng.engine_nop().then_inc(vpb.h, 1)
                eng.wait_ge(lqr.h, (i + 1) * 256)
                pb3 = pb[:, :].rearrange("p (h t) -> p h t", t=8)
                for t in range(6):
                    v_t = kvr(t)[:, 2048:4096].rearrange("p (h f) -> p h f", f=64)
                    pr = pb3[:, :, t].unsqueeze(2).broadcast_to([128, 32, 64])
                    if t == 0:
                        eng.tensor_tensor(out=ctxs[:, :].rearrange("p (h f) -> p h f", f=64),
                                          in0=v_t, in1=pr, op=OP.mult)
                    else:
                        eng.tensor_tensor(out=ctmp[:, :].rearrange("p (h f) -> p h f", f=64),
                                          in0=v_t, in1=pr, op=OP.mult)
                        eng.tensor_tensor(out=ctxs[:, :], in0=ctxs[:, :], in1=ctmp[:, :], op=OP.add)
                eng.wait_ge(pcx.h, i + 1)
                eng.tensor_tensor(out=ctxs[:, :], in0=ctxs[:, :], in1=ctxp[:, :], op=OP.add)
                eng.tensor_copy(out=ctxbf(i)[:, :], in_=ctxs[:, :]).then_inc(vcx.h, 1)
            for tile in range(16):
                for kc in range(16):
                    gi = tile * 16 + kc
                    eng.wait_ge(tsp.h, gi + 1)
                    eng.tensor_copy(out=ctxf(kc)[:, tile * 128:(tile + 1) * 128],
                                    in_=tslot[gi % 2]).then_inc(ext.h, 1)
            for i in range(16):
                bi = i % 2
                eng.wait_ge(lx2[bi].h, (i // 2 + 1) * 16)
                if i >= 2:
                    eng.wait_ge(sst2[bi].h, (i // 2) * 16)
                for c in range(4):
                    oi = i * 4 + c
                    eng.wait_ge(mmo.h, oi + 1)
                    eng.tensor_tensor(out=x3b[bi][:, c * 512:(c + 1) * 512], in0=pqs[oi % 4],
                                      in1=xin[bi][:, c * 512:(c + 1) * 512], op=OP.add).then_inc(evo.h, 1)
                sl = x3b[bi][:, :]
                eng.tensor_reduce(out=mrow[:, 0:1], in_=sl, axis=AX.X, op=OP.add)
                eng.tensor_scalar(out=mrow[:, 0:1], in0=mrow[:, 0:1], scalar1=1.0 / 2048.0,
                                  scalar2=None, op0=OP.mult)
                eng.tensor_scalar(out=sl, in0=sl, scalar1=mrow[:, 0:1], scalar2=None,
                                  op0=OP.subtract).then_inc(lnx.h, 1)
                eng.wait_ge(lnq.h, i + 1)
                eng.tensor_reduce(out=mrow[:, 1:2], in_=v4[:, :], axis=AX.X, op=OP.add)
                eng.tensor_scalar(out=mrow[:, 1:2], in0=mrow[:, 1:2], scalar1=1.0 / 2048.0,
                                  scalar2=1e-5, op0=OP.mult, op1=OP.add)
                eng.reciprocal(out=mrow[:, 2:3], in_=mrow[:, 1:2]).then_inc(lnr.h, 1)
                eng.wait_ge(lns.h, i + 1)
                eng.tensor_scalar(out=sl, in0=sl, scalar1=mrow[:, 3:4], scalar2=None,
                                  op0=OP.mult).then_inc(x3r.h, 1)
                if i >= 1:
                    eng.wait_ge(tspE.h, i * 16)
                eng.tensor_copy(out=xfb[:, :], in_=sl).then_inc(cxe.h, 1)

        @block.scalar
        def _(eng):
            eng = D(eng)
            for i in range(16):
                eng.wait_ge(sxd.h, i + 1)
                ex3 = exb[:, :].rearrange("p (h t) -> p h t", t=8)
                eng.activation(out=ex3, in_=ex3, func=AF.Exp, scale=0.125).then_inc(sxe.h, 1)
            for i in range(16):
                bi = i % 2
                eng.wait_ge(lnx.h, i + 1)
                sl = x3b[bi][:, :]
                for c in range(4):
                    a = eng.activation(out=m1[:, :], in_=sl[:, c * 512:(c + 1) * 512],
                                       func=AF.Square, accum_out=v4[:, c:c + 1])
                a.then_inc(lnq.h, 1)
                eng.wait_ge(lnr.h, i + 1)
                eng.sqrt(out=mrow[:, 3:4], in_=mrow[:, 2:3]).then_inc(lns.h, 1)
                eng.wait_ge(x3r.h, i + 1)
                eng.dma_start(out=x3[i * 128:(i + 1) * 128, :], in_=x3b[bi][:, :]).then_inc(sst2[i % 2].h, 16)
                sb = i % 2
                if i >= 2:
                    eng.wait_ge(sxf2[i % 2].h, (i // 2) * 16)
                for kc in range(16):
                    gE = i * 16 + kc
                    eng.wait_ge(tspE.h, gE + 1)
                    eng.copy(out=stgE[sb][:, kc * 128:(kc + 1) * 128],
                             in_=tslot[gE % 2]).then_inc(extE.h, 1)
            eng.wait_ge(sst2[0].h, 128)
            eng.wait_ge(sst2[1].h, 128)



def _wq_perm():
    perm = []
    for g in range(4):
        for h in range(4 * g, 4 * g + 4):
            perm.extend(range(h * 128, h * 128 + 64))
        for h in range(4 * g, 4 * g + 4):
            perm.extend(range(h * 128 + 64, h * 128 + 128))
        for h in range(4 * g, 4 * g + 4):
            perm.extend(range(2048 + h * 64, 2048 + (h + 1) * 64))
    return np.array(perm)


_WQPERM = _wq_perm()


def _l0a_consts():
    inv = 10000.0 ** (-np.arange(0, 64, 2) / 64)
    fr = np.outer(np.arange(2048), inv)
    cos1 = np.cos(fr).astype(np.float32)
    sin1 = np.sin(fr).astype(np.float32)
    trimask = np.triu(np.full((128, 128), NEG_BIG, np.float32), 1)
    return cos1, sin1, trimask


def _l1a_consts():
    D = 64
    inv = 1.0 / (10000.0 ** (np.arange(0, D, 2) / D))
    s_of_tok = np.arange(2048) // 256
    fr = s_of_tok[:, None] * inv[None, :]
    emb = np.concatenate([fr, fr], -1)
    return np.cos(emb).astype(np.float32), np.sin(emb).astype(np.float32)


def kernel(**inputs):
    C = np.ascontiguousarray
    cos1, sin1, trimask = _l0a_consts()
    cos2, sin2 = _l1a_consts()
    qg = C(inputs["L0_attn_q_gain"].reshape(1, 16))

    nc = _get_nc("l0a", build_l0a)
    wq_p = C(inputs["L0_attn_w_q_krope"][:, _WQPERM])
    ims = [{"x": C(inputs["L0_attn_x"][c]), "wq": wq_p,
            "wkvd": inputs["L0_attn_w_kv_down"], "wkvu": inputs["L0_attn_w_kv_up"],
            "wproj": inputs["L0_attn_w_proj"], "qgain": qg, "cos1": cos1, "sin1": sin1,
            "trimask": trimask, "ident": _IDENT_BF16} for c in range(NCORES)]
    res = run_bass_kernel_spmd(nc, ims, core_ids=list(range(NCORES)))
    x1 = [res.results[c]["x1"] for c in range(NCORES)]

    nc = _get_nc("l0f", build_l0f)
    ims = [{"x1": x1[c], "gup": inputs["L0_ffn_gate_up_proj"],
            "dwn": inputs["L0_ffn_down_proj"], "ident": _IDENT_BF16} for c in range(NCORES)]
    res = run_bass_kernel_spmd(nc, ims, core_ids=list(range(NCORES)))
    x2 = np.stack([res.results[c]["x2"] for c in range(NCORES)], 0)

    nc = _get_nc("l1a", build_l1a)
    ims = [{"x2": C(x2[:, 256 * c:256 * (c + 1), :].reshape(2048, 2048)),
            "wqkv": inputs["L1_attn_qkv_w"], "bqkv": C(inputs["L1_attn_qkv_b"].reshape(1, 6144)),
            "wout": inputs["L1_attn_out_w"], "bout": C(inputs["L1_attn_out_b"].reshape(1, 2048)),
            "cos2": cos2, "sin2": sin2, "ident": _IDENT_BF16} for c in range(NCORES)]
    res = run_bass_kernel_spmd(nc, ims, core_ids=list(range(NCORES)))
    x3 = [res.results[c]["x3"] for c in range(NCORES)]

    wgT, wuT, wdT = _decode_weights(inputs)

    nc = _get_nc("l1f", build_l1f)
    ims = [{"x3": x3[c], "wg3": wgT, "wu3": wuT, "wd": wdT, "ident": _IDENT_BF16}
           for c in range(NCORES)]
    res = run_bass_kernel_spmd(nc, ims, core_ids=list(range(NCORES)))

    out = np.empty((8, 2048, 2048), np.float32)
    for c in range(NCORES):
        out[:, 256 * c:256 * (c + 1), :] = res.results[c]["x4"].reshape(8, 256, 2048)
    return out




def _clear_sems_block(nc, persist=()):
    bs = set(nc.barrier_sems) | {getattr(s, "num", s) for s in persist}
    rng = bass.get_kernel_semaphore_range()
    runs = []
    start = None
    for s in rng:
        if s in bs:
            if start is not None:
                runs.append(range(start, s))
                start = None
        elif start is None:
            start = s
    if start is not None:
        runs.append(range(start, rng.stop))
    with nc.Block() as b:
        @b.gpsimd
        def _(eng):
            for r in runs:
                eng.sem_clear(r)


def build_fused():
    nc = bass.Bass("TRN2", target_bir_lowering=False, debug=False)
    T = nc.dram_tensor
    x = T("x", [2048, 2048], F32, kind="ExternalInput")
    wq = T("wq", [2048, 3072], F32, kind="ExternalInput")
    wkvd = T("wkvd", [2048, 512], F32, kind="ExternalInput")
    wkvu = T("wkvu", [512, 3072], F32, kind="ExternalInput")
    wproj = T("wproj", [2048, 2048], F32, kind="ExternalInput")
    qgain = T("qgain", [1, 16], F32, kind="ExternalInput")
    cos1 = T("cos1", [2048, 32], F32, kind="ExternalInput")
    sin1 = T("sin1", [2048, 32], F32, kind="ExternalInput")
    trimask = T("trimask", [128, 128], F32, kind="ExternalInput")
    ident = T("ident", [128, 128], BF16, kind="ExternalInput")
    gup = T("gup", [4, 2048, 4096], F32, kind="ExternalInput")
    dwn = T("dwn", [4, 2048, 2048], F32, kind="ExternalInput")
    wqkv = T("wqkv", [2048, 6144], F32, kind="ExternalInput")
    bqkv = T("bqkv", [1, 6144], F32, kind="ExternalInput")
    wout = T("wout", [2048, 2048], F32, kind="ExternalInput")
    bout = T("bout", [1, 2048], F32, kind="ExternalInput")
    cos2 = T("cos2", [128, 64], F32, kind="ExternalInput")
    sin2 = T("sin2", [128, 64], F32, kind="ExternalInput")
    mask8 = T("mask8", [128, 8], F32, kind="ExternalInput")
    dins = {}
    for nm in ("gp", "up", "dn"):
        dins[nm + "_pk"] = T(nm + "_pk", [2048, 1024], U8, kind="ExternalInput")
        dins[nm + "_sc"] = T(nm + "_sc", [2048, 64], F32, kind="ExternalInput")
    x4 = T("x4", [2048, 2048], F32, kind="ExternalOutput")

    qk_scr = T("qk_scr", [2048, 3072], F32)
    x1_scr = T("x1_scr", [2048, 2048], F32)
    x2_scr = T("x2_scr", [2048, 2048], F32)
    x3_scr = T("x3_scr", [2048, 2048], F32)
    x2f_scr = T("x2f_scr", [16, 128, 2048], BF16)
    x3f_scr = T("x3f_scr", [16, 128, 2048], BF16)
    q_scr = T("q_scr", [2048, 2048], BF16)
    kvk_in = T("kvk_in", [2048, 2048], BF16)
    kvv_in = T("kvv_in", [2048, 2048], BF16)
    kvk_all = T("kvk_all", [16384, 2048], BF16, addr_space="Shared")
    kvv_all = T("kvv_all", [16384, 2048], BF16, addr_space="Shared")
    wg_loc = T("wg_loc", [16, 2048, 128], BF16)
    wu_loc = T("wu_loc", [16, 2048, 128], BF16)
    wd_loc = T("wd_loc", [2048, 2048], BF16)
    wg_all = T("wg_all", [128, 2048, 128], BF16, addr_space="Shared")
    wu_all = T("wu_all", [128, 2048, 128], BF16, addr_space="Shared")
    wd_all = T("wd_all", [16384, 2048], BF16, addr_space="Shared")

    pccs_h = nc.alloc_semaphore(name="pccs")
    flat = lambda t: t.ap().rearrange("a b f -> (a b) f")
    emit_decode(nc, dins, ident, {"gp": wg_loc, "up": wu_loc, "dn": wd_loc},
                collect=(pccs_h, [(flat(wg_loc).opt(), flat(wg_all).opt()),
                                  (flat(wu_loc).opt(), flat(wu_all).opt()),
                                  (wd_loc.ap().opt(), wd_all.ap().opt())]))
    _clear_sems_block(nc, persist=(pccs_h,))
    emit_l0a(nc, x, wq, wkvd, wkvu, wproj, qgain, cos1, sin1, trimask, ident, x1_scr, qk_scr)
    _clear_sems_block(nc, persist=(pccs_h,))
    emit_l0f(nc, x1_scr, gup, dwn, ident, x2_scr, x2f_scr=x2f_scr)
    _clear_sems_block(nc, persist=(pccs_h,))
    emit_l1a2(nc, x2_scr, wqkv, bqkv, wout, bout, cos2, sin2, mask8, ident, x3_scr,
              q_scr, kvk_in, kvv_in, kvk_all, kvv_all, x2f_scr, x3f_scr)
    _clear_sems_block(nc, persist=(pccs_h,))
    emit_l1f(nc, x3_scr, wg_all, wu_all, wd_all, ident, x4, pccs=pccs_h, x3f_scr=x3f_scr)
    return nc


def kernel(**inputs):
    C = np.ascontiguousarray
    cos1, sin1, trimask = _l0a_consts()
    qg = C(inputs["L0_attn_q_gain"].reshape(1, 16))
    gp = _u8_view_int32_bytes(inputs["L1_ffn_gate_W_packed"])
    up = _u8_view_int32_bytes(inputs["L1_ffn_up_W_packed"])
    dn = _u8_view_int32_bytes(inputs["L1_ffn_down_W_packed"])
    inv = 1.0 / (10000.0 ** (np.arange(0, 64, 2) / 64))
    wq_p = C(inputs["L0_attn_w_q_krope"][:, _WQPERM])

    nc = _get_nc("fused", build_fused)
    ims = []
    for b in range(NCORES):
        fr = b * inv
        emb = np.concatenate([fr, fr], -1)
        cosb = np.broadcast_to(np.cos(emb).astype(np.float32), (128, 64)).copy()
        sinb = np.broadcast_to(np.sin(emb).astype(np.float32), (128, 64)).copy()
        msk = np.where(np.arange(8) <= b, 0.0, NEG_BIG).astype(np.float32)
        mask8 = np.broadcast_to(msk, (128, 8)).copy()
        r = slice(2048 * b, 2048 * (b + 1))
        ims.append({
            "x": C(inputs["L0_attn_x"][b]),
            "wq": wq_p, "wkvd": inputs["L0_attn_w_kv_down"],
            "wkvu": inputs["L0_attn_w_kv_up"], "wproj": inputs["L0_attn_w_proj"],
            "qgain": qg, "cos1": cos1, "sin1": sin1, "trimask": trimask,
            "ident": _IDENT_BF16,
            "gup": inputs["L0_ffn_gate_up_proj"], "dwn": inputs["L0_ffn_down_proj"],
            "wqkv": inputs["L1_attn_qkv_w"], "bqkv": C(inputs["L1_attn_qkv_b"].reshape(1, 6144)),
            "wout": inputs["L1_attn_out_w"], "bout": C(inputs["L1_attn_out_b"].reshape(1, 2048)),
            "cos2": cosb, "sin2": sinb, "mask8": mask8,
            "gp_pk": gp[r], "gp_sc": C(inputs["L1_ffn_gate_scales"][r]),
            "up_pk": up[r], "up_sc": C(inputs["L1_ffn_up_scales"][r]),
            "dn_pk": C(dn[:, 1024 * b:1024 * (b + 1)]),
            "dn_sc": C(inputs["L1_ffn_down_scales"][:, 64 * b:64 * (b + 1)]),
        })
    res = run_bass_kernel_spmd(nc, ims, core_ids=list(range(NCORES)))
    return np.stack([res.results[b]["x4"] for b in range(NCORES)], 0)
